# version 10
# baseline (speedup 1.0000x reference)
import numpy as np
import ml_dtypes
from contextlib import ExitStack

import concourse.bass as bass
import concourse.mybir as mybir
from concourse.bass_utils import run_bass_kernel_spmd

F32 = mybir.dt.float32
BF16 = mybir.dt.bfloat16
AF = mybir.ActivationFunctionType
ALU = mybir.AluOpType

D = 1024
DFF = 4096
KD = D // 128
KF = DFF // 128
NT = 512
NP = 3

ENG_SEM = {"tensor": "pe", "scalar": "sc", "vector": "ve", "gpsimd": "gp"}
NQSEM = 8


class Slot:
    __slots__ = ("ready", "weng", "wseq", "users", "frozen", "name")

    def __init__(self, name="", users=None, frozen=False):
        self.ready = None
        self.weng = None
        self.wseq = 0
        self.users = list(users) if users else []
        self.frozen = frozen
        self.name = name


def marks_max(slots):
    mx = {}
    for s in slots:
        for m in ([s.ready] if s.ready else []) + s.users:
            if m[1] > mx.get(m[0], 0):
                mx[m[0]] = m[1]
    return [(k, v) for k, v in mx.items()]


class Region:

    def __init__(self, name, marks0=()):
        self.name = name
        self.marks = list(marks0)
        self.cur = []

    def slot(self, name):
        s = Slot(name, users=list(self.marks))
        self.cur.append(s)
        return s

    def turn(self):
        if self.cur:
            self.marks = marks_max(self.cur)
            self.cur = []

    def snapshot(self):
        mx = dict(self.marks)
        for sem, v in marks_max(self.cur):
            if v > mx.get(sem, 0):
                mx[sem] = v
        return mx


class Arena:

    def __init__(self):
        self.carves = []

    def region(self, name, start, nbytes):
        mx = {}
        for s, e, r in self.carves:
            if s < start + nbytes and start < e:
                for sem, v in r.snapshot().items():
                    if v > mx.get(sem, 0):
                        mx[sem] = v
        reg = Region(name, list(mx.items()))
        self.carves.append((start, start + nbytes, reg))
        return reg


class P:
    def __init__(self, nc, ctx):
        self.nc = nc
        names = ["pe", "sc", "ve", "gp"] + [f"sy{i}" for i in range(NQSEM)] + \
                [f"wg{i}" for i in range(NQSEM)]
        self.sem = {s: ctx.enter_context(nc.semaphore("s_" + s)) for s in names}
        self.cnt = {s: 0 for s in names}
        ENGS = ("tensor", "scalar", "vector", "sync", "gpsimd")
        self.waited = {e: {s: 0 for s in names} for e in ENGS}
        self.nseq = {e: 0 for e in ENGS}
        self.sealed = {e: 0 for e in ENGS}
        self.qi = {"sync": 0, "gpsimd": 0}
        self.bank_i = 0
        self.banks = None

    def eng(self, name):
        return getattr(self.nc, name)

    def wait(self, ename, dep):
        if dep is None:
            return
        sem, val = dep
        if val <= 0 or self.waited[ename][sem] >= val or ENG_SEM.get(ename) == sem:
            return
        self.eng(ename).wait_ge(self.sem[sem], val)
        self.waited[ename][sem] = val

    def _pre(self, ename, reads, writes):
        own = ENG_SEM.get(ename)
        need_drain = False
        for s in reads:
            if s.ready is None:
                assert s.frozen, f"read of unwritten slot {s.name}"
                continue
            if own is not None and s.ready[0] == own:
                if s.ready[1] > self.sealed[ename]:
                    need_drain = True
            else:
                self.wait(ename, s.ready)
        for s in writes:
            assert not s.frozen, f"write to frozen slot {s.name}"
            for u in s.users:
                if own is not None and u[0] == own:
                    if u[1] > self.sealed[ename]:
                        need_drain = True
                else:
                    self.wait(ename, u)
            if s.ready is not None:
                if not (s.weng == ename and own is not None and s.ready[0] == own):
                    self.wait(ename, s.ready)
        if need_drain:
            self.eng(ename).drain()
            self.sealed[ename] = self.cnt[own]

    def _post(self, ename, ins, sem, n, reads, writes):
        self.nseq[ename] += 1
        ins.then_inc(self.sem[sem], n)
        self.cnt[sem] += n
        mark = (sem, self.cnt[sem])
        for s in reads:
            if not s.frozen:
                s.users.append(mark)
        for s in writes:
            s.ready = mark
            s.weng = ename
            s.wseq = self.nseq[ename]
            s.users = []
        return mark

    def op(self, ename, emit, reads=(), writes=()):
        self._pre(ename, reads, writes)
        ins = emit(self.eng(ename))
        return self._post(ename, ins, ENG_SEM[ename], 1, reads, writes)

    def dma(self, q, out_ap, in_ap, reads=(), writes=(), **kw):
        pfx = "sy" if q == "sync" else "wg"
        sem = f"{pfx}{self.qi[q] % NQSEM}"
        self.qi[q] += 1
        self.wait(q, (sem, self.cnt[sem]))
        self._pre(q, reads, writes)
        ins = self.eng(q).dma_start(out_ap, in_ap, **kw)
        return self._post(q, ins, sem, 16, reads, writes)

    def mm_group(self, bank, mms, start=True, stop=True):
        ename = "tensor"
        if start:
            for u in bank.users:
                self.wait(ename, u)
        n = len(mms)
        for i, (l, r, rds) in enumerate(mms):
            for s in rds:
                if s.ready is not None:
                    self.wait(ename, s.ready)
            ins = self.nc.tensor.matmul(bank.ap, l, r, start=(start and i == 0),
                                        stop=(stop and i == n - 1))
            self.nseq[ename] += 1
            if i == n - 1:
                ins.then_inc(self.sem["pe"], 1)
                self.cnt["pe"] += 1
        mark = ("pe", self.cnt["pe"])
        if stop:
            bank.ready = mark
            bank.weng = ename
            bank.users = []
        for _, _, rds in mms:
            for s in rds:
                if not s.frozen:
                    s.users.append(mark)
        return mark

    def next_bank(self):
        b = self.banks[self.bank_i % 8]
        self.bank_i += 1
        return b


def build_program(T=2048):
    HT = T // 2
    NH = HT // NT
    NCH = T // NT
    nc = bass.Bass("TRN2", target_bir_lowering=False)

    xin = nc.dram_tensor("xin", [D, T], F32, kind="ExternalInput").ap()
    y_d = nc.dram_tensor("y", [D, T], F32, kind="ExternalOutput").ap()
    xdram = nc.dram_tensor("xdram", [D, T], F32).ap()

    wp = {}
    for name, mch, kch in [("p_w1", KF, KD), ("p_w2", KF, KD), ("p_h1", KF, KF),
                           ("p_h2", KF, KF), ("p_w3", KD, KF), ("p_wg", KD, KD),
                           ("p_wc", KD, KD), ("p_aw1", KF, KD), ("p_aw3", KF, KD),
                           ("p_aw2", KD, KF)]:
        wp[name] = nc.dram_tensor(name, [mch, 128, kch * 128], F32,
                                  kind="ExternalInput").ap()
    vin = {}
    for name, chk in [("v_bg", KD), ("v_bc", KD), ("v_n1", KD), ("v_n2", KD),
                      ("v_s1", KF), ("v_s2", KF), ("v_t1", KF), ("v_t2", KF),
                      ("v_s3", KD)]:
        vin[name] = nc.dram_tensor(name, [chk, 128], F32, kind="ExternalInput").ap()

    def x_kview(xap, t0, tl):
        return xap.rearrange("(k p) t -> p k t", p=128)[:, :, t0:t0 + tl]

    with ExitStack() as ctx:
        p = P(nc, ctx)
        BASE = (nc.sbuf_base + 31) & ~31
        SLAB = 211968
        ctx.enter_context(nc.sbuf_tensor("slab", [128, SLAB], mybir.dt.int8))
        at = lambda name, off, shape, dt: nc.alloc_sbuf_tensor_at(
            name, shape, dt, offset=BASE + off)

        ones_bf = at("ones_bf", 0, [128, 128], BF16)
        consts = at("consts", 256, [128, 4], F32)
        t_bg = at("t_bg", 288, [128, KD], F32)
        t_bcn = at("t_bcn", 320, [128, KD], F32)
        t_n1 = at("t_n1", 352, [128, KD], F32)
        t_n2 = at("t_n2", 384, [128, KD], F32)
        t_s3h = at("t_s3h", 416, [128, KD], F32)
        t_s1 = at("t_s1", 448, [128, KF], F32)
        t_s2 = at("t_s2", 576, [128, KF], F32)
        t_t1 = at("t_t1", 704, [128, KF], F32)
        t_t2 = at("t_t2", 832, [128, KF], F32)
        xsq_t = [at(f"xsq{i}", 1024 + i * 1024, [128, NT], BF16) for i in range(2)]
        tf_t = [at(f"tf{i}", 3072 + i * 2048, [128, NT], F32) for i in range(4)]
        rs_t = [at(f"rs{i}", 11264 + i * 2048, [128, NT], F32) for i in range(2)]
        mr_t = [at(f"mr{i}", 15360 + i * 2048, [128, NT], F32) for i in range(2)]
        xt_t = [at(f"xt{i}", 19456 + i * 2048, [128, NT], F32) for i in range(2)]
        xb_t = [at(f"xb{i}", 23552 + i * 8192, [128, KD * NT], BF16) for i in range(2)]
        pool_t = [at(f"pool{i}", 39936 + i * 8192, [128, KF * 128], BF16)
                  for i in range(NP)]
        OV = 64512

        pb_t = [ctx.enter_context(nc.psum_tensor(f"pb{i}", [128, NT], F32))
                for i in range(8)]

        EPS_LN = consts[:, 0:1]
        EPS_RMS = consts[:, 1:2]
        NEG_HALF = consts[:, 2:3]
        NEG_ONE = consts[:, 3:4]

        class BankSlot(Slot):
            __slots__ = ("ap",)
        banks = []
        for i in range(8):
            s = BankSlot(f"pb{i}")
            s.ap = pb_t[i][:, :]
            banks.append(s)
        p.banks = banks

        arena = Arena()

        sl_ones = Slot("ones")
        sl_consts = Slot("consts")
        sl_vec = {n: Slot(n) for n in
                  ("bg", "bcn", "n1", "n2", "s1", "s2", "t1", "t2", "s3h")}
        sl_xb = [Slot(f"xb{i}") for i in range(2)]
        sl_xsq = [Slot(f"xsq{i}") for i in range(2)]
        sl_tf = [Slot(f"tf{i}") for i in range(4)]
        sl_rs = [Slot(f"rs{i}") for i in range(2)]
        sl_mr = [Slot(f"mr{i}") for i in range(2)]
        sl_xt = [Slot(f"xt{i}") for i in range(2)]
        sl_xdram = [[Slot(f"xd{r}_{t}") for t in range(NCH)] for r in range(KD)]
        sl_y = [[Slot(f"yy{r}_{t}") for t in range(NCH)] for r in range(KD)]
        reg_pool = [Region(f"pool{i}") for i in range(NP)]

        block = ctx.enter_context(nc.Block())

        p.op("vector", lambda e: e.memset(ones_bf[:, :], 1.0 / D), writes=[sl_ones])
        p.op("vector", lambda e: e.memset(consts[:, 0:1], 1e-5), writes=[sl_consts])
        p.op("vector", lambda e: e.memset(consts[:, 1:2], 1e-6), writes=[sl_consts])
        p.op("vector", lambda e: e.memset(consts[:, 2:3], -0.5), writes=[sl_consts])
        p.op("vector", lambda e: e.memset(consts[:, 3:4], -1.0), writes=[sl_consts])
        sl_consts.frozen = True
        sl_ones.frozen = True
        with nc.allow_non_contiguous_dma(reason="tiny vector loads"):
            for nm, tile, slot in [("v_bg", t_bg, sl_vec["bg"]), ("v_bc", t_bcn, sl_vec["bcn"]),
                                   ("v_n1", t_n1, sl_vec["n1"]), ("v_n2", t_n2, sl_vec["n2"]),
                                   ("v_s1", t_s1, sl_vec["s1"]), ("v_s2", t_s2, sl_vec["s2"]),
                                   ("v_t1", t_t1, sl_vec["t1"]), ("v_t2", t_t2, sl_vec["t2"]),
                                   ("v_s3", t_s3h, sl_vec["s3h"])]:
                p.dma("sync", tile[:, :], vin[nm].transpose([1, 0]), writes=[slot])
        p.op("vector", lambda e: e.tensor_scalar_mul(t_bcn[:, :], t_bcn[:, :], -1.0),
             reads=[sl_vec["bcn"]], writes=[sl_vec["bcn"]])
        p.op("vector", lambda e: e.tensor_scalar_mul(t_s3h[:, :], t_s3h[:, :], 0.5),
             reads=[sl_vec["s3h"]], writes=[sl_vec["s3h"]])
        for s in sl_vec.values():
            s.frozen = True

        def load_xb(src_ap, src_slots, t0, bi):
            dst = xb_t[bi][:, :].rearrange("p (k t) -> p k t", t=NT)
            p.dma("gpsimd", dst, x_kview(src_ap, t0, NT),
                  reads=src_slots, writes=[sl_xb[bi]])
            return sl_xb[bi], xb_t[bi]

        def stats(xbslot, xb_tile, kind, si, sq_src=None, f1_out=None,
                  sq_eng="scalar"):
            xb_ap = lambda k: xb_tile[:, k * NT:(k + 1) * NT]
            sqs, sq_ap = (xbslot, xb_ap) if sq_src is None else sq_src
            bank_m = None
            if kind == "ln":
                bank_m = p.next_bank()
                p.mm_group(bank_m, [(ones_bf[:, :], xb_ap(k), (xbslot, sl_ones))
                                    for k in range(KD)])
            bank_q = p.next_bank()
            for batch in range(KD // 2):
                mms = []
                for j in range(2):
                    k = batch * 2 + j
                    if sq_eng == "scalar":
                        p.op("scalar", lambda e, k=k, j=j:
                             e.activation(xsq_t[j][:, :], sq_ap(k), AF.Square),
                             reads=[sqs], writes=[sl_xsq[j]])
                    else:
                        p.op(sq_eng, lambda e, k=k, j=j:
                             e.tensor_mul(xsq_t[j][:, :], sq_ap(k), sq_ap(k)),
                             reads=[sqs], writes=[sl_xsq[j]])
                    mms.append((ones_bf[:, :], xsq_t[j][:, :], (sl_xsq[j], sl_ones)))
                p.mm_group(bank_q, mms, start=(batch == 0), stop=(batch == KD // 2 - 1))
            r_sl, r_ap = sl_rs[si], rs_t[si][:, :]
            if kind == "ln":
                mr_sl, mr_ap = sl_mr[si], mr_t[si][:, :]
                p.op("scalar", lambda e: e.activation(mr_ap, bank_m.ap, AF.Square),
                     reads=[bank_m], writes=[mr_sl])
                p.op("vector", lambda e: e.tensor_sub(mr_ap, bank_q.ap, mr_ap),
                     reads=[bank_q, mr_sl], writes=[mr_sl])
                p.op("scalar", lambda e: e.activation(r_ap, mr_ap, AF.Ln, bias=EPS_LN),
                     reads=[mr_sl, sl_consts], writes=[r_sl])
                p.op("scalar", lambda e: e.activation(r_ap, r_ap, AF.Exp, scale=NEG_HALF),
                     reads=[r_sl, sl_consts], writes=[r_sl])
                if f1_out is not None:
                    f1sl, f1ap = f1_out
                    p.op("vector", lambda e: e.tensor_mul(f1ap, mr_ap, r_ap),
                         reads=[mr_sl, r_sl], writes=[f1sl])
                    p.op("vector", lambda e: e.tensor_mul(f1ap, f1ap, r_ap),
                         reads=[f1sl, r_sl], writes=[f1sl])
                    p.op("scalar", lambda e: e.activation(f1ap, f1ap, AF.Ln, bias=EPS_RMS),
                         reads=[f1sl, sl_consts], writes=[f1sl])
                    p.op("scalar", lambda e: e.activation(f1ap, f1ap, AF.Exp, scale=NEG_HALF),
                         reads=[f1sl, sl_consts], writes=[f1sl])
                return r_sl, r_ap, mr_sl, bank_m
            else:
                p.op("scalar", lambda e: e.activation(r_ap, bank_q.ap, AF.Ln, bias=EPS_RMS),
                     reads=[bank_q, sl_consts], writes=[r_sl])
                p.op("scalar", lambda e: e.activation(r_ap, r_ap, AF.Exp, scale=NEG_HALF),
                     reads=[r_sl, sl_consts], writes=[r_sl])
                return r_sl, r_ap, None, None

        def ln_apply(in_f, r_sl, r_ap, bank_m, out_f, tfi=(0, 1, 2, 3),
                     mul_eng="vector"):
            nb = len(tfi)
            for h4 in range(KD // nb):
                ks = range(h4 * nb, h4 * nb + nb)
                for k in ks:
                    iap, irds = in_f(k)
                    ti = tfi[k % nb]
                    p.op("vector", lambda e, iap=iap, ti=ti:
                         e.tensor_sub(tf_t[ti][:, :], iap, bank_m.ap),
                         reads=list(irds) + [bank_m], writes=[sl_tf[ti]])
                for k in ks:
                    oap, osl = out_f(k)
                    ti = tfi[k % nb]
                    p.op(mul_eng, lambda e, oap=oap, ti=ti:
                         e.tensor_mul(oap, tf_t[ti][:, :], r_ap),
                         reads=[sl_tf[ti], r_sl], writes=[osl])

        def pool_stage(M, loadf, K=2):
            st = {"slots": {}, "primed": False}

            def load(m):
                pi = m % NP
                reg_pool[pi].turn()
                st["slots"][m] = loadf(m, pool_t[pi], reg_pool[pi])

            def prime():
                if not st["primed"]:
                    st["primed"] = True
                    for m in range(min(K, M)):
                        load(m)

            def run(usef, hook=None):
                prime()
                for m in range(M):
                    if m + K < M:
                        load(m + K)
                    usef(m, pool_t[m % NP], st["slots"].pop(m))
                    if hook:
                        hook(m)
            return prime, run

        def s1_loads(src_ap, src_slots2d, half, tt):
            t0 = half * HT + tt * NT
            ch = t0 // NT
            return load_xb(src_ap, [src_slots2d[r][ch] for r in range(KD)],
                           t0, tt % 2)

        def s1_work(sl_zf_col, zf_ap_half, xb, tt, sq_eng="scalar",
                    tfi=(0, 1, 2, 3), mul_eng="vector"):
            xbs, xbt = xb
            r_sl, r_ap, mr_sl, bank_m = stats(xbs, xbt, "ln", tt % 2, sq_eng=sq_eng)
            ln_apply(lambda k: (xbt[:, k * NT:(k + 1) * NT], (xbs,)),
                     r_sl, r_ap, bank_m,
                     lambda k: (zf_ap_half(k, tt), sl_zf_col[k]), tfi=tfi,
                     mul_eng=mul_eng)

        def emit_ffn(src_ap, src_slots2d, dst_ap, dst_slots2d, A_t, reg_A,
                     zf_t, reg_zf, hooks=None, pre_s1h0=None):
            hooks = hooks or {}
            zf_ap = lambda k, t: zf_t[:, k * HT + t * NT: k * HT + t * NT + NT]
            zf_state = {}

            def do_s1(half, xbs=None):
                reg_zf.turn()
                sl_zf = [[reg_zf.slot(f"zf{half}_{k}_{t}") for t in range(NH)]
                         for k in range(KD)]
                for tt in range(NH):
                    xb = xbs[tt] if xbs else s1_loads(src_ap, src_slots2d, half, tt)
                    s1_work([sl_zf[k][tt] for k in range(KD)], zf_ap, xb, tt)
                zf_state[half] = sl_zf

            def s2_stage(half):
                def loadf(m, pt, reg):
                    w1sl = reg.slot(f"w1_{half}_{m}")
                    w2sl = reg.slot(f"w2_{half}_{m}")
                    p.dma("gpsimd", pt[:, 0:KD * 128], wp["p_w1"][m], writes=[w1sl])
                    p.dma("gpsimd", pt[:, KD * 128:2 * KD * 128], wp["p_w2"][m],
                          writes=[w2sl])
                    return (w1sl, w2sl)
                return pool_stage(KF, loadf)

            def h_pool(wname, tag):
                def loadf(m, pt, reg):
                    wsl = reg.slot(f"{tag}w_{m}")
                    p.dma("gpsimd", pt[:, :], wp[wname][m], writes=[wsl])
                    return wsl
                return pool_stage(KF, loadf)

            def s5_pool(tag):
                def loadf(mo, pt, reg):
                    wsl = reg.slot(f"{tag}_{mo}")
                    p.dma("gpsimd", pt[:, :], wp["p_w3"][mo], writes=[wsl])
                    return wsl
                return pool_stage(KD, loadf)

            for half in range(2):
                th0 = half * HT
                u_i, h1_i = half % 2, (half + 1) % 2

                pr_s2, run_s2 = s2_stage(half)
                if half == 0:
                    if pre_s1h0 is not None:
                        pr_s2()
                        zf_state[0] = pre_s1h0
                    else:
                        xbs = [s1_loads(src_ap, src_slots2d, 0, tt)
                               for tt in range(NH)]
                        pr_s2()
                        do_s1(0, xbs=xbs)
                else:
                    pr_s2()
                sl_zf = zf_state[half]

                reg_A[u_i].turn()
                sl_u = [[reg_A[u_i].slot(f"u{half}_{m}_{t}") for t in range(NH)]
                        for m in range(KF)]
                u_ap = lambda m, t, u_i=u_i: A_t[u_i][:, m * HT + t * NT: m * HT + t * NT + NT]

                def s2_use(m, pt, sl):
                    w1sl, w2sl = sl
                    for t in range(NH):
                        b1 = p.next_bank()
                        b2 = p.next_bank()
                        p.mm_group(b1, [(pt[:, k * 128:(k + 1) * 128], zf_ap(k, t),
                                         (w1sl, sl_zf[k][t])) for k in range(KD)])
                        p.mm_group(b2, [(pt[:, KD * 128 + k * 128:KD * 128 + (k + 1) * 128],
                                         zf_ap(k, t), (w2sl, sl_zf[k][t])) for k in range(KD)])
                        ti = (m * NH + t) % 2
                        p.op("scalar", lambda e, b1=b1, m=m, ti=ti:
                             e.activation(tf_t[ti][:, :], b1.ap, AF.Silu, scale=t_s1[:, m:m + 1]),
                             reads=[b1, sl_vec["s1"]], writes=[sl_tf[ti]])
                        p.op("vector", lambda e, b2=b2, m=m, t=t, ti=ti:
                             e.scalar_tensor_tensor(u_ap(m, t), b2.ap, t_s2[:, m:m + 1],
                                                    tf_t[ti][:, :], ALU.mult, ALU.mult),
                             reads=[b2, sl_tf[ti], sl_vec["s2"]], writes=[sl_u[m][t]])
                run_s2(s2_use)

                def hstage(prun, tvec_t, tvec_sl, in_ap, in_sl, out_i, tag,
                           hook=None):
                    reg_A[out_i].turn()
                    o_sl = [[reg_A[out_i].slot(f"{tag}_{m}_{t}") for t in range(NH)]
                            for m in range(KF)]
                    o_ap = lambda m, t, out_i=out_i: A_t[out_i][:, m * HT + t * NT: m * HT + t * NT + NT]

                    def use(m, pt, wsl):
                        for t in range(NH):
                            b = p.next_bank()
                            p.mm_group(b, [(pt[:, k * 128:(k + 1) * 128],
                                            in_ap(k, t), (wsl, in_sl[k][t]))
                                           for k in range(KF)])
                            p.op("scalar", lambda e, b=b, m=m, t=t:
                                 e.activation(o_ap(m, t), b.ap, AF.Gelu,
                                              scale=tvec_t[:, m:m + 1]),
                                 reads=[b, tvec_sl], writes=[o_sl[m][t]])
                    prun(use, hook=hook)
                    return o_sl, o_ap

                s3hook = None
                if half == 0:
                    s3st = {}

                    def s3hook(m):
                        if m == 4:
                            s3st["xb"] = [s1_loads(src_ap, src_slots2d, 1, tt)
                                          for tt in range(NH)]
                            reg_zf.turn()
                            s3st["sl"] = [[reg_zf.slot(f"zf1_{k}_{t}")
                                           for t in range(NH)] for k in range(KD)]
                elif "s3h1mo" in hooks:
                    s3hook = hooks["s3h1mo"]

                pr_h1, run_h1 = h_pool("p_h1", f"h1_{half}")
                pr_h1()
                sl_h1, h1_ap = hstage(run_h1, t_t1, sl_vec["t1"], u_ap, sl_u,
                                      h1_i, f"h1_{half}", hook=s3hook)
                if half == 0:
                    for tt in range(NH):
                        s1_work([s3st["sl"][k][tt] for k in range(KD)],
                                zf_ap, s3st["xb"][tt], tt)
                    zf_state[1] = s3st["sl"]
                pr_h2, run_h2 = h_pool("p_h2", f"h2_{half}")
                pr_h2()
                sl_h2, h2_ap = hstage(run_h2, t_t2, sl_vec["t2"], h1_ap, sl_h1,
                                      u_i, f"h2_{half}")
                if half == 1 and "s4h1" in hooks:
                    hooks["s4h1"]()

                pr_s5, run_s5 = s5_pool(f"w3_{half}")
                pr_s5()
                ntile = KD * NH
                xt_pend = {}

                def xt_load(i):
                    mo_, t_ = divmod(i, NH)
                    ch = (th0 + t_ * NT) // NT
                    xi_ = i % 2
                    p.dma("sync", xt_t[xi_][:, :],
                          src_ap[mo_ * 128:(mo_ + 1) * 128, th0 + t_ * NT: th0 + (t_ + 1) * NT],
                          reads=[src_slots2d[mo_][ch]], writes=[sl_xt[xi_]])
                xt_load(0)

                def s5_use(mo, pt, wsl):
                    for t in range(NH):
                        i = mo * NH + t
                        ch = (th0 + t * NT) // NT
                        b = p.next_bank()
                        p.mm_group(b, [(pt[:, k * 128:(k + 1) * 128],
                                        h2_ap(k, t), (wsl, sl_h2[k][t]))
                                       for k in range(KF)])
                        if i + 1 < ntile:
                            xt_load(i + 1)
                        xi_ = i % 2
                        xs = sl_xt[xi_]
                        p.op("vector", lambda e, b=b, mo=mo, xi_=xi_:
                             e.scalar_tensor_tensor(xt_t[xi_][:, :], b.ap, t_s3h[:, mo:mo + 1],
                                                    xt_t[xi_][:, :], ALU.mult, ALU.add),
                             reads=[b, xs, sl_vec["s3h"]], writes=[xs])
                        p.dma("sync",
                              dst_ap[mo * 128:(mo + 1) * 128, th0 + t * NT: th0 + (t + 1) * NT],
                              xt_t[xi_][:, :], reads=[xs], writes=[dst_slots2d[mo][ch]])
                hook = hooks.get("s5h1mo") if half == 1 else None
                run_s5(s5_use, hook=hook)

        attn = {}

        def attn_carve_head():
            attn["wg_t"] = at("wg", OV + 0, [128, KD * 1024], BF16)
            attn["reg_wg"] = arena.region("wg", 0, KD * 2048)
            attn["sl_wg"] = [attn["reg_wg"].slot(f"wg{m}") for m in range(KD)]

        def attn_carve_head2():
            attn["wc_t"] = at("wc", OV + 16384, [128, KD * 1024], BF16)
            attn["reg_wc"] = arena.region("wc", 16384, KD * 2048)
            attn["sl_wc"] = [attn["reg_wc"].slot(f"wc{m}") for m in range(KD)]
            attn["xi_t"] = [at(f"xi{i}", OV + 32768 + i * 16384, [128, KD * NT], F32)
                            for i in range(2)]
            attn["reg_xi"] = [arena.region(f"xi{i}", 32768 + i * 16384, KD * NT * 4)
                              for i in range(2)]
            attn["z1_t"] = [at(f"z1{i}", OV + 65536 + i * 8192, [128, KD * NT], BF16)
                            for i in range(2)]
            attn["reg_z1"] = [arena.region(f"z1{i}", 65536 + i * 8192, KD * NT * 2)
                              for i in range(2)]

        def attn_carve_body():
            attn["g_t"] = at("g", OV + 81920, [128, 2 * NT], F32)
            rg = arena.region("g", 81920, 2 * NT * 4)
            attn["b_t"] = at("b", OV + 86016, [128, 2 * NT], F32)
            rb = arena.region("b", 86016, 2 * NT * 4)
            attn["h_t"] = at("h", OV + 90112, [128, 2 * NT], F32)
            rh = arena.region("h", 90112, 2 * NT * 4)
            attn["hl_t"] = at("hl", OV + 94208, [128, 16], F32)
            rhl = arena.region("hl", 94208, 64)
            attn["tb_t"] = at("tb", OV + 94720, [128, NT], F32)
            rtb = arena.region("tb", 94720, NT * 4)
            attn["sl_g"] = [rg.slot(f"gp{i}") for i in range(2)]
            attn["sl_b"] = [rb.slot(f"bp{i}") for i in range(2)]
            attn["sl_h"] = [rh.slot(f"hp{i}") for i in range(2)]
            attn["sl_hl"] = [rhl.slot(f"hl{i}") for i in range(2)]
            attn["sl_tb"] = rtb.slot("tb")
            attn["z2_t"] = at("z2", OV + 98304, [128, KD * T], BF16)
            attn["reg_z2"] = arena.region("z2", 98304, KD * T * 2)
            attn["sl_z2"] = [[attn["reg_z2"].slot(f"z2_{k}_{c}") for c in range(NCH)]
                             for k in range(KD)]

        def z2_ap(k, c):
            return attn["z2_t"][:, k * T + c * NT: k * T + c * NT + NT]

        AC = {}

        def attn_load_xb(c):
            bi = c % 2
            src_sl = [sl_xdram[r][c] for r in range(KD)]
            AC.setdefault(c, {})["xb"] = load_xb(xdram, src_sl, c * NT, bi)

        def attn_load_xi(c):
            bi = c % 2
            attn["reg_xi"][bi].turn()
            xisl = attn["reg_xi"][bi].slot(f"xi{c}")
            src_sl = [sl_xdram[r][c] for r in range(KD)]
            p.dma("gpsimd", attn["xi_t"][bi][:, :].rearrange("p (k t) -> p k t", t=NT),
                  x_kview(xdram, c * NT, NT), reads=src_sl, writes=[xisl])
            AC.setdefault(c, {})["xi"] = xisl

        def attn_loads(c):
            attn_load_xb(c)
            attn_load_xi(c)

        def attn_head(c):
            bi = c % 2
            xbs, xbt = AC[c]["xb"]
            xisl = AC[c]["xi"]
            xi_ap = lambda k, bi=bi: attn["xi_t"][bi][:, k * NT:(k + 1) * NT]
            r_sl, r_ap, mr_sl, bank_m = stats(
                xbs, xbt, "ln", bi, f1_out=(sl_mr[bi], mr_t[bi][:, :]),
                sq_eng="gpsimd")
            ln_apply(lambda k: (xi_ap(k), (xisl,)), r_sl, r_ap, bank_m,
                     lambda k: (xi_ap(k), xisl), mul_eng="gpsimd")
            attn["reg_z1"][bi].turn()
            z1sl = attn["reg_z1"][bi].slot(f"z1_{c}")
            z1_ap = lambda k, bi=bi: attn["z1_t"][bi][:, k * NT:(k + 1) * NT]
            for k in range(KD):
                p.op("vector", lambda e, k=k, bi=bi:
                     e.scalar_tensor_tensor(z1_ap(k), xi_ap(k), t_n1[:, k:k + 1],
                                            mr_t[bi][:, :], ALU.mult, ALU.mult),
                     reads=[xisl, sl_mr[bi], sl_vec["n1"]], writes=[z1sl])
            AC[c]["z1"] = z1sl

        def attn_gates_scan(c):
            bi = c % 2
            z1sl = AC[c]["z1"]
            xisl = AC[c]["xi"]
            xi_ap = lambda k, bi=bi: attn["xi_t"][bi][:, k * NT:(k + 1) * NT]
            z1_ap = lambda k, bi=bi: attn["z1_t"][bi][:, k * NT:(k + 1) * NT]
            wg_ap = lambda m, k: attn["wg_t"][:, m * 1024 + k * 128: m * 1024 + (k + 1) * 128]
            wc_ap = lambda m, k: attn["wc_t"][:, m * 1024 + k * 128: m * 1024 + (k + 1) * 128]
            g_t, b_t, h_t = attn["g_t"], attn["b_t"], attn["h_t"]
            hl_t, tb_t = attn["hl_t"], attn["tb_t"]
            sl_g, sl_b, sl_h = attn["sl_g"], attn["sl_b"], attn["sl_h"]
            sl_hl, sl_tb = attn["sl_hl"], attn["sl_tb"]
            for m in range(KD):
                pi = m % 2
                g_ap = g_t[:, pi * NT:(pi + 1) * NT]
                b_ap = b_t[:, pi * NT:(pi + 1) * NT]
                h_ap = h_t[:, pi * NT:(pi + 1) * NT]
                bg_ = p.next_bank()
                bc_ = p.next_bank()
                p.mm_group(bg_, [(wg_ap(m, k), z1_ap(k), (attn["sl_wg"][m], z1sl))
                                 for k in range(KD)])
                p.mm_group(bc_, [(wc_ap(m, k), z1_ap(k), (attn["sl_wc"][m], z1sl))
                                 for k in range(KD)])
                p.op("scalar", lambda e, bg_=bg_, m=m, g_ap=g_ap:
                     e.activation(g_ap, bg_.ap, AF.Sigmoid, bias=t_bg[:, m:m + 1]),
                     reads=[bg_, sl_vec["bg"]], writes=[sl_g[pi]])
                p.op("scalar", lambda e, bc_=bc_, m=m, b_ap=b_ap:
                     e.activation(b_ap, bc_.ap, AF.Tanh,
                                  scale=NEG_ONE, bias=t_bcn[:, m:m + 1]),
                     reads=[bc_, sl_vec["bcn"], sl_consts], writes=[sl_b[pi]])
                p.op("vector", lambda e, g_ap=g_ap, b_ap=b_ap:
                     e.scalar_tensor_tensor(b_ap, g_ap, 1.0, b_ap,
                                            ALU.subtract, ALU.mult),
                     reads=[sl_g[pi], sl_b[pi]], writes=[sl_b[pi]])
                if c == 0:
                    init = 0.0
                    rds = [sl_g[pi], sl_b[pi]]
                else:
                    init = hl_t[:, ((c - 1) % 2) * 8 + m: ((c - 1) % 2) * 8 + m + 1]
                    rds = [sl_g[pi], sl_b[pi], sl_hl[(c - 1) % 2]]
                p.op("vector", lambda e, init=init, h_ap=h_ap, g_ap=g_ap, b_ap=b_ap:
                     e.tensor_tensor_scan(h_ap, g_ap, b_ap, init,
                                          ALU.mult, ALU.add),
                     reads=rds, writes=[sl_h[pi]])
                p.op("gpsimd", lambda e, c=c, m=m, h_ap=h_ap:
                     e.tensor_copy(hl_t[:, (c % 2) * 8 + m: (c % 2) * 8 + m + 1],
                                   h_ap[:, NT - 1:NT]),
                     reads=[sl_h[pi]], writes=[sl_hl[c % 2]])
                p.op("gpsimd", lambda e, m=m, h_ap=h_ap:
                     e.tensor_add(xi_ap(m), xi_ap(m), h_ap),
                     reads=[sl_h[pi], xisl], writes=[xisl])

        def attn_deferred(c):
            bi = c % 2
            xisl = AC[c]["xi"]
            xi_ap = lambda k, bi=bi: attn["xi_t"][bi][:, k * NT:(k + 1) * NT]
            r2_sl, r2_ap, _, _ = stats(None, xb_t[bi], "rms", bi,
                                       sq_src=(xisl, xi_ap))
            for k in range(KD):
                p.op("vector", lambda e, k=k, c=c:
                     e.scalar_tensor_tensor(z2_ap(k, c), xi_ap(k), t_n2[:, k:k + 1],
                                            r2_ap, ALU.mult, ALU.mult),
                     reads=[xisl, r2_sl, sl_vec["n2"]], writes=[attn["sl_z2"][k][c]])
            p.dma("gpsimd", x_kview(xdram, c * NT, NT),
                  attn["xi_t"][bi][:, :].rearrange("p (k t) -> p k t", t=NT),
                  reads=[xisl] + [sl_xdram[r][c] for r in range(KD)],
                  writes=[sl_xdram[r][c] for r in range(KD)],
                  accum_op=ALU.add)

        def make_attn_B(w13h1_hook=None):
            def w13_pool(hb):
                def loadf(m, pt, reg):
                    w1sl = reg.slot(f"aw1_{hb}_{m}")
                    w3sl = reg.slot(f"aw3_{hb}_{m}")
                    p.dma("gpsimd", pt[:, 0:KD * 128], wp["p_aw1"][m], writes=[w1sl])
                    p.dma("gpsimd", pt[:, KD * 128:2 * KD * 128], wp["p_aw3"][m],
                          writes=[w3sl])
                    return (w1sl, w3sl)
                return pool_stage(KF, loadf)

            def w2_pool(hb):
                def loadf(mo, pt, reg):
                    wsl = reg.slot(f"aw2_{hb}_{mo}")
                    p.dma("gpsimd", pt[:, :], wp["p_aw2"][mo], writes=[wsl])
                    return wsl
                return pool_stage(KD, loadf)

            st_w13 = [w13_pool(0), w13_pool(1)]
            st_w2 = [w2_pool(0), w2_pool(1)]

            def prime_h0():
                st_w13[0][0]()

            def run():
                uh_t = at("uh", OV + 0, [128, KF * HT], BF16)
                reg_uh = arena.region("uh", 0, KF * HT * 2)
                for hb in range(2):
                    th0 = hb * HT
                    reg_uh.turn()
                    sl_uh = [[reg_uh.slot(f"uh{hb}_{m}_{t}") for t in range(NH)]
                             for m in range(KF)]
                    uh_ap = lambda m, t: uh_t[:, m * HT + t * NT: m * HT + t * NT + NT]

                    def w13_use(m, pt, sl):
                        w1sl, w3sl = sl
                        for t in range(NH):
                            c = hb * NH + t
                            b1 = p.next_bank()
                            b3 = p.next_bank()
                            p.mm_group(b1, [(pt[:, k * 128:(k + 1) * 128], z2_ap(k, c),
                                             (w1sl, attn["sl_z2"][k][c])) for k in range(KD)])
                            p.mm_group(b3, [(pt[:, KD * 128 + k * 128:KD * 128 + (k + 1) * 128],
                                             z2_ap(k, c), (w3sl, attn["sl_z2"][k][c]))
                                            for k in range(KD)])
                            ti = (m * NH + t) % 2
                            p.op("scalar", lambda e, b1=b1, ti=ti:
                                 e.activation(tf_t[ti][:, :], b1.ap, AF.Silu),
                                 reads=[b1], writes=[sl_tf[ti]])
                            p.op("vector", lambda e, b3=b3, m=m, t=t, ti=ti:
                                 e.tensor_mul(uh_ap(m, t), tf_t[ti][:, :], b3.ap),
                                 reads=[b3, sl_tf[ti]], writes=[sl_uh[m][t]])
                    hook = w13h1_hook if hb == 1 else None
                    st_w13[hb][1](w13_use, hook=hook)

                    ntile = KD * NH
                    st_w2[hb][0]()

                    def xt_load(i):
                        mo_, t_ = divmod(i, NH)
                        c = hb * NH + t_
                        xi_ = i % 2
                        p.dma("sync", xt_t[xi_][:, :],
                              xdram[mo_ * 128:(mo_ + 1) * 128, th0 + t_ * NT: th0 + (t_ + 1) * NT],
                              reads=[sl_xdram[mo_][c]], writes=[sl_xt[xi_]])
                    xt_load(0)

                    def w2_use(mo, pt, wsl):
                        for t in range(NH):
                            c = hb * NH + t
                            i = mo * NH + t
                            b = p.next_bank()
                            p.mm_group(b, [(pt[:, k * 128:(k + 1) * 128],
                                            uh_ap(k, t), (wsl, sl_uh[k][t]))
                                           for k in range(KF)])
                            if i + 1 < ntile:
                                xt_load(i + 1)
                            xi_ = i % 2
                            xs = sl_xt[xi_]
                            p.op("vector", lambda e, b=b, xi_=xi_:
                                 e.tensor_add(xt_t[xi_][:, :], xt_t[xi_][:, :], b.ap),
                                 reads=[b, xs], writes=[xs])
                            p.dma("sync",
                                  xdram[mo * 128:(mo + 1) * 128, th0 + t * NT: th0 + (t + 1) * NT],
                                  xt_t[xi_][:, :], reads=[xs], writes=[sl_xdram[mo][c]])
                    st_w2[hb][1](w2_use)
                    if hb == 0:
                        st_w13[1][0]()
            return prime_h0, run

        xin_slots = [[Slot(f"xin{r}_{t}", frozen=True) for t in range(NCH)]
                     for r in range(KD)]

        zf_t = at("zf", OV + 0, [128, KD * HT], BF16)
        reg_zf = arena.region("zf", 0, KD * HT * 2)
        A_t = [at("A0", OV + 16384, [128, KF * HT], BF16),
               at("A1", OV + 81920, [128, KF * HT], BF16)]
        reg_A = [arena.region("A0", 16384, KF * HT * 2),
                 arena.region("A1", 81920, KF * HT * 2)]

        def hook_s3h1mo(m):
            if m == 2:
                attn_carve_head()
            if m >= 4 and m % 4 == 0 and m <= 28:
                for mw in ([m // 4 - 1] if m < 28 else [6, 7]):
                    p.dma("gpsimd", attn["wg_t"][:, mw * 1024:(mw + 1) * 1024],
                          wp["p_wg"][mw], writes=[attn["sl_wg"][mw]])

        def hook_s4h1():
            attn_carve_head2()

        def hook_s5h1mo(mo):
            if mo == 0:
                attn_loads(0)
            elif mo == 1:
                attn_loads(1)
            elif mo == 2:
                for m in range(4):
                    p.dma("gpsimd", attn["wc_t"][:, m * 1024:(m + 1) * 1024],
                          wp["p_wc"][m], writes=[attn["sl_wc"][m]])
            elif mo == 3:
                for m in range(4, KD):
                    p.dma("gpsimd", attn["wc_t"][:, m * 1024:(m + 1) * 1024],
                          wp["p_wc"][m], writes=[attn["sl_wc"][m]])
            elif mo == 4:
                attn_head(0)
            elif mo == 6:
                attn_head(1)

        emit_ffn(xin, xin_slots, xdram, sl_xdram, A_t, reg_A, zf_t, reg_zf,
                 hooks={"s3h1mo": hook_s3h1mo, "s4h1": hook_s4h1,
                        "s5h1mo": hook_s5h1mo})

        zfb_t = None

        def mk_ffn2_s1_hooks():
            state = {}

            def hook(m):
                nonlocal zfb_t
                if m == 2:
                    zfb_t = at("zfb", OV + 65536, [128, KD * HT], BF16)
                    state["reg"] = arena.region("zfb", 65536, KD * HT * 2)
                    state["sl"] = [[state["reg"].slot(f"zfb0_{k}_{t}")
                                    for t in range(NH)] for k in range(KD)]
                    state["ap"] = lambda k, t: zfb_t[:, k * HT + t * NT: k * HT + t * NT + NT]
                    state["xb0"] = load_xb(xdram, [sl_xdram[r][0] for r in range(KD)],
                                           0, 0)
                elif m == 8:
                    xbs, xbt = state["xb0"]
                    r_sl, r_ap, mr_sl, bank_m = stats(xbs, xbt, "ln", 0,
                                                      sq_eng="scalar")
                    ln_apply(lambda k: (xbt[:, k * NT:(k + 1) * NT], (xbs,)),
                             r_sl, r_ap, bank_m,
                             lambda k: (state["ap"](k, 0), state["sl"][k][0]),
                             tfi=(2, 3))
                elif m == 16:
                    state["xb1"] = load_xb(xdram, [sl_xdram[r][1] for r in range(KD)],
                                           NT, 1)
                elif m == 22:
                    xbs, xbt = state["xb1"]
                    r_sl, r_ap, mr_sl, bank_m = stats(xbs, xbt, "ln", 1,
                                                      sq_eng="scalar")
                    ln_apply(lambda k: (xbt[:, k * NT:(k + 1) * NT], (xbs,)),
                             r_sl, r_ap, bank_m,
                             lambda k: (state["ap"](k, 1), state["sl"][k][1]),
                             tfi=(2, 3))
            return state, hook

        attn_carve_body()
        s1_state, s1_hook = mk_ffn2_s1_hooks()
        primeB, runB = make_attn_B(w13h1_hook=s1_hook)
        primeB()

        for c in range(NCH):
            if 1 <= c and c + 1 < NCH:
                attn_load_xb(c + 1)
            attn_gates_scan(c)
            if c >= 1:
                attn_deferred(c - 1)
                if c + 1 < NCH:
                    attn_load_xi(c + 1)
            if 1 <= c and c + 1 < NCH:
                attn_head(c + 1)
        attn_deferred(NCH - 1)

        runB()

        A2_t = [at("A0b", OV + 0, [128, KF * HT], BF16),
                at("A1b", OV + 81920, [128, KF * HT], BF16)]
        reg_A2 = [arena.region("A0b", 0, KF * HT * 2),
                  arena.region("A1b", 81920, KF * HT * 2)]
        reg_zf2 = s1_state["reg"]
        emit_ffn(xdram, sl_xdram, y_d, sl_y, A2_t, reg_A2, zfb_t, reg_zf2,
                 pre_s1h0=[[s1_state["sl"][k][t] for t in range(NH)]
                           for k in range(KD)])

        for i in range(NQSEM):
            if p.cnt[f"sy{i}"]:
                p.eng("sync").wait_ge(p.sem[f"sy{i}"], p.cnt[f"sy{i}"])
            if p.cnt[f"wg{i}"]:
                p.eng("gpsimd").wait_ge(p.sem[f"wg{i}"], p.cnt[f"wg{i}"])

    return nc



def _swizzle_panels(w_io):
    I, O = w_io.shape
    return np.ascontiguousarray(
        w_io.reshape(I // 128, 128, O // 128, 128).transpose(2, 1, 0, 3)
        .reshape(O // 128, 128, I))


def make_core_inputs(xb_core, W):
    d = {"xin": np.ascontiguousarray(xb_core.T)}
    d["p_w1"] = W["p_w1"]; d["p_w2"] = W["p_w2"]; d["p_h1"] = W["p_h1"]
    d["p_h2"] = W["p_h2"]; d["p_w3"] = W["p_w3"]; d["p_wg"] = W["p_wg"]
    d["p_wc"] = W["p_wc"]; d["p_aw1"] = W["p_aw1"]; d["p_aw3"] = W["p_aw3"]
    d["p_aw2"] = W["p_aw2"]
    for k in ("v_bg", "v_bc", "v_n1", "v_n2", "v_s1", "v_s2", "v_t1", "v_t2", "v_s3"):
        d[k] = W[k]
    return d


def prep_weights(L0_attn_Wg, L0_attn_bg, L0_attn_Wc, L0_attn_bc, L0_attn_n1_w,
                 L0_attn_n2_w, L0_attn_W1, L0_attn_W3, L0_attn_W2,
                 L0_ffn_w1_w, L0_ffn_w1_s, L0_ffn_w2_w, L0_ffn_w2_s,
                 L0_ffn_h1_w, L0_ffn_h1_s, L0_ffn_h2_w, L0_ffn_h2_s,
                 L0_ffn_w3_w, L0_ffn_w3_s):
    f32 = lambda a: np.ascontiguousarray(np.asarray(a), dtype=np.float32)
    W = {}
    W["p_w1"] = _swizzle_panels(f32(L0_ffn_w1_w).T)
    W["p_w2"] = _swizzle_panels(f32(L0_ffn_w2_w).T)
    W["p_h1"] = _swizzle_panels(f32(L0_ffn_h1_w).T)
    W["p_h2"] = _swizzle_panels(f32(L0_ffn_h2_w).T)
    W["p_w3"] = _swizzle_panels(f32(L0_ffn_w3_w).T)
    W["p_wg"] = _swizzle_panels(f32(L0_attn_Wg))
    W["p_wc"] = _swizzle_panels(f32(L0_attn_Wc))
    W["p_aw1"] = _swizzle_panels(f32(L0_attn_W1))
    W["p_aw3"] = _swizzle_panels(f32(L0_attn_W3))
    W["p_aw2"] = _swizzle_panels(f32(L0_attn_W2))
    W["v_bg"] = f32(L0_attn_bg).reshape(KD, 128)
    W["v_bc"] = f32(L0_attn_bc).reshape(KD, 128)
    W["v_n1"] = f32(L0_attn_n1_w).reshape(KD, 128)
    W["v_n2"] = f32(L0_attn_n2_w).reshape(KD, 128)
    W["v_s1"] = f32(L0_ffn_w1_s).reshape(KF, 128)
    W["v_s2"] = f32(L0_ffn_w2_s).reshape(KF, 128)
    W["v_t1"] = f32(L0_ffn_h1_s).reshape(KF, 128)
    W["v_t2"] = f32(L0_ffn_h2_s).reshape(KF, 128)
    W["v_s3"] = f32(L0_ffn_w3_s).reshape(KD, 128)
    return W


def kernel(L0_attn_x, L0_attn_Wg, L0_attn_bg, L0_attn_Wc, L0_attn_bc,
           L0_attn_n1_w, L0_attn_n2_w, L0_attn_W1, L0_attn_W3, L0_attn_W2,
           L0_ffn_w1_w, L0_ffn_w1_s, L0_ffn_w2_w, L0_ffn_w2_s,
           L0_ffn_h1_w, L0_ffn_h1_s, L0_ffn_h2_w, L0_ffn_h2_s,
           L0_ffn_w3_w, L0_ffn_w3_s):
    x = np.ascontiguousarray(np.asarray(L0_attn_x), dtype=np.float32)
    B, T, _ = x.shape
    W = prep_weights(L0_attn_Wg, L0_attn_bg, L0_attn_Wc, L0_attn_bc, L0_attn_n1_w,
                     L0_attn_n2_w, L0_attn_W1, L0_attn_W3, L0_attn_W2,
                     L0_ffn_w1_w, L0_ffn_w1_s, L0_ffn_w2_w, L0_ffn_w2_s,
                     L0_ffn_h1_w, L0_ffn_h1_s, L0_ffn_h2_w, L0_ffn_h2_s,
                     L0_ffn_w3_w, L0_ffn_w3_s)
    nc = build_program(T=T)
    in_maps = [make_core_inputs(x[i], W) for i in range(B)]
    res = run_bass_kernel_spmd(nc, in_maps, core_ids=list(range(B)))
    out = np.stack([res.results[i]["y"] for i in range(B)], axis=0)
    return np.ascontiguousarray(out.transpose(0, 2, 1))


# version 11
# speedup vs baseline: 1.0341x; 1.0341x over previous
import numpy as np
import ml_dtypes
from contextlib import ExitStack

import concourse.bass as bass
import concourse.mybir as mybir
from concourse.bass_utils import run_bass_kernel_spmd

F32 = mybir.dt.float32
BF16 = mybir.dt.bfloat16
AF = mybir.ActivationFunctionType
ALU = mybir.AluOpType

D = 1024
DFF = 4096
KD = D // 128
KF = DFF // 128
NT = 512
NP = 3

ENG_SEM = {"tensor": "pe", "scalar": "sc", "vector": "ve", "gpsimd": "gp"}
NQSEM = 8


class Slot:
    __slots__ = ("ready", "weng", "wseq", "users", "frozen", "name")

    def __init__(self, name="", users=None, frozen=False):
        self.ready = None
        self.weng = None
        self.wseq = 0
        self.users = list(users) if users else []
        self.frozen = frozen
        self.name = name


def marks_max(slots):
    mx = {}
    for s in slots:
        for m in ([s.ready] if s.ready else []) + s.users:
            if m[1] > mx.get(m[0], 0):
                mx[m[0]] = m[1]
    return [(k, v) for k, v in mx.items()]


class Region:

    def __init__(self, name, marks0=()):
        self.name = name
        self.marks = list(marks0)
        self.cur = []

    def slot(self, name):
        s = Slot(name, users=list(self.marks))
        self.cur.append(s)
        return s

    def turn(self):
        if self.cur:
            self.marks = marks_max(self.cur)
            self.cur = []

    def snapshot(self):
        mx = dict(self.marks)
        for sem, v in marks_max(self.cur):
            if v > mx.get(sem, 0):
                mx[sem] = v
        return mx


class Arena:

    def __init__(self):
        self.carves = []

    def region(self, name, start, nbytes):
        mx = {}
        for s, e, r in self.carves:
            if s < start + nbytes and start < e:
                for sem, v in r.snapshot().items():
                    if v > mx.get(sem, 0):
                        mx[sem] = v
        reg = Region(name, list(mx.items()))
        self.carves.append((start, start + nbytes, reg))
        return reg


class P:
    def __init__(self, nc, ctx):
        self.nc = nc
        names = ["pe", "sc", "ve", "gp"] + [f"sy{i}" for i in range(NQSEM)] + \
                [f"wg{i}" for i in range(NQSEM)]
        self.sem = {s: ctx.enter_context(nc.semaphore("s_" + s)) for s in names}
        self.cnt = {s: 0 for s in names}
        ENGS = ("tensor", "scalar", "vector", "sync", "gpsimd")
        self.waited = {e: {s: 0 for s in names} for e in ENGS}
        self.nseq = {e: 0 for e in ENGS}
        self.sealed = {e: 0 for e in ENGS}
        self.qi = {"sync": 0, "gpsimd": 0}
        self.bank_i = 0
        self.banks = None

    def eng(self, name):
        return getattr(self.nc, name)

    def wait(self, ename, dep):
        if dep is None:
            return
        sem, val = dep
        if val <= 0 or self.waited[ename][sem] >= val or ENG_SEM.get(ename) == sem:
            return
        self.eng(ename).wait_ge(self.sem[sem], val)
        self.waited[ename][sem] = val

    def _pre(self, ename, reads, writes):
        own = ENG_SEM.get(ename)
        need_drain = False
        for s in reads:
            if s.ready is None:
                assert s.frozen, f"read of unwritten slot {s.name}"
                continue
            if own is not None and s.ready[0] == own:
                if s.ready[1] > self.sealed[ename]:
                    need_drain = True
            else:
                self.wait(ename, s.ready)
        for s in writes:
            assert not s.frozen, f"write to frozen slot {s.name}"
            for u in s.users:
                if own is not None and u[0] == own:
                    if u[1] > self.sealed[ename]:
                        need_drain = True
                else:
                    self.wait(ename, u)
            if s.ready is not None:
                if not (s.weng == ename and own is not None and s.ready[0] == own):
                    self.wait(ename, s.ready)
        if need_drain:
            self.eng(ename).drain()
            self.sealed[ename] = self.cnt[own]

    def _post(self, ename, ins, sem, n, reads, writes):
        self.nseq[ename] += 1
        ins.then_inc(self.sem[sem], n)
        self.cnt[sem] += n
        mark = (sem, self.cnt[sem])
        for s in reads:
            if not s.frozen:
                s.users.append(mark)
        for s in writes:
            s.ready = mark
            s.weng = ename
            s.wseq = self.nseq[ename]
            s.users = []
        return mark

    def op(self, ename, emit, reads=(), writes=()):
        self._pre(ename, reads, writes)
        ins = emit(self.eng(ename))
        return self._post(ename, ins, ENG_SEM[ename], 1, reads, writes)

    def dma(self, q, out_ap, in_ap, reads=(), writes=(), **kw):
        pfx = "sy" if q == "sync" else "wg"
        sem = f"{pfx}{self.qi[q] % NQSEM}"
        self.qi[q] += 1
        self.wait(q, (sem, self.cnt[sem]))
        self._pre(q, reads, writes)
        ins = self.eng(q).dma_start(out_ap, in_ap, **kw)
        return self._post(q, ins, sem, 16, reads, writes)

    def mm_group(self, bank, mms, start=True, stop=True):
        ename = "tensor"
        if start:
            for u in bank.users:
                self.wait(ename, u)
        n = len(mms)
        for _, _, rds in mms:
            for s in rds:
                if s.ready is not None:
                    self.wait(ename, s.ready)
        for i, (l, r, rds) in enumerate(mms):
            ins = self.nc.tensor.matmul(bank.ap, l, r, start=(start and i == 0),
                                        stop=(stop and i == n - 1))
            self.nseq[ename] += 1
            if i == n - 1:
                ins.then_inc(self.sem["pe"], 1)
                self.cnt["pe"] += 1
        mark = ("pe", self.cnt["pe"])
        if stop:
            bank.ready = mark
            bank.weng = ename
            bank.users = []
        for _, _, rds in mms:
            for s in rds:
                if not s.frozen:
                    s.users.append(mark)
        return mark

    def next_bank(self):
        b = self.banks[self.bank_i % 8]
        self.bank_i += 1
        return b


def build_program(T=2048):
    HT = T // 2
    NH = HT // NT
    NCH = T // NT
    nc = bass.Bass("TRN2", target_bir_lowering=False)

    xin = nc.dram_tensor("xin", [D, T], F32, kind="ExternalInput").ap()
    y_d = nc.dram_tensor("y", [D, T], F32, kind="ExternalOutput").ap()
    xdram = nc.dram_tensor("xdram", [D, T], F32).ap()

    wp = {}
    for name, mch, kch in [("p_w1", KF, KD), ("p_w2", KF, KD), ("p_h1", KF, KF),
                           ("p_h2", KF, KF), ("p_w3", KD, KF), ("p_wg", KD, KD),
                           ("p_wc", KD, KD), ("p_aw1", KF, KD), ("p_aw3", KF, KD),
                           ("p_aw2", KD, KF)]:
        wp[name] = nc.dram_tensor(name, [mch, 128, kch * 128], F32,
                                  kind="ExternalInput").ap()
    vin = {}
    for name, chk in [("v_bg", KD), ("v_bc", KD), ("v_n1", KD), ("v_n2", KD),
                      ("v_s1", KF), ("v_s2", KF), ("v_t1", KF), ("v_t2", KF),
                      ("v_s3", KD)]:
        vin[name] = nc.dram_tensor(name, [chk, 128], F32, kind="ExternalInput").ap()

    def x_kview(xap, t0, tl):
        return xap.rearrange("(k p) t -> p k t", p=128)[:, :, t0:t0 + tl]

    with ExitStack() as ctx:
        p = P(nc, ctx)
        BASE = (nc.sbuf_base + 31) & ~31
        SLAB = 211968
        ctx.enter_context(nc.sbuf_tensor("slab", [128, SLAB], mybir.dt.int8))
        at = lambda name, off, shape, dt: nc.alloc_sbuf_tensor_at(
            name, shape, dt, offset=BASE + off)

        ones_bf = at("ones_bf", 0, [128, 128], BF16)
        consts = at("consts", 256, [128, 4], F32)
        t_bg = at("t_bg", 288, [128, KD], F32)
        t_bcn = at("t_bcn", 320, [128, KD], F32)
        t_n1 = at("t_n1", 352, [128, KD], F32)
        t_n2 = at("t_n2", 384, [128, KD], F32)
        t_s3h = at("t_s3h", 416, [128, KD], F32)
        t_s1 = at("t_s1", 448, [128, KF], F32)
        t_s2 = at("t_s2", 576, [128, KF], F32)
        t_t1 = at("t_t1", 704, [128, KF], F32)
        t_t2 = at("t_t2", 832, [128, KF], F32)
        xsq_t = [at(f"xsq{i}", 1024 + i * 1024, [128, NT], BF16) for i in range(2)]
        tf_t = [at(f"tf{i}", 3072 + i * 2048, [128, NT], F32) for i in range(4)]
        rs_t = [at(f"rs{i}", 11264 + i * 2048, [128, NT], F32) for i in range(2)]
        mr_t = [at(f"mr{i}", 15360 + i * 2048, [128, NT], F32) for i in range(2)]
        xt_t = [at(f"xt{i}", 19456 + i * 2048, [128, NT], F32) for i in range(2)]
        xb_t = [at(f"xb{i}", 23552 + i * 8192, [128, KD * NT], BF16) for i in range(2)]
        pool_t = [at(f"pool{i}", 39936 + i * 8192, [128, KF * 128], BF16)
                  for i in range(NP)]
        OV = 64512

        pb_t = [ctx.enter_context(nc.psum_tensor(f"pb{i}", [128, NT], F32))
                for i in range(8)]

        EPS_LN = consts[:, 0:1]
        EPS_RMS = consts[:, 1:2]
        NEG_HALF = consts[:, 2:3]
        NEG_ONE = consts[:, 3:4]

        class BankSlot(Slot):
            __slots__ = ("ap",)
        banks = []
        for i in range(8):
            s = BankSlot(f"pb{i}")
            s.ap = pb_t[i][:, :]
            banks.append(s)
        p.banks = banks

        arena = Arena()

        sl_ones = Slot("ones")
        sl_consts = Slot("consts")
        sl_vec = {n: Slot(n) for n in
                  ("bg", "bcn", "n1", "n2", "s1", "s2", "t1", "t2", "s3h")}
        sl_xb = [Slot(f"xb{i}") for i in range(2)]
        sl_xsq = [Slot(f"xsq{i}") for i in range(2)]
        sl_tf = [Slot(f"tf{i}") for i in range(4)]
        sl_rs = [Slot(f"rs{i}") for i in range(2)]
        sl_mr = [Slot(f"mr{i}") for i in range(2)]
        sl_xt = [Slot(f"xt{i}") for i in range(2)]
        sl_xdram = [[Slot(f"xd{r}_{t}") for t in range(NCH)] for r in range(KD)]
        sl_y = [[Slot(f"yy{r}_{t}") for t in range(NCH)] for r in range(KD)]
        reg_pool = [Region(f"pool{i}") for i in range(NP)]

        block = ctx.enter_context(nc.Block())

        p.op("vector", lambda e: e.memset(ones_bf[:, :], 1.0 / D), writes=[sl_ones])
        p.op("vector", lambda e: e.memset(consts[:, 0:1], 1e-5), writes=[sl_consts])
        p.op("vector", lambda e: e.memset(consts[:, 1:2], 1e-6), writes=[sl_consts])
        p.op("vector", lambda e: e.memset(consts[:, 2:3], -0.5), writes=[sl_consts])
        p.op("vector", lambda e: e.memset(consts[:, 3:4], -1.0), writes=[sl_consts])
        sl_consts.frozen = True
        sl_ones.frozen = True
        with nc.allow_non_contiguous_dma(reason="tiny vector loads"):
            for nm, tile, slot in [("v_bg", t_bg, sl_vec["bg"]), ("v_bc", t_bcn, sl_vec["bcn"]),
                                   ("v_n1", t_n1, sl_vec["n1"]), ("v_n2", t_n2, sl_vec["n2"]),
                                   ("v_s1", t_s1, sl_vec["s1"]), ("v_s2", t_s2, sl_vec["s2"]),
                                   ("v_t1", t_t1, sl_vec["t1"]), ("v_t2", t_t2, sl_vec["t2"]),
                                   ("v_s3", t_s3h, sl_vec["s3h"])]:
                p.dma("sync", tile[:, :], vin[nm].transpose([1, 0]), writes=[slot])
        p.op("vector", lambda e: e.tensor_scalar_mul(t_bcn[:, :], t_bcn[:, :], -1.0),
             reads=[sl_vec["bcn"]], writes=[sl_vec["bcn"]])
        p.op("vector", lambda e: e.tensor_scalar_mul(t_s3h[:, :], t_s3h[:, :], 0.5),
             reads=[sl_vec["s3h"]], writes=[sl_vec["s3h"]])
        for s in sl_vec.values():
            s.frozen = True

        def load_xb(src_ap, src_slots, t0, bi):
            dst = xb_t[bi][:, :].rearrange("p (k t) -> p k t", t=NT)
            p.dma("gpsimd", dst, x_kview(src_ap, t0, NT),
                  reads=src_slots, writes=[sl_xb[bi]])
            return sl_xb[bi], xb_t[bi]

        def stats(xbslot, xb_tile, kind, si, sq_src=None, f1_out=None,
                  sq_eng="scalar"):
            xb_ap = lambda k: xb_tile[:, k * NT:(k + 1) * NT]
            sqs, sq_ap = (xbslot, xb_ap) if sq_src is None else sq_src
            bank_m = None
            if kind == "ln":
                bank_m = p.next_bank()
                p.mm_group(bank_m, [(ones_bf[:, :], xb_ap(k), (xbslot, sl_ones))
                                    for k in range(KD)])
            bank_q = p.next_bank()
            for batch in range(KD // 2):
                mms = []
                for j in range(2):
                    k = batch * 2 + j
                    if sq_eng == "scalar":
                        p.op("scalar", lambda e, k=k, j=j:
                             e.activation(xsq_t[j][:, :], sq_ap(k), AF.Square),
                             reads=[sqs], writes=[sl_xsq[j]])
                    else:
                        p.op(sq_eng, lambda e, k=k, j=j:
                             e.tensor_mul(xsq_t[j][:, :], sq_ap(k), sq_ap(k)),
                             reads=[sqs], writes=[sl_xsq[j]])
                    mms.append((ones_bf[:, :], xsq_t[j][:, :], (sl_xsq[j], sl_ones)))
                p.mm_group(bank_q, mms, start=(batch == 0), stop=(batch == KD // 2 - 1))
            r_sl, r_ap = sl_rs[si], rs_t[si][:, :]
            if kind == "ln":
                mr_sl, mr_ap = sl_mr[si], mr_t[si][:, :]
                p.op("scalar", lambda e: e.activation(mr_ap, bank_m.ap, AF.Square),
                     reads=[bank_m], writes=[mr_sl])
                p.op("vector", lambda e: e.tensor_sub(mr_ap, bank_q.ap, mr_ap),
                     reads=[bank_q, mr_sl], writes=[mr_sl])
                p.op("scalar", lambda e: e.activation(r_ap, mr_ap, AF.Ln, bias=EPS_LN),
                     reads=[mr_sl, sl_consts], writes=[r_sl])
                p.op("scalar", lambda e: e.activation(r_ap, r_ap, AF.Exp, scale=NEG_HALF),
                     reads=[r_sl, sl_consts], writes=[r_sl])
                if f1_out is not None:
                    f1sl, f1ap = f1_out
                    p.op("vector", lambda e: e.tensor_mul(f1ap, mr_ap, r_ap),
                         reads=[mr_sl, r_sl], writes=[f1sl])
                    p.op("vector", lambda e: e.tensor_mul(f1ap, f1ap, r_ap),
                         reads=[f1sl, r_sl], writes=[f1sl])
                    p.op("scalar", lambda e: e.activation(f1ap, f1ap, AF.Ln, bias=EPS_RMS),
                         reads=[f1sl, sl_consts], writes=[f1sl])
                    p.op("scalar", lambda e: e.activation(f1ap, f1ap, AF.Exp, scale=NEG_HALF),
                         reads=[f1sl, sl_consts], writes=[f1sl])
                return r_sl, r_ap, mr_sl, bank_m
            else:
                p.op("scalar", lambda e: e.activation(r_ap, bank_q.ap, AF.Ln, bias=EPS_RMS),
                     reads=[bank_q, sl_consts], writes=[r_sl])
                p.op("scalar", lambda e: e.activation(r_ap, r_ap, AF.Exp, scale=NEG_HALF),
                     reads=[r_sl, sl_consts], writes=[r_sl])
                return r_sl, r_ap, None, None

        def ln_apply(in_f, r_sl, r_ap, bank_m, out_f, tfi=(0, 1, 2, 3),
                     mul_eng="vector"):
            nb = len(tfi)
            for h4 in range(KD // nb):
                ks = range(h4 * nb, h4 * nb + nb)
                for k in ks:
                    iap, irds = in_f(k)
                    ti = tfi[k % nb]
                    p.op("vector", lambda e, iap=iap, ti=ti:
                         e.tensor_sub(tf_t[ti][:, :], iap, bank_m.ap),
                         reads=list(irds) + [bank_m], writes=[sl_tf[ti]])
                for k in ks:
                    oap, osl = out_f(k)
                    ti = tfi[k % nb]
                    p.op(mul_eng, lambda e, oap=oap, ti=ti:
                         e.tensor_mul(oap, tf_t[ti][:, :], r_ap),
                         reads=[sl_tf[ti], r_sl], writes=[osl])

        def pool_stage(M, loadf, K=2):
            st = {"slots": {}, "primed": False}

            def load(m):
                pi = m % NP
                reg_pool[pi].turn()
                st["slots"][m] = loadf(m, pool_t[pi], reg_pool[pi])

            def prime():
                if not st["primed"]:
                    st["primed"] = True
                    for m in range(min(K, M)):
                        load(m)

            def run(usef, hook=None):
                prime()
                for m in range(M):
                    if m + K < M:
                        load(m + K)
                    usef(m, pool_t[m % NP], st["slots"].pop(m))
                    if hook:
                        hook(m)
            return prime, run

        def s1_loads(src_ap, src_slots2d, half, tt):
            t0 = half * HT + tt * NT
            ch = t0 // NT
            return load_xb(src_ap, [src_slots2d[r][ch] for r in range(KD)],
                           t0, tt % 2)

        def s1_work(sl_zf_col, zf_ap_half, xb, tt, sq_eng="scalar",
                    tfi=(0, 1, 2, 3), mul_eng="vector"):
            xbs, xbt = xb
            r_sl, r_ap, mr_sl, bank_m = stats(xbs, xbt, "ln", tt % 2, sq_eng=sq_eng)
            ln_apply(lambda k: (xbt[:, k * NT:(k + 1) * NT], (xbs,)),
                     r_sl, r_ap, bank_m,
                     lambda k: (zf_ap_half(k, tt), sl_zf_col[k]), tfi=tfi,
                     mul_eng=mul_eng)

        def emit_ffn(src_ap, src_slots2d, dst_ap, dst_slots2d, A_t, reg_A,
                     zf_t, reg_zf, hooks=None, pre_s1h0=None):
            hooks = hooks or {}
            zf_ap = lambda k, t: zf_t[:, k * HT + t * NT: k * HT + t * NT + NT]
            zf_state = {}

            def do_s1(half, xbs=None):
                reg_zf.turn()
                sl_zf = [[reg_zf.slot(f"zf{half}_{k}_{t}") for t in range(NH)]
                         for k in range(KD)]
                for tt in range(NH):
                    xb = xbs[tt] if xbs else s1_loads(src_ap, src_slots2d, half, tt)
                    s1_work([sl_zf[k][tt] for k in range(KD)], zf_ap, xb, tt)
                zf_state[half] = sl_zf

            def s2_stage(half):
                def loadf(m, pt, reg):
                    w1sl = reg.slot(f"w1_{half}_{m}")
                    w2sl = reg.slot(f"w2_{half}_{m}")
                    p.dma("gpsimd", pt[:, 0:KD * 128], wp["p_w1"][m], writes=[w1sl])
                    p.dma("gpsimd", pt[:, KD * 128:2 * KD * 128], wp["p_w2"][m],
                          writes=[w2sl])
                    return (w1sl, w2sl)
                return pool_stage(KF, loadf)

            def h_pool(wname, tag):
                def loadf(m, pt, reg):
                    wsl = reg.slot(f"{tag}w_{m}")
                    p.dma("gpsimd", pt[:, :], wp[wname][m], writes=[wsl])
                    return wsl
                return pool_stage(KF, loadf)

            def s5_pool(tag):
                def loadf(mo, pt, reg):
                    wsl = reg.slot(f"{tag}_{mo}")
                    p.dma("gpsimd", pt[:, :], wp["p_w3"][mo], writes=[wsl])
                    return wsl
                return pool_stage(KD, loadf)

            for half in range(2):
                th0 = half * HT
                u_i, h1_i = half % 2, (half + 1) % 2

                pr_s2, run_s2 = s2_stage(half)
                if half == 0:
                    if pre_s1h0 is not None:
                        pr_s2()
                        zf_state[0] = pre_s1h0
                    else:
                        xbs = [s1_loads(src_ap, src_slots2d, 0, tt)
                               for tt in range(NH)]
                        pr_s2()
                        do_s1(0, xbs=xbs)
                else:
                    pr_s2()
                sl_zf = zf_state[half]

                reg_A[u_i].turn()
                sl_u = [[reg_A[u_i].slot(f"u{half}_{m}_{t}") for t in range(NH)]
                        for m in range(KF)]
                u_ap = lambda m, t, u_i=u_i: A_t[u_i][:, m * HT + t * NT: m * HT + t * NT + NT]

                def s2_use(m, pt, sl):
                    w1sl, w2sl = sl
                    for t in range(NH):
                        b1 = p.next_bank()
                        b2 = p.next_bank()
                        p.mm_group(b1, [(pt[:, k * 128:(k + 1) * 128], zf_ap(k, t),
                                         (w1sl, sl_zf[k][t])) for k in range(KD)])
                        p.mm_group(b2, [(pt[:, KD * 128 + k * 128:KD * 128 + (k + 1) * 128],
                                         zf_ap(k, t), (w2sl, sl_zf[k][t])) for k in range(KD)])
                        ti = (m * NH + t) % 2
                        p.op("scalar", lambda e, b1=b1, m=m, ti=ti:
                             e.activation(tf_t[ti][:, :], b1.ap, AF.Silu, scale=t_s1[:, m:m + 1]),
                             reads=[b1, sl_vec["s1"]], writes=[sl_tf[ti]])
                        p.op("vector", lambda e, b2=b2, m=m, t=t, ti=ti:
                             e.scalar_tensor_tensor(u_ap(m, t), b2.ap, t_s2[:, m:m + 1],
                                                    tf_t[ti][:, :], ALU.mult, ALU.mult),
                             reads=[b2, sl_tf[ti], sl_vec["s2"]], writes=[sl_u[m][t]])
                run_s2(s2_use)

                def hstage(prun, tvec_t, tvec_sl, in_ap, in_sl, out_i, tag,
                           hook=None):
                    reg_A[out_i].turn()
                    o_sl = [[reg_A[out_i].slot(f"{tag}_{m}_{t}") for t in range(NH)]
                            for m in range(KF)]
                    o_ap = lambda m, t, out_i=out_i: A_t[out_i][:, m * HT + t * NT: m * HT + t * NT + NT]

                    def use(m, pt, wsl):
                        for t in range(NH):
                            b = p.next_bank()
                            p.mm_group(b, [(pt[:, k * 128:(k + 1) * 128],
                                            in_ap(k, t), (wsl, in_sl[k][t]))
                                           for k in range(KF)])
                            p.op("scalar", lambda e, b=b, m=m, t=t:
                                 e.activation(o_ap(m, t), b.ap, AF.Gelu,
                                              scale=tvec_t[:, m:m + 1]),
                                 reads=[b, tvec_sl], writes=[o_sl[m][t]])
                    prun(use, hook=hook)
                    return o_sl, o_ap

                s3hook = None
                if half == 0:
                    s3st = {}

                    def s3hook(m):
                        if m == 4:
                            s3st["xb"] = [s1_loads(src_ap, src_slots2d, 1, tt)
                                          for tt in range(NH)]
                            reg_zf.turn()
                            s3st["sl"] = [[reg_zf.slot(f"zf1_{k}_{t}")
                                           for t in range(NH)] for k in range(KD)]
                elif "s3h1mo" in hooks:
                    s3hook = hooks["s3h1mo"]

                pr_h1, run_h1 = h_pool("p_h1", f"h1_{half}")
                pr_h1()
                sl_h1, h1_ap = hstage(run_h1, t_t1, sl_vec["t1"], u_ap, sl_u,
                                      h1_i, f"h1_{half}", hook=s3hook)
                if half == 0:
                    for tt in range(NH):
                        s1_work([s3st["sl"][k][tt] for k in range(KD)],
                                zf_ap, s3st["xb"][tt], tt)
                    zf_state[1] = s3st["sl"]
                pr_h2, run_h2 = h_pool("p_h2", f"h2_{half}")
                pr_h2()
                sl_h2, h2_ap = hstage(run_h2, t_t2, sl_vec["t2"], h1_ap, sl_h1,
                                      u_i, f"h2_{half}")
                if half == 1 and "s4h1" in hooks:
                    hooks["s4h1"]()

                pr_s5, run_s5 = s5_pool(f"w3_{half}")
                pr_s5()
                ntile = KD * NH
                xt_pend = {}

                def xt_load(i):
                    mo_, t_ = divmod(i, NH)
                    ch = (th0 + t_ * NT) // NT
                    xi_ = i % 2
                    p.dma("sync", xt_t[xi_][:, :],
                          src_ap[mo_ * 128:(mo_ + 1) * 128, th0 + t_ * NT: th0 + (t_ + 1) * NT],
                          reads=[src_slots2d[mo_][ch]], writes=[sl_xt[xi_]])
                xt_load(0)

                def s5_use(mo, pt, wsl):
                    for t in range(NH):
                        i = mo * NH + t
                        ch = (th0 + t * NT) // NT
                        b = p.next_bank()
                        p.mm_group(b, [(pt[:, k * 128:(k + 1) * 128],
                                        h2_ap(k, t), (wsl, sl_h2[k][t]))
                                       for k in range(KF)])
                        if i + 1 < ntile:
                            xt_load(i + 1)
                        xi_ = i % 2
                        xs = sl_xt[xi_]
                        p.op("vector", lambda e, b=b, mo=mo, xi_=xi_:
                             e.scalar_tensor_tensor(xt_t[xi_][:, :], b.ap, t_s3h[:, mo:mo + 1],
                                                    xt_t[xi_][:, :], ALU.mult, ALU.add),
                             reads=[b, xs, sl_vec["s3h"]], writes=[xs])
                        p.dma("sync",
                              dst_ap[mo * 128:(mo + 1) * 128, th0 + t * NT: th0 + (t + 1) * NT],
                              xt_t[xi_][:, :], reads=[xs], writes=[dst_slots2d[mo][ch]])
                hook = hooks.get("s5h1mo") if half == 1 else None
                run_s5(s5_use, hook=hook)

        attn = {}

        def attn_carve_head():
            attn["wg_t"] = at("wg", OV + 0, [128, KD * 1024], BF16)
            attn["reg_wg"] = arena.region("wg", 0, KD * 2048)
            attn["sl_wg"] = [attn["reg_wg"].slot(f"wg{m}") for m in range(KD)]

        def attn_carve_head2():
            attn["wc_t"] = at("wc", OV + 16384, [128, KD * 1024], BF16)
            attn["reg_wc"] = arena.region("wc", 16384, KD * 2048)
            attn["sl_wc"] = [attn["reg_wc"].slot(f"wc{m}") for m in range(KD)]
            attn["xi_t"] = [at(f"xi{i}", OV + 32768 + i * 16384, [128, KD * NT], F32)
                            for i in range(2)]
            attn["reg_xi"] = [arena.region(f"xi{i}", 32768 + i * 16384, KD * NT * 4)
                              for i in range(2)]
            attn["z1_t"] = [at(f"z1{i}", OV + 65536 + i * 8192, [128, KD * NT], BF16)
                            for i in range(2)]
            attn["reg_z1"] = [arena.region(f"z1{i}", 65536 + i * 8192, KD * NT * 2)
                              for i in range(2)]

        def attn_carve_body():
            attn["g_t"] = at("g", OV + 81920, [128, 2 * NT], F32)
            rg = arena.region("g", 81920, 2 * NT * 4)
            attn["b_t"] = at("b", OV + 86016, [128, 2 * NT], F32)
            rb = arena.region("b", 86016, 2 * NT * 4)
            attn["h_t"] = at("h", OV + 90112, [128, 2 * NT], F32)
            rh = arena.region("h", 90112, 2 * NT * 4)
            attn["hl_t"] = at("hl", OV + 94208, [128, 16], F32)
            rhl = arena.region("hl", 94208, 64)
            attn["tb_t"] = at("tb", OV + 94720, [128, NT], F32)
            rtb = arena.region("tb", 94720, NT * 4)
            attn["sl_g"] = [rg.slot(f"gp{i}") for i in range(2)]
            attn["sl_b"] = [rb.slot(f"bp{i}") for i in range(2)]
            attn["sl_h"] = [rh.slot(f"hp{i}") for i in range(2)]
            attn["sl_hl"] = [rhl.slot(f"hl{i}") for i in range(2)]
            attn["sl_tb"] = rtb.slot("tb")
            attn["z2_t"] = at("z2", OV + 98304, [128, KD * T], BF16)
            attn["reg_z2"] = arena.region("z2", 98304, KD * T * 2)
            attn["sl_z2"] = [[attn["reg_z2"].slot(f"z2_{k}_{c}") for c in range(NCH)]
                             for k in range(KD)]

        def z2_ap(k, c):
            return attn["z2_t"][:, k * T + c * NT: k * T + c * NT + NT]

        AC = {}

        def attn_load_xb(c):
            bi = c % 2
            src_sl = [sl_xdram[r][c] for r in range(KD)]
            AC.setdefault(c, {})["xb"] = load_xb(xdram, src_sl, c * NT, bi)

        def attn_load_xi(c):
            bi = c % 2
            attn["reg_xi"][bi].turn()
            xisl = attn["reg_xi"][bi].slot(f"xi{c}")
            src_sl = [sl_xdram[r][c] for r in range(KD)]
            p.dma("gpsimd", attn["xi_t"][bi][:, :].rearrange("p (k t) -> p k t", t=NT),
                  x_kview(xdram, c * NT, NT), reads=src_sl, writes=[xisl])
            AC.setdefault(c, {})["xi"] = xisl

        def attn_loads(c):
            attn_load_xb(c)
            attn_load_xi(c)

        def attn_head(c):
            bi = c % 2
            xbs, xbt = AC[c]["xb"]
            xisl = AC[c]["xi"]
            xi_ap = lambda k, bi=bi: attn["xi_t"][bi][:, k * NT:(k + 1) * NT]
            r_sl, r_ap, mr_sl, bank_m = stats(
                xbs, xbt, "ln", bi, f1_out=(sl_mr[bi], mr_t[bi][:, :]),
                sq_eng="gpsimd")
            ln_apply(lambda k: (xi_ap(k), (xisl,)), r_sl, r_ap, bank_m,
                     lambda k: (xi_ap(k), xisl), mul_eng="gpsimd")
            attn["reg_z1"][bi].turn()
            z1sl = attn["reg_z1"][bi].slot(f"z1_{c}")
            z1_ap = lambda k, bi=bi: attn["z1_t"][bi][:, k * NT:(k + 1) * NT]
            for k in range(KD):
                p.op("vector", lambda e, k=k, bi=bi:
                     e.scalar_tensor_tensor(z1_ap(k), xi_ap(k), t_n1[:, k:k + 1],
                                            mr_t[bi][:, :], ALU.mult, ALU.mult),
                     reads=[xisl, sl_mr[bi], sl_vec["n1"]], writes=[z1sl])
            AC[c]["z1"] = z1sl

        def attn_gates_scan(c):
            bi = c % 2
            z1sl = AC[c]["z1"]
            xisl = AC[c]["xi"]
            xi_ap = lambda k, bi=bi: attn["xi_t"][bi][:, k * NT:(k + 1) * NT]
            z1_ap = lambda k, bi=bi: attn["z1_t"][bi][:, k * NT:(k + 1) * NT]
            wg_ap = lambda m, k: attn["wg_t"][:, m * 1024 + k * 128: m * 1024 + (k + 1) * 128]
            wc_ap = lambda m, k: attn["wc_t"][:, m * 1024 + k * 128: m * 1024 + (k + 1) * 128]
            g_t, b_t, h_t = attn["g_t"], attn["b_t"], attn["h_t"]
            hl_t, tb_t = attn["hl_t"], attn["tb_t"]
            sl_g, sl_b, sl_h = attn["sl_g"], attn["sl_b"], attn["sl_h"]
            sl_hl, sl_tb = attn["sl_hl"], attn["sl_tb"]
            for m in range(KD):
                pi = m % 2
                g_ap = g_t[:, pi * NT:(pi + 1) * NT]
                b_ap = b_t[:, pi * NT:(pi + 1) * NT]
                h_ap = h_t[:, pi * NT:(pi + 1) * NT]
                bg_ = p.next_bank()
                bc_ = p.next_bank()
                p.mm_group(bg_, [(wg_ap(m, k), z1_ap(k), (attn["sl_wg"][m], z1sl))
                                 for k in range(KD)])
                p.mm_group(bc_, [(wc_ap(m, k), z1_ap(k), (attn["sl_wc"][m], z1sl))
                                 for k in range(KD)])
                p.op("scalar", lambda e, bg_=bg_, m=m, g_ap=g_ap:
                     e.activation(g_ap, bg_.ap, AF.Sigmoid, bias=t_bg[:, m:m + 1]),
                     reads=[bg_, sl_vec["bg"]], writes=[sl_g[pi]])
                p.op("scalar", lambda e, bc_=bc_, m=m, b_ap=b_ap:
                     e.activation(b_ap, bc_.ap, AF.Tanh,
                                  scale=NEG_ONE, bias=t_bcn[:, m:m + 1]),
                     reads=[bc_, sl_vec["bcn"], sl_consts], writes=[sl_b[pi]])
                p.op("vector", lambda e, g_ap=g_ap, b_ap=b_ap:
                     e.scalar_tensor_tensor(b_ap, g_ap, 1.0, b_ap,
                                            ALU.subtract, ALU.mult),
                     reads=[sl_g[pi], sl_b[pi]], writes=[sl_b[pi]])
                if c == 0:
                    init = 0.0
                    rds = [sl_g[pi], sl_b[pi]]
                else:
                    init = hl_t[:, ((c - 1) % 2) * 8 + m: ((c - 1) % 2) * 8 + m + 1]
                    rds = [sl_g[pi], sl_b[pi], sl_hl[(c - 1) % 2]]
                p.op("vector", lambda e, init=init, h_ap=h_ap, g_ap=g_ap, b_ap=b_ap:
                     e.tensor_tensor_scan(h_ap, g_ap, b_ap, init,
                                          ALU.mult, ALU.add),
                     reads=rds, writes=[sl_h[pi]])
                p.op("gpsimd", lambda e, c=c, m=m, h_ap=h_ap:
                     e.tensor_copy(hl_t[:, (c % 2) * 8 + m: (c % 2) * 8 + m + 1],
                                   h_ap[:, NT - 1:NT]),
                     reads=[sl_h[pi]], writes=[sl_hl[c % 2]])
                p.op("gpsimd", lambda e, m=m, h_ap=h_ap:
                     e.tensor_add(xi_ap(m), xi_ap(m), h_ap),
                     reads=[sl_h[pi], xisl], writes=[xisl])

        def attn_deferred(c):
            bi = c % 2
            xisl = AC[c]["xi"]
            xi_ap = lambda k, bi=bi: attn["xi_t"][bi][:, k * NT:(k + 1) * NT]
            r2_sl, r2_ap, _, _ = stats(None, xb_t[bi], "rms", bi,
                                       sq_src=(xisl, xi_ap))
            for k in range(KD):
                p.op("vector", lambda e, k=k, c=c:
                     e.scalar_tensor_tensor(z2_ap(k, c), xi_ap(k), t_n2[:, k:k + 1],
                                            r2_ap, ALU.mult, ALU.mult),
                     reads=[xisl, r2_sl, sl_vec["n2"]], writes=[attn["sl_z2"][k][c]])
            p.dma("gpsimd", x_kview(xdram, c * NT, NT),
                  attn["xi_t"][bi][:, :].rearrange("p (k t) -> p k t", t=NT),
                  reads=[xisl] + [sl_xdram[r][c] for r in range(KD)],
                  writes=[sl_xdram[r][c] for r in range(KD)],
                  accum_op=ALU.add)

        def make_attn_B(w13h1_hook=None):
            def w13_pool(hb):
                def loadf(m, pt, reg):
                    w1sl = reg.slot(f"aw1_{hb}_{m}")
                    w3sl = reg.slot(f"aw3_{hb}_{m}")
                    p.dma("gpsimd", pt[:, 0:KD * 128], wp["p_aw1"][m], writes=[w1sl])
                    p.dma("gpsimd", pt[:, KD * 128:2 * KD * 128], wp["p_aw3"][m],
                          writes=[w3sl])
                    return (w1sl, w3sl)
                return pool_stage(KF, loadf)

            def w2_pool(hb):
                def loadf(mo, pt, reg):
                    wsl = reg.slot(f"aw2_{hb}_{mo}")
                    p.dma("gpsimd", pt[:, :], wp["p_aw2"][mo], writes=[wsl])
                    return wsl
                return pool_stage(KD, loadf)

            st_w13 = [w13_pool(0), w13_pool(1)]
            st_w2 = [w2_pool(0), w2_pool(1)]

            def prime_h0():
                st_w13[0][0]()

            def run():
                uh_t = at("uh", OV + 0, [128, KF * HT], BF16)
                reg_uh = arena.region("uh", 0, KF * HT * 2)
                for hb in range(2):
                    th0 = hb * HT
                    reg_uh.turn()
                    sl_uh = [[reg_uh.slot(f"uh{hb}_{m}_{t}") for t in range(NH)]
                             for m in range(KF)]
                    uh_ap = lambda m, t: uh_t[:, m * HT + t * NT: m * HT + t * NT + NT]

                    def w13_use(m, pt, sl):
                        w1sl, w3sl = sl
                        for t in range(NH):
                            c = hb * NH + t
                            b1 = p.next_bank()
                            b3 = p.next_bank()
                            p.mm_group(b1, [(pt[:, k * 128:(k + 1) * 128], z2_ap(k, c),
                                             (w1sl, attn["sl_z2"][k][c])) for k in range(KD)])
                            p.mm_group(b3, [(pt[:, KD * 128 + k * 128:KD * 128 + (k + 1) * 128],
                                             z2_ap(k, c), (w3sl, attn["sl_z2"][k][c]))
                                            for k in range(KD)])
                            ti = (m * NH + t) % 2
                            p.op("scalar", lambda e, b1=b1, ti=ti:
                                 e.activation(tf_t[ti][:, :], b1.ap, AF.Silu),
                                 reads=[b1], writes=[sl_tf[ti]])
                            p.op("vector", lambda e, b3=b3, m=m, t=t, ti=ti:
                                 e.tensor_mul(uh_ap(m, t), tf_t[ti][:, :], b3.ap),
                                 reads=[b3, sl_tf[ti]], writes=[sl_uh[m][t]])
                    hook = w13h1_hook if hb == 1 else None
                    st_w13[hb][1](w13_use, hook=hook)

                    ntile = KD * NH
                    st_w2[hb][0]()

                    def xt_load(i):
                        mo_, t_ = divmod(i, NH)
                        c = hb * NH + t_
                        xi_ = i % 2
                        p.dma("sync", xt_t[xi_][:, :],
                              xdram[mo_ * 128:(mo_ + 1) * 128, th0 + t_ * NT: th0 + (t_ + 1) * NT],
                              reads=[sl_xdram[mo_][c]], writes=[sl_xt[xi_]])
                    xt_load(0)

                    def w2_use(mo, pt, wsl):
                        for t in range(NH):
                            c = hb * NH + t
                            i = mo * NH + t
                            b = p.next_bank()
                            p.mm_group(b, [(pt[:, k * 128:(k + 1) * 128],
                                            uh_ap(k, t), (wsl, sl_uh[k][t]))
                                           for k in range(KF)])
                            if i + 1 < ntile:
                                xt_load(i + 1)
                            xi_ = i % 2
                            xs = sl_xt[xi_]
                            p.op("vector", lambda e, b=b, xi_=xi_:
                                 e.tensor_add(xt_t[xi_][:, :], xt_t[xi_][:, :], b.ap),
                                 reads=[b, xs], writes=[xs])
                            p.dma("sync",
                                  xdram[mo * 128:(mo + 1) * 128, th0 + t * NT: th0 + (t + 1) * NT],
                                  xt_t[xi_][:, :], reads=[xs], writes=[sl_xdram[mo][c]])
                    st_w2[hb][1](w2_use)
                    if hb == 0:
                        st_w13[1][0]()
            return prime_h0, run

        xin_slots = [[Slot(f"xin{r}_{t}", frozen=True) for t in range(NCH)]
                     for r in range(KD)]

        zf_t = at("zf", OV + 0, [128, KD * HT], BF16)
        reg_zf = arena.region("zf", 0, KD * HT * 2)
        A_t = [at("A0", OV + 16384, [128, KF * HT], BF16),
               at("A1", OV + 81920, [128, KF * HT], BF16)]
        reg_A = [arena.region("A0", 16384, KF * HT * 2),
                 arena.region("A1", 81920, KF * HT * 2)]

        def hook_s3h1mo(m):
            if m == 2:
                attn_carve_head()
            if m >= 4 and m % 4 == 0 and m <= 28:
                for mw in ([m // 4 - 1] if m < 28 else [6, 7]):
                    p.dma("gpsimd", attn["wg_t"][:, mw * 1024:(mw + 1) * 1024],
                          wp["p_wg"][mw], writes=[attn["sl_wg"][mw]])

        def hook_s4h1():
            attn_carve_head2()

        def hook_s5h1mo(mo):
            if mo == 0:
                attn_loads(0)
            elif mo == 1:
                attn_loads(1)
            elif mo == 2:
                for m in range(4):
                    p.dma("gpsimd", attn["wc_t"][:, m * 1024:(m + 1) * 1024],
                          wp["p_wc"][m], writes=[attn["sl_wc"][m]])
            elif mo == 3:
                for m in range(4, KD):
                    p.dma("gpsimd", attn["wc_t"][:, m * 1024:(m + 1) * 1024],
                          wp["p_wc"][m], writes=[attn["sl_wc"][m]])
            elif mo == 4:
                attn_head(0)
            elif mo == 6:
                attn_head(1)

        emit_ffn(xin, xin_slots, xdram, sl_xdram, A_t, reg_A, zf_t, reg_zf,
                 hooks={"s3h1mo": hook_s3h1mo, "s4h1": hook_s4h1,
                        "s5h1mo": hook_s5h1mo})

        zfb_t = None

        def mk_ffn2_s1_hooks():
            state = {}

            def hook(m):
                nonlocal zfb_t
                if m == 2:
                    zfb_t = at("zfb", OV + 65536, [128, KD * HT], BF16)
                    state["reg"] = arena.region("zfb", 65536, KD * HT * 2)
                    state["sl"] = [[state["reg"].slot(f"zfb0_{k}_{t}")
                                    for t in range(NH)] for k in range(KD)]
                    state["ap"] = lambda k, t: zfb_t[:, k * HT + t * NT: k * HT + t * NT + NT]
                    state["xb0"] = load_xb(xdram, [sl_xdram[r][0] for r in range(KD)],
                                           0, 0)
                elif m == 8:
                    xbs, xbt = state["xb0"]
                    r_sl, r_ap, mr_sl, bank_m = stats(xbs, xbt, "ln", 0,
                                                      sq_eng="scalar")
                    ln_apply(lambda k: (xbt[:, k * NT:(k + 1) * NT], (xbs,)),
                             r_sl, r_ap, bank_m,
                             lambda k: (state["ap"](k, 0), state["sl"][k][0]),
                             tfi=(2, 3))
                elif m == 16:
                    state["xb1"] = load_xb(xdram, [sl_xdram[r][1] for r in range(KD)],
                                           NT, 1)
                elif m == 22:
                    xbs, xbt = state["xb1"]
                    r_sl, r_ap, mr_sl, bank_m = stats(xbs, xbt, "ln", 1,
                                                      sq_eng="scalar")
                    ln_apply(lambda k: (xbt[:, k * NT:(k + 1) * NT], (xbs,)),
                             r_sl, r_ap, bank_m,
                             lambda k: (state["ap"](k, 1), state["sl"][k][1]),
                             tfi=(2, 3))
            return state, hook

        attn_carve_body()
        s1_state, s1_hook = mk_ffn2_s1_hooks()
        primeB, runB = make_attn_B(w13h1_hook=s1_hook)
        primeB()

        for c in range(NCH):
            if 1 <= c and c + 1 < NCH:
                attn_load_xb(c + 1)
            attn_gates_scan(c)
            if c >= 1:
                attn_deferred(c - 1)
                if c + 1 < NCH:
                    attn_load_xi(c + 1)
            if 1 <= c and c + 1 < NCH:
                attn_head(c + 1)
        attn_deferred(NCH - 1)

        runB()

        A2_t = [at("A0b", OV + 0, [128, KF * HT], BF16),
                at("A1b", OV + 81920, [128, KF * HT], BF16)]
        reg_A2 = [arena.region("A0b", 0, KF * HT * 2),
                  arena.region("A1b", 81920, KF * HT * 2)]
        reg_zf2 = s1_state["reg"]
        emit_ffn(xdram, sl_xdram, y_d, sl_y, A2_t, reg_A2, zfb_t, reg_zf2,
                 pre_s1h0=[[s1_state["sl"][k][t] for t in range(NH)]
                           for k in range(KD)])

        for i in range(NQSEM):
            if p.cnt[f"sy{i}"]:
                p.eng("sync").wait_ge(p.sem[f"sy{i}"], p.cnt[f"sy{i}"])
            if p.cnt[f"wg{i}"]:
                p.eng("gpsimd").wait_ge(p.sem[f"wg{i}"], p.cnt[f"wg{i}"])

    return nc



def _swizzle_panels(w_io):
    I, O = w_io.shape
    return np.ascontiguousarray(
        w_io.reshape(I // 128, 128, O // 128, 128).transpose(2, 1, 0, 3)
        .reshape(O // 128, 128, I))


def make_core_inputs(xb_core, W):
    d = {"xin": np.ascontiguousarray(xb_core.T)}
    d["p_w1"] = W["p_w1"]; d["p_w2"] = W["p_w2"]; d["p_h1"] = W["p_h1"]
    d["p_h2"] = W["p_h2"]; d["p_w3"] = W["p_w3"]; d["p_wg"] = W["p_wg"]
    d["p_wc"] = W["p_wc"]; d["p_aw1"] = W["p_aw1"]; d["p_aw3"] = W["p_aw3"]
    d["p_aw2"] = W["p_aw2"]
    for k in ("v_bg", "v_bc", "v_n1", "v_n2", "v_s1", "v_s2", "v_t1", "v_t2", "v_s3"):
        d[k] = W[k]
    return d


def prep_weights(L0_attn_Wg, L0_attn_bg, L0_attn_Wc, L0_attn_bc, L0_attn_n1_w,
                 L0_attn_n2_w, L0_attn_W1, L0_attn_W3, L0_attn_W2,
                 L0_ffn_w1_w, L0_ffn_w1_s, L0_ffn_w2_w, L0_ffn_w2_s,
                 L0_ffn_h1_w, L0_ffn_h1_s, L0_ffn_h2_w, L0_ffn_h2_s,
                 L0_ffn_w3_w, L0_ffn_w3_s):
    f32 = lambda a: np.ascontiguousarray(np.asarray(a), dtype=np.float32)
    W = {}
    W["p_w1"] = _swizzle_panels(f32(L0_ffn_w1_w).T)
    W["p_w2"] = _swizzle_panels(f32(L0_ffn_w2_w).T)
    W["p_h1"] = _swizzle_panels(f32(L0_ffn_h1_w).T)
    W["p_h2"] = _swizzle_panels(f32(L0_ffn_h2_w).T)
    W["p_w3"] = _swizzle_panels(f32(L0_ffn_w3_w).T)
    W["p_wg"] = _swizzle_panels(f32(L0_attn_Wg))
    W["p_wc"] = _swizzle_panels(f32(L0_attn_Wc))
    W["p_aw1"] = _swizzle_panels(f32(L0_attn_W1))
    W["p_aw3"] = _swizzle_panels(f32(L0_attn_W3))
    W["p_aw2"] = _swizzle_panels(f32(L0_attn_W2))
    W["v_bg"] = f32(L0_attn_bg).reshape(KD, 128)
    W["v_bc"] = f32(L0_attn_bc).reshape(KD, 128)
    W["v_n1"] = f32(L0_attn_n1_w).reshape(KD, 128)
    W["v_n2"] = f32(L0_attn_n2_w).reshape(KD, 128)
    W["v_s1"] = f32(L0_ffn_w1_s).reshape(KF, 128)
    W["v_s2"] = f32(L0_ffn_w2_s).reshape(KF, 128)
    W["v_t1"] = f32(L0_ffn_h1_s).reshape(KF, 128)
    W["v_t2"] = f32(L0_ffn_h2_s).reshape(KF, 128)
    W["v_s3"] = f32(L0_ffn_w3_s).reshape(KD, 128)
    return W


def kernel(L0_attn_x, L0_attn_Wg, L0_attn_bg, L0_attn_Wc, L0_attn_bc,
           L0_attn_n1_w, L0_attn_n2_w, L0_attn_W1, L0_attn_W3, L0_attn_W2,
           L0_ffn_w1_w, L0_ffn_w1_s, L0_ffn_w2_w, L0_ffn_w2_s,
           L0_ffn_h1_w, L0_ffn_h1_s, L0_ffn_h2_w, L0_ffn_h2_s,
           L0_ffn_w3_w, L0_ffn_w3_s):
    x = np.ascontiguousarray(np.asarray(L0_attn_x), dtype=np.float32)
    B, T, _ = x.shape
    W = prep_weights(L0_attn_Wg, L0_attn_bg, L0_attn_Wc, L0_attn_bc, L0_attn_n1_w,
                     L0_attn_n2_w, L0_attn_W1, L0_attn_W3, L0_attn_W2,
                     L0_ffn_w1_w, L0_ffn_w1_s, L0_ffn_w2_w, L0_ffn_w2_s,
                     L0_ffn_h1_w, L0_ffn_h1_s, L0_ffn_h2_w, L0_ffn_h2_s,
                     L0_ffn_w3_w, L0_ffn_w3_s)
    nc = build_program(T=T)
    in_maps = [make_core_inputs(x[i], W) for i in range(B)]
    res = run_bass_kernel_spmd(nc, in_maps, core_ids=list(range(B)))
    out = np.stack([res.results[i]["y"] for i in range(B)], axis=0)
    return np.ascontiguousarray(out.transpose(0, 2, 1))


# version 12
# speedup vs baseline: 1.0344x; 1.0003x over previous
import numpy as np
import ml_dtypes
from contextlib import ExitStack

import concourse.bass as bass
import concourse.mybir as mybir
from concourse.bass_utils import run_bass_kernel_spmd

F32 = mybir.dt.float32
BF16 = mybir.dt.bfloat16
AF = mybir.ActivationFunctionType
ALU = mybir.AluOpType

D = 1024
DFF = 4096
KD = D // 128
KF = DFF // 128
NT = 512
NP = 3

ENG_SEM = {"tensor": "pe", "scalar": "sc", "vector": "ve", "gpsimd": "gp"}
NQSEM = 8


class Slot:
    __slots__ = ("ready", "weng", "wseq", "users", "frozen", "name")

    def __init__(self, name="", users=None, frozen=False):
        self.ready = None
        self.weng = None
        self.wseq = 0
        self.users = list(users) if users else []
        self.frozen = frozen
        self.name = name


def marks_max(slots):
    mx = {}
    for s in slots:
        for m in ([s.ready] if s.ready else []) + s.users:
            if m[1] > mx.get(m[0], 0):
                mx[m[0]] = m[1]
    return [(k, v) for k, v in mx.items()]


class Region:

    def __init__(self, name, marks0=()):
        self.name = name
        self.marks = list(marks0)
        self.cur = []

    def slot(self, name):
        s = Slot(name, users=list(self.marks))
        self.cur.append(s)
        return s

    def turn(self):
        if self.cur:
            self.marks = marks_max(self.cur)
            self.cur = []

    def snapshot(self):
        mx = dict(self.marks)
        for sem, v in marks_max(self.cur):
            if v > mx.get(sem, 0):
                mx[sem] = v
        return mx


class Arena:

    def __init__(self):
        self.carves = []

    def region(self, name, start, nbytes):
        mx = {}
        for s, e, r in self.carves:
            if s < start + nbytes and start < e:
                for sem, v in r.snapshot().items():
                    if v > mx.get(sem, 0):
                        mx[sem] = v
        reg = Region(name, list(mx.items()))
        self.carves.append((start, start + nbytes, reg))
        return reg


class P:
    def __init__(self, nc, ctx):
        self.nc = nc
        names = ["pe", "sc", "ve", "gp"] + [f"sy{i}" for i in range(NQSEM)] + \
                [f"wg{i}" for i in range(NQSEM)]
        self.sem = {s: ctx.enter_context(nc.semaphore("s_" + s)) for s in names}
        self.cnt = {s: 0 for s in names}
        ENGS = ("tensor", "scalar", "vector", "sync", "gpsimd")
        self.waited = {e: {s: 0 for s in names} for e in ENGS}
        self.nseq = {e: 0 for e in ENGS}
        self.sealed = {e: 0 for e in ENGS}
        self.qi = {"sync": 0, "gpsimd": 0}
        self.bank_i = 0
        self.banks = None

    def eng(self, name):
        return getattr(self.nc, name)

    def wait(self, ename, dep):
        if dep is None:
            return
        sem, val = dep
        if val <= 0 or self.waited[ename][sem] >= val or ENG_SEM.get(ename) == sem:
            return
        self.eng(ename).wait_ge(self.sem[sem], val)
        self.waited[ename][sem] = val

    def _pre(self, ename, reads, writes):
        own = ENG_SEM.get(ename)
        need_drain = False
        for s in reads:
            if s.ready is None:
                assert s.frozen, f"read of unwritten slot {s.name}"
                continue
            if own is not None and s.ready[0] == own:
                if s.ready[1] > self.sealed[ename]:
                    need_drain = True
            else:
                self.wait(ename, s.ready)
        for s in writes:
            assert not s.frozen, f"write to frozen slot {s.name}"
            for u in s.users:
                if own is not None and u[0] == own:
                    if u[1] > self.sealed[ename]:
                        need_drain = True
                else:
                    self.wait(ename, u)
            if s.ready is not None:
                if not (s.weng == ename and own is not None and s.ready[0] == own):
                    self.wait(ename, s.ready)
        if need_drain:
            self.eng(ename).drain()
            self.sealed[ename] = self.cnt[own]

    def _post(self, ename, ins, sem, n, reads, writes):
        self.nseq[ename] += 1
        ins.then_inc(self.sem[sem], n)
        self.cnt[sem] += n
        mark = (sem, self.cnt[sem])
        for s in reads:
            if not s.frozen:
                s.users.append(mark)
        for s in writes:
            s.ready = mark
            s.weng = ename
            s.wseq = self.nseq[ename]
            s.users = []
        return mark

    def op(self, ename, emit, reads=(), writes=()):
        self._pre(ename, reads, writes)
        ins = emit(self.eng(ename))
        return self._post(ename, ins, ENG_SEM[ename], 1, reads, writes)

    def dma(self, q, out_ap, in_ap, reads=(), writes=(), **kw):
        pfx = "sy" if q == "sync" else "wg"
        sem = f"{pfx}{self.qi[q] % NQSEM}"
        self.qi[q] += 1
        self.wait(q, (sem, self.cnt[sem]))
        self._pre(q, reads, writes)
        ins = self.eng(q).dma_start(out_ap, in_ap, **kw)
        return self._post(q, ins, sem, 16, reads, writes)

    def mm_group(self, bank, mms, start=True, stop=True):
        ename = "tensor"
        if start:
            for u in bank.users:
                self.wait(ename, u)
        n = len(mms)
        for i, (l, r, rds) in enumerate(mms):
            for s in rds:
                if s.ready is not None:
                    self.wait(ename, s.ready)
            ins = self.nc.tensor.matmul(bank.ap, l, r, start=(start and i == 0),
                                        stop=(stop and i == n - 1))
            self.nseq[ename] += 1
            if i == n - 1:
                ins.then_inc(self.sem["pe"], 1)
                self.cnt["pe"] += 1
        mark = ("pe", self.cnt["pe"])
        if stop:
            bank.ready = mark
            bank.weng = ename
            bank.users = []
        for _, _, rds in mms:
            for s in rds:
                if not s.frozen:
                    s.users.append(mark)
        return mark

    def next_bank(self):
        b = self.banks[self.bank_i % 8]
        self.bank_i += 1
        return b


def build_program(T=2048):
    HT = T // 2
    NH = HT // NT
    NCH = T // NT
    nc = bass.Bass("TRN2", target_bir_lowering=False)

    xin = nc.dram_tensor("xin", [D, T], F32, kind="ExternalInput").ap()
    y_d = nc.dram_tensor("y", [D, T], F32, kind="ExternalOutput").ap()
    xdram = nc.dram_tensor("xdram", [D, T], F32).ap()

    wp = {}
    for name, mch, kch in [("p_w1", KF, KD), ("p_w2", KF, KD), ("p_h1", KF, KF),
                           ("p_h2", KF, KF), ("p_w3", KD, KF), ("p_wg", KD, KD),
                           ("p_wc", KD, KD), ("p_aw1", KF, KD), ("p_aw3", KF, KD),
                           ("p_aw2", KD, KF)]:
        wp[name] = nc.dram_tensor(name, [mch, 128, kch * 128], F32,
                                  kind="ExternalInput").ap()
    vin = {}
    for name, chk in [("v_bg", KD), ("v_bc", KD), ("v_n1", KD), ("v_n2", KD),
                      ("v_s1", KF), ("v_s2", KF), ("v_t1", KF), ("v_t2", KF),
                      ("v_s3", KD)]:
        vin[name] = nc.dram_tensor(name, [chk, 128], F32, kind="ExternalInput").ap()

    def x_kview(xap, t0, tl):
        return xap.rearrange("(k p) t -> p k t", p=128)[:, :, t0:t0 + tl]

    with ExitStack() as ctx:
        p = P(nc, ctx)
        BASE = (nc.sbuf_base + 31) & ~31
        SLAB = 211968
        ctx.enter_context(nc.sbuf_tensor("slab", [128, SLAB], mybir.dt.int8))
        at = lambda name, off, shape, dt: nc.alloc_sbuf_tensor_at(
            name, shape, dt, offset=BASE + off)

        ones_bf = at("ones_bf", 0, [128, 128], BF16)
        consts = at("consts", 256, [128, 4], F32)
        t_bg = at("t_bg", 288, [128, KD], F32)
        t_bcn = at("t_bcn", 320, [128, KD], F32)
        t_n1 = at("t_n1", 352, [128, KD], F32)
        t_n2 = at("t_n2", 384, [128, KD], F32)
        t_s3h = at("t_s3h", 416, [128, KD], F32)
        t_s1 = at("t_s1", 448, [128, KF], F32)
        t_s2 = at("t_s2", 576, [128, KF], F32)
        t_t1 = at("t_t1", 704, [128, KF], F32)
        t_t2 = at("t_t2", 832, [128, KF], F32)
        xsq_t = [at(f"xsq{i}", 1024 + i * 1024, [128, NT], BF16) for i in range(2)]
        tf_t = [at(f"tf{i}", 3072 + i * 2048, [128, NT], F32) for i in range(4)]
        rs_t = [at(f"rs{i}", 11264 + i * 2048, [128, NT], F32) for i in range(2)]
        mr_t = [at(f"mr{i}", 15360 + i * 2048, [128, NT], F32) for i in range(2)]
        xt_t = [at(f"xt{i}", 19456 + i * 2048, [128, NT], F32) for i in range(2)]
        xb_t = [at(f"xb{i}", 23552 + i * 8192, [128, KD * NT], BF16) for i in range(2)]
        pool_t = [at(f"pool{i}", 39936 + i * 8192, [128, KF * 128], BF16)
                  for i in range(NP)]
        OV = 64512

        pb_t = [ctx.enter_context(nc.psum_tensor(f"pb{i}", [128, NT], F32))
                for i in range(8)]

        EPS_LN = consts[:, 0:1]
        EPS_RMS = consts[:, 1:2]
        NEG_HALF = consts[:, 2:3]
        NEG_ONE = consts[:, 3:4]

        class BankSlot(Slot):
            __slots__ = ("ap",)
        banks = []
        for i in range(8):
            s = BankSlot(f"pb{i}")
            s.ap = pb_t[i][:, :]
            banks.append(s)
        p.banks = banks

        arena = Arena()

        sl_ones = Slot("ones")
        sl_consts = Slot("consts")
        sl_vec = {n: Slot(n) for n in
                  ("bg", "bcn", "n1", "n2", "s1", "s2", "t1", "t2", "s3h")}
        sl_xb = [Slot(f"xb{i}") for i in range(2)]
        sl_xsq = [Slot(f"xsq{i}") for i in range(2)]
        sl_tf = [Slot(f"tf{i}") for i in range(4)]
        sl_rs = [Slot(f"rs{i}") for i in range(2)]
        sl_mr = [Slot(f"mr{i}") for i in range(2)]
        sl_xt = [Slot(f"xt{i}") for i in range(2)]
        sl_xdram = [[Slot(f"xd{r}_{t}") for t in range(NCH)] for r in range(KD)]
        sl_y = [[Slot(f"yy{r}_{t}") for t in range(NCH)] for r in range(KD)]
        reg_pool = [Region(f"pool{i}") for i in range(NP)]

        block = ctx.enter_context(nc.Block())

        p.op("vector", lambda e: e.memset(ones_bf[:, :], 1.0 / D), writes=[sl_ones])
        p.op("vector", lambda e: e.memset(consts[:, 0:1], 1e-5), writes=[sl_consts])
        p.op("vector", lambda e: e.memset(consts[:, 1:2], 1e-6), writes=[sl_consts])
        p.op("vector", lambda e: e.memset(consts[:, 2:3], -0.5), writes=[sl_consts])
        p.op("vector", lambda e: e.memset(consts[:, 3:4], -1.0), writes=[sl_consts])
        sl_consts.frozen = True
        sl_ones.frozen = True
        with nc.allow_non_contiguous_dma(reason="tiny vector loads"):
            for nm, tile, slot in [("v_bg", t_bg, sl_vec["bg"]), ("v_bc", t_bcn, sl_vec["bcn"]),
                                   ("v_n1", t_n1, sl_vec["n1"]), ("v_n2", t_n2, sl_vec["n2"]),
                                   ("v_s1", t_s1, sl_vec["s1"]), ("v_s2", t_s2, sl_vec["s2"]),
                                   ("v_t1", t_t1, sl_vec["t1"]), ("v_t2", t_t2, sl_vec["t2"]),
                                   ("v_s3", t_s3h, sl_vec["s3h"])]:
                p.dma("sync", tile[:, :], vin[nm].transpose([1, 0]), writes=[slot])
        p.op("vector", lambda e: e.tensor_scalar_mul(t_bcn[:, :], t_bcn[:, :], -1.0),
             reads=[sl_vec["bcn"]], writes=[sl_vec["bcn"]])
        p.op("vector", lambda e: e.tensor_scalar_mul(t_s3h[:, :], t_s3h[:, :], 0.5),
             reads=[sl_vec["s3h"]], writes=[sl_vec["s3h"]])
        for s in sl_vec.values():
            s.frozen = True

        def load_xb(src_ap, src_slots, t0, bi):
            dst = xb_t[bi][:, :].rearrange("p (k t) -> p k t", t=NT)
            p.dma("gpsimd", dst, x_kview(src_ap, t0, NT),
                  reads=src_slots, writes=[sl_xb[bi]])
            return sl_xb[bi], xb_t[bi]

        def stats(xbslot, xb_tile, kind, si, sq_src=None, f1_out=None,
                  sq_eng="scalar"):
            xb_ap = lambda k: xb_tile[:, k * NT:(k + 1) * NT]
            sqs, sq_ap = (xbslot, xb_ap) if sq_src is None else sq_src
            bank_m = None
            if kind == "ln":
                bank_m = p.next_bank()
                p.mm_group(bank_m, [(ones_bf[:, :], xb_ap(k), (xbslot, sl_ones))
                                    for k in range(KD)])
            bank_q = p.next_bank()
            for batch in range(KD // 2):
                mms = []
                for j in range(2):
                    k = batch * 2 + j
                    if sq_eng == "scalar":
                        p.op("scalar", lambda e, k=k, j=j:
                             e.activation(xsq_t[j][:, :], sq_ap(k), AF.Square),
                             reads=[sqs], writes=[sl_xsq[j]])
                    else:
                        p.op(sq_eng, lambda e, k=k, j=j:
                             e.tensor_mul(xsq_t[j][:, :], sq_ap(k), sq_ap(k)),
                             reads=[sqs], writes=[sl_xsq[j]])
                    mms.append((ones_bf[:, :], xsq_t[j][:, :], (sl_xsq[j], sl_ones)))
                p.mm_group(bank_q, mms, start=(batch == 0), stop=(batch == KD // 2 - 1))
            r_sl, r_ap = sl_rs[si], rs_t[si][:, :]
            if kind == "ln":
                mr_sl, mr_ap = sl_mr[si], mr_t[si][:, :]
                p.op("scalar", lambda e: e.activation(mr_ap, bank_m.ap, AF.Square),
                     reads=[bank_m], writes=[mr_sl])
                p.op("vector", lambda e: e.tensor_sub(mr_ap, bank_q.ap, mr_ap),
                     reads=[bank_q, mr_sl], writes=[mr_sl])
                p.op("scalar", lambda e: e.activation(r_ap, mr_ap, AF.Ln, bias=EPS_LN),
                     reads=[mr_sl, sl_consts], writes=[r_sl])
                p.op("scalar", lambda e: e.activation(r_ap, r_ap, AF.Exp, scale=NEG_HALF),
                     reads=[r_sl, sl_consts], writes=[r_sl])
                if f1_out is not None:
                    f1sl, f1ap = f1_out
                    p.op("vector", lambda e: e.tensor_mul(f1ap, mr_ap, r_ap),
                         reads=[mr_sl, r_sl], writes=[f1sl])
                    p.op("vector", lambda e: e.tensor_mul(f1ap, f1ap, r_ap),
                         reads=[f1sl, r_sl], writes=[f1sl])
                    p.op("scalar", lambda e: e.activation(f1ap, f1ap, AF.Ln, bias=EPS_RMS),
                         reads=[f1sl, sl_consts], writes=[f1sl])
                    p.op("scalar", lambda e: e.activation(f1ap, f1ap, AF.Exp, scale=NEG_HALF),
                         reads=[f1sl, sl_consts], writes=[f1sl])
                return r_sl, r_ap, mr_sl, bank_m
            else:
                p.op("scalar", lambda e: e.activation(r_ap, bank_q.ap, AF.Ln, bias=EPS_RMS),
                     reads=[bank_q, sl_consts], writes=[r_sl])
                p.op("scalar", lambda e: e.activation(r_ap, r_ap, AF.Exp, scale=NEG_HALF),
                     reads=[r_sl, sl_consts], writes=[r_sl])
                return r_sl, r_ap, None, None

        def ln_apply(in_f, r_sl, r_ap, bank_m, out_f, tfi=(0, 1, 2, 3),
                     mul_eng="vector"):
            nb = len(tfi)
            for h4 in range(KD // nb):
                ks = range(h4 * nb, h4 * nb + nb)
                for k in ks:
                    iap, irds = in_f(k)
                    ti = tfi[k % nb]
                    p.op("vector", lambda e, iap=iap, ti=ti:
                         e.tensor_sub(tf_t[ti][:, :], iap, bank_m.ap),
                         reads=list(irds) + [bank_m], writes=[sl_tf[ti]])
                for k in ks:
                    oap, osl = out_f(k)
                    ti = tfi[k % nb]
                    p.op(mul_eng, lambda e, oap=oap, ti=ti:
                         e.tensor_mul(oap, tf_t[ti][:, :], r_ap),
                         reads=[sl_tf[ti], r_sl], writes=[osl])

        def pool_stage(M, loadf, K=2):
            st = {"slots": {}, "primed": False}

            def load(m):
                pi = m % NP
                reg_pool[pi].turn()
                st["slots"][m] = loadf(m, pool_t[pi], reg_pool[pi])

            def prime():
                if not st["primed"]:
                    st["primed"] = True
                    for m in range(min(K, M)):
                        load(m)

            def run(usef, hook=None):
                prime()
                for m in range(M):
                    if m + K < M:
                        load(m + K)
                    usef(m, pool_t[m % NP], st["slots"].pop(m))
                    if hook:
                        hook(m)
            return prime, run

        def s1_loads(src_ap, src_slots2d, half, tt):
            t0 = half * HT + tt * NT
            ch = t0 // NT
            return load_xb(src_ap, [src_slots2d[r][ch] for r in range(KD)],
                           t0, tt % 2)

        def s1_work(sl_zf_col, zf_ap_half, xb, tt, sq_eng="scalar",
                    tfi=(0, 1, 2, 3), mul_eng="vector"):
            xbs, xbt = xb
            r_sl, r_ap, mr_sl, bank_m = stats(xbs, xbt, "ln", tt % 2, sq_eng=sq_eng)
            ln_apply(lambda k: (xbt[:, k * NT:(k + 1) * NT], (xbs,)),
                     r_sl, r_ap, bank_m,
                     lambda k: (zf_ap_half(k, tt), sl_zf_col[k]), tfi=tfi,
                     mul_eng=mul_eng)

        def emit_ffn(src_ap, src_slots2d, dst_ap, dst_slots2d, A_t, reg_A,
                     zf_t, reg_zf, hooks=None, pre_s1h0=None):
            hooks = hooks or {}
            zf_ap = lambda k, t: zf_t[:, k * HT + t * NT: k * HT + t * NT + NT]
            zf_state = {}

            def do_s1(half, xbs=None):
                reg_zf.turn()
                sl_zf = [[reg_zf.slot(f"zf{half}_{k}_{t}") for t in range(NH)]
                         for k in range(KD)]
                for tt in range(NH):
                    xb = xbs[tt] if xbs else s1_loads(src_ap, src_slots2d, half, tt)
                    s1_work([sl_zf[k][tt] for k in range(KD)], zf_ap, xb, tt)
                zf_state[half] = sl_zf

            def s2_stage(half):
                def loadf(m, pt, reg):
                    w1sl = reg.slot(f"w1_{half}_{m}")
                    w2sl = reg.slot(f"w2_{half}_{m}")
                    p.dma("gpsimd", pt[:, 0:KD * 128], wp["p_w1"][m], writes=[w1sl])
                    p.dma("gpsimd", pt[:, KD * 128:2 * KD * 128], wp["p_w2"][m],
                          writes=[w2sl])
                    return (w1sl, w2sl)
                return pool_stage(KF, loadf)

            def h_pool(wname, tag):
                def loadf(m, pt, reg):
                    wsl = reg.slot(f"{tag}w_{m}")
                    p.dma("gpsimd", pt[:, :], wp[wname][m], writes=[wsl])
                    return wsl
                return pool_stage(KF, loadf)

            def s5_pool(tag):
                def loadf(mo, pt, reg):
                    wsl = reg.slot(f"{tag}_{mo}")
                    p.dma("gpsimd", pt[:, :], wp["p_w3"][mo], writes=[wsl])
                    return wsl
                return pool_stage(KD, loadf)

            for half in range(2):
                th0 = half * HT
                u_i, h1_i = half % 2, (half + 1) % 2

                pr_s2, run_s2 = s2_stage(half)
                if half == 0:
                    if pre_s1h0 is not None:
                        pr_s2()
                        zf_state[0] = pre_s1h0
                    else:
                        xbs = [s1_loads(src_ap, src_slots2d, 0, tt)
                               for tt in range(NH)]
                        pr_s2()
                        do_s1(0, xbs=xbs)
                else:
                    pr_s2()
                sl_zf = zf_state[half]

                reg_A[u_i].turn()
                sl_u = [[reg_A[u_i].slot(f"u{half}_{m}_{t}") for t in range(NH)]
                        for m in range(KF)]
                u_ap = lambda m, t, u_i=u_i: A_t[u_i][:, m * HT + t * NT: m * HT + t * NT + NT]

                def s2_use(m, pt, sl):
                    w1sl, w2sl = sl
                    for t in range(NH):
                        b1 = p.next_bank()
                        b2 = p.next_bank()
                        p.mm_group(b1, [(pt[:, k * 128:(k + 1) * 128], zf_ap(k, t),
                                         (w1sl, sl_zf[k][t])) for k in range(KD)])
                        p.mm_group(b2, [(pt[:, KD * 128 + k * 128:KD * 128 + (k + 1) * 128],
                                         zf_ap(k, t), (w2sl, sl_zf[k][t])) for k in range(KD)])
                        ti = (m * NH + t) % 2
                        p.op("scalar", lambda e, b1=b1, m=m, ti=ti:
                             e.activation(tf_t[ti][:, :], b1.ap, AF.Silu, scale=t_s1[:, m:m + 1]),
                             reads=[b1, sl_vec["s1"]], writes=[sl_tf[ti]])
                        p.op("vector", lambda e, b2=b2, m=m, t=t, ti=ti:
                             e.scalar_tensor_tensor(u_ap(m, t), b2.ap, t_s2[:, m:m + 1],
                                                    tf_t[ti][:, :], ALU.mult, ALU.mult),
                             reads=[b2, sl_tf[ti], sl_vec["s2"]], writes=[sl_u[m][t]])
                run_s2(s2_use)

                def hstage(prun, tvec_t, tvec_sl, in_ap, in_sl, out_i, tag,
                           hook=None):
                    reg_A[out_i].turn()
                    o_sl = [[reg_A[out_i].slot(f"{tag}_{m}_{t}") for t in range(NH)]
                            for m in range(KF)]
                    o_ap = lambda m, t, out_i=out_i: A_t[out_i][:, m * HT + t * NT: m * HT + t * NT + NT]

                    def use(m, pt, wsl):
                        for t in range(NH):
                            b = p.next_bank()
                            p.mm_group(b, [(pt[:, k * 128:(k + 1) * 128],
                                            in_ap(k, t), (wsl, in_sl[k][t]))
                                           for k in range(KF)])
                            p.op("scalar", lambda e, b=b, m=m, t=t:
                                 e.activation(o_ap(m, t), b.ap, AF.Gelu,
                                              scale=tvec_t[:, m:m + 1]),
                                 reads=[b, tvec_sl], writes=[o_sl[m][t]])
                    prun(use, hook=hook)
                    return o_sl, o_ap

                s3hook = None
                if half == 0:
                    s3st = {}

                    def s3hook(m):
                        if m == 4:
                            s3st["xb"] = [s1_loads(src_ap, src_slots2d, 1, tt)
                                          for tt in range(NH)]
                            reg_zf.turn()
                            s3st["sl"] = [[reg_zf.slot(f"zf1_{k}_{t}")
                                           for t in range(NH)] for k in range(KD)]
                elif "s3h1mo" in hooks:
                    s3hook = hooks["s3h1mo"]

                pr_h1, run_h1 = h_pool("p_h1", f"h1_{half}")
                pr_h1()
                sl_h1, h1_ap = hstage(run_h1, t_t1, sl_vec["t1"], u_ap, sl_u,
                                      h1_i, f"h1_{half}", hook=s3hook)
                if half == 0:
                    for tt in range(NH):
                        s1_work([s3st["sl"][k][tt] for k in range(KD)],
                                zf_ap, s3st["xb"][tt], tt)
                    zf_state[1] = s3st["sl"]
                pr_h2, run_h2 = h_pool("p_h2", f"h2_{half}")
                pr_h2()
                sl_h2, h2_ap = hstage(run_h2, t_t2, sl_vec["t2"], h1_ap, sl_h1,
                                      u_i, f"h2_{half}")
                if half == 1 and "s4h1" in hooks:
                    hooks["s4h1"]()

                pr_s5, run_s5 = s5_pool(f"w3_{half}")
                pr_s5()
                ntile = KD * NH
                xt_pend = {}

                def xt_load(i):
                    mo_, t_ = divmod(i, NH)
                    ch = (th0 + t_ * NT) // NT
                    xi_ = i % 2
                    p.dma("sync", xt_t[xi_][:, :],
                          src_ap[mo_ * 128:(mo_ + 1) * 128, th0 + t_ * NT: th0 + (t_ + 1) * NT],
                          reads=[src_slots2d[mo_][ch]], writes=[sl_xt[xi_]])
                xt_load(0)

                def s5_use(mo, pt, wsl):
                    for t in range(NH):
                        i = mo * NH + t
                        ch = (th0 + t * NT) // NT
                        b = p.next_bank()
                        p.mm_group(b, [(pt[:, k * 128:(k + 1) * 128],
                                        h2_ap(k, t), (wsl, sl_h2[k][t]))
                                       for k in range(KF)])
                        if i + 1 < ntile:
                            xt_load(i + 1)
                        xi_ = i % 2
                        xs = sl_xt[xi_]
                        p.op("vector", lambda e, b=b, mo=mo, xi_=xi_:
                             e.scalar_tensor_tensor(xt_t[xi_][:, :], b.ap, t_s3h[:, mo:mo + 1],
                                                    xt_t[xi_][:, :], ALU.mult, ALU.add),
                             reads=[b, xs, sl_vec["s3h"]], writes=[xs])
                        p.dma("sync",
                              dst_ap[mo * 128:(mo + 1) * 128, th0 + t * NT: th0 + (t + 1) * NT],
                              xt_t[xi_][:, :], reads=[xs], writes=[dst_slots2d[mo][ch]])
                hook = hooks.get("s5h1mo") if half == 1 else None
                run_s5(s5_use, hook=hook)

        attn = {}

        def attn_carve_head():
            attn["wg_t"] = at("wg", OV + 0, [128, KD * 1024], BF16)
            attn["reg_wg"] = arena.region("wg", 0, KD * 2048)
            attn["sl_wg"] = [attn["reg_wg"].slot(f"wg{m}") for m in range(KD)]

        def attn_carve_head2():
            attn["wc_t"] = at("wc", OV + 16384, [128, KD * 1024], BF16)
            attn["reg_wc"] = arena.region("wc", 16384, KD * 2048)
            attn["sl_wc"] = [attn["reg_wc"].slot(f"wc{m}") for m in range(KD)]
            attn["xi_t"] = [at(f"xi{i}", OV + 32768 + i * 16384, [128, KD * NT], F32)
                            for i in range(2)]
            attn["reg_xi"] = [arena.region(f"xi{i}", 32768 + i * 16384, KD * NT * 4)
                              for i in range(2)]
            attn["z1_t"] = [at(f"z1{i}", OV + 65536 + i * 8192, [128, KD * NT], BF16)
                            for i in range(2)]
            attn["reg_z1"] = [arena.region(f"z1{i}", 65536 + i * 8192, KD * NT * 2)
                              for i in range(2)]

        def attn_carve_body():
            attn["g_t"] = at("g", OV + 81920, [128, 2 * NT], F32)
            rg = arena.region("g", 81920, 2 * NT * 4)
            attn["b_t"] = at("b", OV + 86016, [128, 2 * NT], F32)
            rb = arena.region("b", 86016, 2 * NT * 4)
            attn["h_t"] = at("h", OV + 90112, [128, 2 * NT], F32)
            rh = arena.region("h", 90112, 2 * NT * 4)
            attn["hl_t"] = at("hl", OV + 94208, [128, 16], F32)
            rhl = arena.region("hl", 94208, 64)
            attn["tb_t"] = at("tb", OV + 94720, [128, NT], F32)
            rtb = arena.region("tb", 94720, NT * 4)
            attn["sl_g"] = [rg.slot(f"gp{i}") for i in range(2)]
            attn["sl_b"] = [rb.slot(f"bp{i}") for i in range(2)]
            attn["sl_h"] = [rh.slot(f"hp{i}") for i in range(2)]
            attn["sl_hl"] = [rhl.slot(f"hl{i}") for i in range(2)]
            attn["sl_tb"] = rtb.slot("tb")
            attn["z2_t"] = at("z2", OV + 98304, [128, KD * T], BF16)
            attn["reg_z2"] = arena.region("z2", 98304, KD * T * 2)
            attn["sl_z2"] = [[attn["reg_z2"].slot(f"z2_{k}_{c}") for c in range(NCH)]
                             for k in range(KD)]

        def z2_ap(k, c):
            return attn["z2_t"][:, k * T + c * NT: k * T + c * NT + NT]

        AC = {}

        def attn_load_xb(c):
            bi = c % 2
            src_sl = [sl_xdram[r][c] for r in range(KD)]
            AC.setdefault(c, {})["xb"] = load_xb(xdram, src_sl, c * NT, bi)

        def attn_load_xi(c):
            bi = c % 2
            attn["reg_xi"][bi].turn()
            xisl = attn["reg_xi"][bi].slot(f"xi{c}")
            src_sl = [sl_xdram[r][c] for r in range(KD)]
            p.dma("gpsimd", attn["xi_t"][bi][:, :].rearrange("p (k t) -> p k t", t=NT),
                  x_kview(xdram, c * NT, NT), reads=src_sl, writes=[xisl])
            AC.setdefault(c, {})["xi"] = xisl

        def attn_loads(c):
            attn_load_xb(c)
            attn_load_xi(c)

        def attn_head(c):
            bi = c % 2
            xbs, xbt = AC[c]["xb"]
            xisl = AC[c]["xi"]
            xi_ap = lambda k, bi=bi: attn["xi_t"][bi][:, k * NT:(k + 1) * NT]
            r_sl, r_ap, mr_sl, bank_m = stats(
                xbs, xbt, "ln", bi, f1_out=(sl_mr[bi], mr_t[bi][:, :]),
                sq_eng="gpsimd")
            ln_apply(lambda k: (xi_ap(k), (xisl,)), r_sl, r_ap, bank_m,
                     lambda k: (xi_ap(k), xisl), mul_eng="gpsimd")
            attn["reg_z1"][bi].turn()
            z1sl = attn["reg_z1"][bi].slot(f"z1_{c}")
            z1_ap = lambda k, bi=bi: attn["z1_t"][bi][:, k * NT:(k + 1) * NT]
            for k in range(KD):
                p.op("vector", lambda e, k=k, bi=bi:
                     e.scalar_tensor_tensor(z1_ap(k), xi_ap(k), t_n1[:, k:k + 1],
                                            mr_t[bi][:, :], ALU.mult, ALU.mult),
                     reads=[xisl, sl_mr[bi], sl_vec["n1"]], writes=[z1sl])
            AC[c]["z1"] = z1sl

        def attn_gates_scan(c):
            bi = c % 2
            z1sl = AC[c]["z1"]
            xisl = AC[c]["xi"]
            xi_ap = lambda k, bi=bi: attn["xi_t"][bi][:, k * NT:(k + 1) * NT]
            z1_ap = lambda k, bi=bi: attn["z1_t"][bi][:, k * NT:(k + 1) * NT]
            wg_ap = lambda m, k: attn["wg_t"][:, m * 1024 + k * 128: m * 1024 + (k + 1) * 128]
            wc_ap = lambda m, k: attn["wc_t"][:, m * 1024 + k * 128: m * 1024 + (k + 1) * 128]
            g_t, b_t, h_t = attn["g_t"], attn["b_t"], attn["h_t"]
            hl_t, tb_t = attn["hl_t"], attn["tb_t"]
            sl_g, sl_b, sl_h = attn["sl_g"], attn["sl_b"], attn["sl_h"]
            sl_hl, sl_tb = attn["sl_hl"], attn["sl_tb"]
            for m in range(KD):
                pi = m % 2
                g_ap = g_t[:, pi * NT:(pi + 1) * NT]
                b_ap = b_t[:, pi * NT:(pi + 1) * NT]
                h_ap = h_t[:, pi * NT:(pi + 1) * NT]
                bg_ = p.next_bank()
                bc_ = p.next_bank()
                p.mm_group(bg_, [(wg_ap(m, k), z1_ap(k), (attn["sl_wg"][m], z1sl))
                                 for k in range(KD)])
                p.mm_group(bc_, [(wc_ap(m, k), z1_ap(k), (attn["sl_wc"][m], z1sl))
                                 for k in range(KD)])
                p.op("scalar", lambda e, bg_=bg_, m=m, g_ap=g_ap:
                     e.activation(g_ap, bg_.ap, AF.Sigmoid, bias=t_bg[:, m:m + 1]),
                     reads=[bg_, sl_vec["bg"]], writes=[sl_g[pi]])
                p.op("scalar", lambda e, bc_=bc_, m=m, b_ap=b_ap:
                     e.activation(b_ap, bc_.ap, AF.Tanh,
                                  scale=NEG_ONE, bias=t_bcn[:, m:m + 1]),
                     reads=[bc_, sl_vec["bcn"], sl_consts], writes=[sl_b[pi]])
                p.op("vector", lambda e, g_ap=g_ap, b_ap=b_ap:
                     e.scalar_tensor_tensor(b_ap, g_ap, 1.0, b_ap,
                                            ALU.subtract, ALU.mult),
                     reads=[sl_g[pi], sl_b[pi]], writes=[sl_b[pi]])
                if c == 0:
                    init = 0.0
                    rds = [sl_g[pi], sl_b[pi]]
                else:
                    init = hl_t[:, ((c - 1) % 2) * 8 + m: ((c - 1) % 2) * 8 + m + 1]
                    rds = [sl_g[pi], sl_b[pi], sl_hl[(c - 1) % 2]]
                p.op("vector", lambda e, init=init, h_ap=h_ap, g_ap=g_ap, b_ap=b_ap:
                     e.tensor_tensor_scan(h_ap, g_ap, b_ap, init,
                                          ALU.mult, ALU.add),
                     reads=rds, writes=[sl_h[pi]])
                p.op("gpsimd", lambda e, c=c, m=m, h_ap=h_ap:
                     e.tensor_copy(hl_t[:, (c % 2) * 8 + m: (c % 2) * 8 + m + 1],
                                   h_ap[:, NT - 1:NT]),
                     reads=[sl_h[pi]], writes=[sl_hl[c % 2]])
                p.op("gpsimd", lambda e, m=m, h_ap=h_ap:
                     e.tensor_add(xi_ap(m), xi_ap(m), h_ap),
                     reads=[sl_h[pi], xisl], writes=[xisl])

        def attn_deferred(c):
            bi = c % 2
            xisl = AC[c]["xi"]
            xi_ap = lambda k, bi=bi: attn["xi_t"][bi][:, k * NT:(k + 1) * NT]
            r2_sl, r2_ap, _, _ = stats(None, xb_t[bi], "rms", bi,
                                       sq_src=(xisl, xi_ap))
            for k in range(KD):
                p.op("vector", lambda e, k=k, c=c:
                     e.scalar_tensor_tensor(z2_ap(k, c), xi_ap(k), t_n2[:, k:k + 1],
                                            r2_ap, ALU.mult, ALU.mult),
                     reads=[xisl, r2_sl, sl_vec["n2"]], writes=[attn["sl_z2"][k][c]])
            p.dma("gpsimd", x_kview(xdram, c * NT, NT),
                  attn["xi_t"][bi][:, :].rearrange("p (k t) -> p k t", t=NT),
                  reads=[xisl] + [sl_xdram[r][c] for r in range(KD)],
                  writes=[sl_xdram[r][c] for r in range(KD)],
                  accum_op=ALU.add)

        def make_attn_B(w13h1_hook=None):
            def w13_pool(hb):
                def loadf(m, pt, reg):
                    w1sl = reg.slot(f"aw1_{hb}_{m}")
                    w3sl = reg.slot(f"aw3_{hb}_{m}")
                    p.dma("gpsimd", pt[:, 0:KD * 128], wp["p_aw1"][m], writes=[w1sl])
                    p.dma("gpsimd", pt[:, KD * 128:2 * KD * 128], wp["p_aw3"][m],
                          writes=[w3sl])
                    return (w1sl, w3sl)
                return pool_stage(KF, loadf)

            def w2_pool(hb):
                def loadf(mo, pt, reg):
                    wsl = reg.slot(f"aw2_{hb}_{mo}")
                    p.dma("gpsimd", pt[:, :], wp["p_aw2"][mo], writes=[wsl])
                    return wsl
                return pool_stage(KD, loadf)

            st_w13 = [w13_pool(0), w13_pool(1)]
            st_w2 = [w2_pool(0), w2_pool(1)]

            def prime_h0():
                st_w13[0][0]()

            def run():
                uh_t = at("uh", OV + 0, [128, KF * HT], BF16)
                reg_uh = arena.region("uh", 0, KF * HT * 2)
                for hb in range(2):
                    th0 = hb * HT
                    reg_uh.turn()
                    sl_uh = [[reg_uh.slot(f"uh{hb}_{m}_{t}") for t in range(NH)]
                             for m in range(KF)]
                    uh_ap = lambda m, t: uh_t[:, m * HT + t * NT: m * HT + t * NT + NT]

                    def w13_use(m, pt, sl):
                        w1sl, w3sl = sl
                        for t in range(NH):
                            c = hb * NH + t
                            b1 = p.next_bank()
                            b3 = p.next_bank()
                            p.mm_group(b1, [(pt[:, k * 128:(k + 1) * 128], z2_ap(k, c),
                                             (w1sl, attn["sl_z2"][k][c])) for k in range(KD)])
                            p.mm_group(b3, [(pt[:, KD * 128 + k * 128:KD * 128 + (k + 1) * 128],
                                             z2_ap(k, c), (w3sl, attn["sl_z2"][k][c]))
                                            for k in range(KD)])
                            ti = (m * NH + t) % 2
                            p.op("scalar", lambda e, b1=b1, ti=ti:
                                 e.activation(tf_t[ti][:, :], b1.ap, AF.Silu),
                                 reads=[b1], writes=[sl_tf[ti]])
                            p.op("vector", lambda e, b3=b3, m=m, t=t, ti=ti:
                                 e.tensor_mul(uh_ap(m, t), tf_t[ti][:, :], b3.ap),
                                 reads=[b3, sl_tf[ti]], writes=[sl_uh[m][t]])
                    hook = w13h1_hook if hb == 1 else None
                    st_w13[hb][1](w13_use, hook=hook)

                    ntile = KD * NH
                    st_w2[hb][0]()

                    def xt_load(i):
                        mo_, t_ = divmod(i, NH)
                        c = hb * NH + t_
                        xi_ = i % 2
                        p.dma("sync", xt_t[xi_][:, :],
                              xdram[mo_ * 128:(mo_ + 1) * 128, th0 + t_ * NT: th0 + (t_ + 1) * NT],
                              reads=[sl_xdram[mo_][c]], writes=[sl_xt[xi_]])
                    xt_load(0)

                    def w2_use(mo, pt, wsl):
                        for t in range(NH):
                            c = hb * NH + t
                            i = mo * NH + t
                            b = p.next_bank()
                            p.mm_group(b, [(pt[:, k * 128:(k + 1) * 128],
                                            uh_ap(k, t), (wsl, sl_uh[k][t]))
                                           for k in range(KF)])
                            if i + 1 < ntile:
                                xt_load(i + 1)
                            xi_ = i % 2
                            xs = sl_xt[xi_]
                            p.op("vector", lambda e, b=b, xi_=xi_:
                                 e.tensor_add(xt_t[xi_][:, :], xt_t[xi_][:, :], b.ap),
                                 reads=[b, xs], writes=[xs])
                            p.dma("sync",
                                  xdram[mo * 128:(mo + 1) * 128, th0 + t * NT: th0 + (t + 1) * NT],
                                  xt_t[xi_][:, :], reads=[xs], writes=[sl_xdram[mo][c]])
                    st_w2[hb][1](w2_use)
                    if hb == 0:
                        st_w13[1][0]()
            return prime_h0, run

        xin_slots = [[Slot(f"xin{r}_{t}", frozen=True) for t in range(NCH)]
                     for r in range(KD)]

        zf_t = at("zf", OV + 0, [128, KD * HT], BF16)
        reg_zf = arena.region("zf", 0, KD * HT * 2)
        A_t = [at("A0", OV + 16384, [128, KF * HT], BF16),
               at("A1", OV + 81920, [128, KF * HT], BF16)]
        reg_A = [arena.region("A0", 16384, KF * HT * 2),
                 arena.region("A1", 81920, KF * HT * 2)]

        def hook_s3h1mo(m):
            if m == 2:
                attn_carve_head()
            if m >= 4 and m % 4 == 0 and m <= 28:
                for mw in ([m // 4 - 1] if m < 28 else [6, 7]):
                    p.dma("gpsimd", attn["wg_t"][:, mw * 1024:(mw + 1) * 1024],
                          wp["p_wg"][mw], writes=[attn["sl_wg"][mw]])

        def hook_s4h1():
            attn_carve_head2()

        def hook_s5h1mo(mo):
            if mo == 0:
                attn_loads(0)
            elif mo == 1:
                attn_loads(1)
            elif mo == 2:
                for m in range(4):
                    p.dma("gpsimd", attn["wc_t"][:, m * 1024:(m + 1) * 1024],
                          wp["p_wc"][m], writes=[attn["sl_wc"][m]])
            elif mo == 3:
                for m in range(4, KD):
                    p.dma("gpsimd", attn["wc_t"][:, m * 1024:(m + 1) * 1024],
                          wp["p_wc"][m], writes=[attn["sl_wc"][m]])
            elif mo == 4:
                attn_head(0)
            elif mo == 6:
                attn_head(1)

        emit_ffn(xin, xin_slots, xdram, sl_xdram, A_t, reg_A, zf_t, reg_zf,
                 hooks={"s3h1mo": hook_s3h1mo, "s4h1": hook_s4h1,
                        "s5h1mo": hook_s5h1mo})

        zfb_t = None

        def mk_ffn2_s1_hooks():
            state = {}

            def hook(m):
                nonlocal zfb_t
                if m == 2:
                    zfb_t = at("zfb", OV + 65536, [128, KD * HT], BF16)
                    state["reg"] = arena.region("zfb", 65536, KD * HT * 2)
                    state["sl"] = [[state["reg"].slot(f"zfb0_{k}_{t}")
                                    for t in range(NH)] for k in range(KD)]
                    state["ap"] = lambda k, t: zfb_t[:, k * HT + t * NT: k * HT + t * NT + NT]
                    state["xb0"] = load_xb(xdram, [sl_xdram[r][0] for r in range(KD)],
                                           0, 0)
                elif m == 8:
                    xbs, xbt = state["xb0"]
                    r_sl, r_ap, mr_sl, bank_m = stats(xbs, xbt, "ln", 0,
                                                      sq_eng="scalar")
                    ln_apply(lambda k: (xbt[:, k * NT:(k + 1) * NT], (xbs,)),
                             r_sl, r_ap, bank_m,
                             lambda k: (state["ap"](k, 0), state["sl"][k][0]),
                             tfi=(2, 3))
                elif m == 16:
                    state["xb1"] = load_xb(xdram, [sl_xdram[r][1] for r in range(KD)],
                                           NT, 1)
                elif m == 22:
                    xbs, xbt = state["xb1"]
                    r_sl, r_ap, mr_sl, bank_m = stats(xbs, xbt, "ln", 1,
                                                      sq_eng="scalar")
                    ln_apply(lambda k: (xbt[:, k * NT:(k + 1) * NT], (xbs,)),
                             r_sl, r_ap, bank_m,
                             lambda k: (state["ap"](k, 1), state["sl"][k][1]),
                             tfi=(2, 3))
            return state, hook

        attn_carve_body()
        s1_state, s1_hook = mk_ffn2_s1_hooks()
        primeB, runB = make_attn_B(w13h1_hook=s1_hook)
        primeB()

        attn_load_xb(2)
        for c in range(NCH):
            if c >= 1:
                attn_deferred(c - 1)
                if c + 1 < NCH:
                    attn_load_xi(c + 1)
                if c + 2 < NCH:
                    attn_load_xb(c + 2)
                if c + 1 < NCH:
                    attn_head(c + 1)
            attn_gates_scan(c)
        attn_deferred(NCH - 1)

        runB()

        A2_t = [at("A0b", OV + 0, [128, KF * HT], BF16),
                at("A1b", OV + 81920, [128, KF * HT], BF16)]
        reg_A2 = [arena.region("A0b", 0, KF * HT * 2),
                  arena.region("A1b", 81920, KF * HT * 2)]
        reg_zf2 = s1_state["reg"]
        emit_ffn(xdram, sl_xdram, y_d, sl_y, A2_t, reg_A2, zfb_t, reg_zf2,
                 pre_s1h0=[[s1_state["sl"][k][t] for t in range(NH)]
                           for k in range(KD)])

        for i in range(NQSEM):
            if p.cnt[f"sy{i}"]:
                p.eng("sync").wait_ge(p.sem[f"sy{i}"], p.cnt[f"sy{i}"])
            if p.cnt[f"wg{i}"]:
                p.eng("gpsimd").wait_ge(p.sem[f"wg{i}"], p.cnt[f"wg{i}"])

    return nc



def _swizzle_panels(w_io):
    I, O = w_io.shape
    return np.ascontiguousarray(
        w_io.reshape(I // 128, 128, O // 128, 128).transpose(2, 1, 0, 3)
        .reshape(O // 128, 128, I))


def make_core_inputs(xb_core, W):
    d = {"xin": np.ascontiguousarray(xb_core.T)}
    d["p_w1"] = W["p_w1"]; d["p_w2"] = W["p_w2"]; d["p_h1"] = W["p_h1"]
    d["p_h2"] = W["p_h2"]; d["p_w3"] = W["p_w3"]; d["p_wg"] = W["p_wg"]
    d["p_wc"] = W["p_wc"]; d["p_aw1"] = W["p_aw1"]; d["p_aw3"] = W["p_aw3"]
    d["p_aw2"] = W["p_aw2"]
    for k in ("v_bg", "v_bc", "v_n1", "v_n2", "v_s1", "v_s2", "v_t1", "v_t2", "v_s3"):
        d[k] = W[k]
    return d


def prep_weights(L0_attn_Wg, L0_attn_bg, L0_attn_Wc, L0_attn_bc, L0_attn_n1_w,
                 L0_attn_n2_w, L0_attn_W1, L0_attn_W3, L0_attn_W2,
                 L0_ffn_w1_w, L0_ffn_w1_s, L0_ffn_w2_w, L0_ffn_w2_s,
                 L0_ffn_h1_w, L0_ffn_h1_s, L0_ffn_h2_w, L0_ffn_h2_s,
                 L0_ffn_w3_w, L0_ffn_w3_s):
    f32 = lambda a: np.ascontiguousarray(np.asarray(a), dtype=np.float32)
    W = {}
    W["p_w1"] = _swizzle_panels(f32(L0_ffn_w1_w).T)
    W["p_w2"] = _swizzle_panels(f32(L0_ffn_w2_w).T)
    W["p_h1"] = _swizzle_panels(f32(L0_ffn_h1_w).T)
    W["p_h2"] = _swizzle_panels(f32(L0_ffn_h2_w).T)
    W["p_w3"] = _swizzle_panels(f32(L0_ffn_w3_w).T)
    W["p_wg"] = _swizzle_panels(f32(L0_attn_Wg))
    W["p_wc"] = _swizzle_panels(f32(L0_attn_Wc))
    W["p_aw1"] = _swizzle_panels(f32(L0_attn_W1))
    W["p_aw3"] = _swizzle_panels(f32(L0_attn_W3))
    W["p_aw2"] = _swizzle_panels(f32(L0_attn_W2))
    W["v_bg"] = f32(L0_attn_bg).reshape(KD, 128)
    W["v_bc"] = f32(L0_attn_bc).reshape(KD, 128)
    W["v_n1"] = f32(L0_attn_n1_w).reshape(KD, 128)
    W["v_n2"] = f32(L0_attn_n2_w).reshape(KD, 128)
    W["v_s1"] = f32(L0_ffn_w1_s).reshape(KF, 128)
    W["v_s2"] = f32(L0_ffn_w2_s).reshape(KF, 128)
    W["v_t1"] = f32(L0_ffn_h1_s).reshape(KF, 128)
    W["v_t2"] = f32(L0_ffn_h2_s).reshape(KF, 128)
    W["v_s3"] = f32(L0_ffn_w3_s).reshape(KD, 128)
    return W


def kernel(L0_attn_x, L0_attn_Wg, L0_attn_bg, L0_attn_Wc, L0_attn_bc,
           L0_attn_n1_w, L0_attn_n2_w, L0_attn_W1, L0_attn_W3, L0_attn_W2,
           L0_ffn_w1_w, L0_ffn_w1_s, L0_ffn_w2_w, L0_ffn_w2_s,
           L0_ffn_h1_w, L0_ffn_h1_s, L0_ffn_h2_w, L0_ffn_h2_s,
           L0_ffn_w3_w, L0_ffn_w3_s):
    x = np.ascontiguousarray(np.asarray(L0_attn_x), dtype=np.float32)
    B, T, _ = x.shape
    W = prep_weights(L0_attn_Wg, L0_attn_bg, L0_attn_Wc, L0_attn_bc, L0_attn_n1_w,
                     L0_attn_n2_w, L0_attn_W1, L0_attn_W3, L0_attn_W2,
                     L0_ffn_w1_w, L0_ffn_w1_s, L0_ffn_w2_w, L0_ffn_w2_s,
                     L0_ffn_h1_w, L0_ffn_h1_s, L0_ffn_h2_w, L0_ffn_h2_s,
                     L0_ffn_w3_w, L0_ffn_w3_s)
    nc = build_program(T=T)
    in_maps = [make_core_inputs(x[i], W) for i in range(B)]
    res = run_bass_kernel_spmd(nc, in_maps, core_ids=list(range(B)))
    out = np.stack([res.results[i]["y"] for i in range(B)], axis=0)
    return np.ascontiguousarray(out.transpose(0, 2, 1))


# version 13
# speedup vs baseline: 1.0503x; 1.0154x over previous
import numpy as np
import ml_dtypes
from contextlib import ExitStack

import concourse.bass as bass
import concourse.mybir as mybir
from concourse.bass_utils import run_bass_kernel_spmd

F32 = mybir.dt.float32
BF16 = mybir.dt.bfloat16
AF = mybir.ActivationFunctionType
ALU = mybir.AluOpType

D = 1024
DFF = 4096
KD = D // 128
KF = DFF // 128
NT = 512
NP = 3

ENG_SEM = {"tensor": "pe", "scalar": "sc", "vector": "ve", "gpsimd": "gp"}
NQSEM = 8


class Slot:
    __slots__ = ("ready", "weng", "wseq", "users", "frozen", "name")

    def __init__(self, name="", users=None, frozen=False):
        self.ready = None
        self.weng = None
        self.wseq = 0
        self.users = list(users) if users else []
        self.frozen = frozen
        self.name = name


def marks_max(slots):
    mx = {}
    for s in slots:
        for m in ([s.ready] if s.ready else []) + s.users:
            if m[1] > mx.get(m[0], 0):
                mx[m[0]] = m[1]
    return [(k, v) for k, v in mx.items()]


class Region:

    def __init__(self, name, marks0=()):
        self.name = name
        self.marks = list(marks0)
        self.cur = []

    def slot(self, name):
        s = Slot(name, users=list(self.marks))
        self.cur.append(s)
        return s

    def turn(self):
        if self.cur:
            self.marks = marks_max(self.cur)
            self.cur = []

    def snapshot(self):
        mx = dict(self.marks)
        for sem, v in marks_max(self.cur):
            if v > mx.get(sem, 0):
                mx[sem] = v
        return mx


class Arena:

    def __init__(self):
        self.carves = []

    def region(self, name, start, nbytes):
        mx = {}
        for s, e, r in self.carves:
            if s < start + nbytes and start < e:
                for sem, v in r.snapshot().items():
                    if v > mx.get(sem, 0):
                        mx[sem] = v
        reg = Region(name, list(mx.items()))
        self.carves.append((start, start + nbytes, reg))
        return reg


class P:
    def __init__(self, nc, ctx):
        self.nc = nc
        names = ["pe", "sc", "ve", "gp"] + [f"sy{i}" for i in range(NQSEM)] + \
                [f"wg{i}" for i in range(NQSEM)]
        self.sem = {s: ctx.enter_context(nc.semaphore("s_" + s)) for s in names}
        self.cnt = {s: 0 for s in names}
        ENGS = ("tensor", "scalar", "vector", "sync", "gpsimd")
        self.waited = {e: {s: 0 for s in names} for e in ENGS}
        self.nseq = {e: 0 for e in ENGS}
        self.sealed = {e: 0 for e in ENGS}
        self.qi = {"sync": 0, "gpsimd": 0}
        self.bank_i = 0
        self.banks = None

    def eng(self, name):
        return getattr(self.nc, name)

    def wait(self, ename, dep):
        if dep is None:
            return
        sem, val = dep
        if val <= 0 or self.waited[ename][sem] >= val or ENG_SEM.get(ename) == sem:
            return
        self.eng(ename).wait_ge(self.sem[sem], val)
        self.waited[ename][sem] = val

    def _pre(self, ename, reads, writes):
        own = ENG_SEM.get(ename)
        need_drain = False
        for s in reads:
            if s.ready is None:
                assert s.frozen, f"read of unwritten slot {s.name}"
                continue
            if own is not None and s.ready[0] == own:
                if s.ready[1] > self.sealed[ename]:
                    need_drain = True
            else:
                self.wait(ename, s.ready)
        for s in writes:
            assert not s.frozen, f"write to frozen slot {s.name}"
            for u in s.users:
                if own is not None and u[0] == own:
                    if u[1] > self.sealed[ename]:
                        need_drain = True
                else:
                    self.wait(ename, u)
            if s.ready is not None:
                if not (s.weng == ename and own is not None and s.ready[0] == own):
                    self.wait(ename, s.ready)
        if need_drain:
            self.eng(ename).drain()
            self.sealed[ename] = self.cnt[own]

    def _post(self, ename, ins, sem, n, reads, writes):
        self.nseq[ename] += 1
        ins.then_inc(self.sem[sem], n)
        self.cnt[sem] += n
        mark = (sem, self.cnt[sem])
        for s in reads:
            if not s.frozen:
                s.users.append(mark)
        for s in writes:
            s.ready = mark
            s.weng = ename
            s.wseq = self.nseq[ename]
            s.users = []
        return mark

    def op(self, ename, emit, reads=(), writes=()):
        self._pre(ename, reads, writes)
        ins = emit(self.eng(ename))
        return self._post(ename, ins, ENG_SEM[ename], 1, reads, writes)

    def dma(self, q, out_ap, in_ap, reads=(), writes=(), **kw):
        pfx = "sy" if q == "sync" else "wg"
        sem = f"{pfx}{self.qi[q] % NQSEM}"
        self.qi[q] += 1
        self.wait(q, (sem, self.cnt[sem]))
        self._pre(q, reads, writes)
        ins = self.eng(q).dma_start(out_ap, in_ap, **kw)
        return self._post(q, ins, sem, 16, reads, writes)

    def mm_group(self, bank, mms, start=True, stop=True):
        ename = "tensor"
        if start:
            for u in bank.users:
                self.wait(ename, u)
        n = len(mms)
        for i, (l, r, rds) in enumerate(mms):
            for s in rds:
                if s.ready is not None:
                    self.wait(ename, s.ready)
            ins = self.nc.tensor.matmul(bank.ap, l, r, start=(start and i == 0),
                                        stop=(stop and i == n - 1))
            self.nseq[ename] += 1
            if i == n - 1:
                ins.then_inc(self.sem["pe"], 1)
                self.cnt["pe"] += 1
        mark = ("pe", self.cnt["pe"])
        if stop:
            bank.ready = mark
            bank.weng = ename
            bank.users = []
        for _, _, rds in mms:
            for s in rds:
                if not s.frozen:
                    s.users.append(mark)
        return mark

    def next_bank(self):
        b = self.banks[self.bank_i % 8]
        self.bank_i += 1
        return b


def build_program(T=2048):
    HT = T // 2
    NH = HT // NT
    NCH = T // NT
    nc = bass.Bass("TRN2", target_bir_lowering=False)

    xin = nc.dram_tensor("xin", [D, T], F32, kind="ExternalInput").ap()
    y_d = nc.dram_tensor("y", [D, T], F32, kind="ExternalOutput").ap()
    xdram = nc.dram_tensor("xdram", [D, T], F32).ap()

    wp = {}
    for name, mch, kch in [("p_w1", KF, KD), ("p_w2", KF, KD), ("p_h1", KF, KF),
                           ("p_h2", KF, KF), ("p_w3", KD, KF), ("p_wg", KD, KD),
                           ("p_wc", KD, KD), ("p_aw1", KF, KD), ("p_aw3", KF, KD),
                           ("p_aw2", KD, KF)]:
        wp[name] = nc.dram_tensor(name, [mch, 128, kch * 128], F32,
                                  kind="ExternalInput").ap()
    vin = {}
    for name, chk in [("v_bg", KD), ("v_bc", KD), ("v_n1", KD), ("v_n2", KD),
                      ("v_s1", KF), ("v_s2", KF), ("v_t1", KF), ("v_t2", KF),
                      ("v_s3", KD)]:
        vin[name] = nc.dram_tensor(name, [chk, 128], F32, kind="ExternalInput").ap()

    def x_kview(xap, t0, tl):
        return xap.rearrange("(k p) t -> p k t", p=128)[:, :, t0:t0 + tl]

    with ExitStack() as ctx:
        p = P(nc, ctx)
        BASE = (nc.sbuf_base + 31) & ~31
        SLAB = 211968
        ctx.enter_context(nc.sbuf_tensor("slab", [128, SLAB], mybir.dt.int8))
        at = lambda name, off, shape, dt: nc.alloc_sbuf_tensor_at(
            name, shape, dt, offset=BASE + off)

        ones_bf = at("ones_bf", 0, [128, 128], BF16)
        consts = at("consts", 256, [128, 4], F32)
        t_bg = at("t_bg", 288, [128, KD], F32)
        t_bcn = at("t_bcn", 320, [128, KD], F32)
        t_n1 = at("t_n1", 352, [128, KD], F32)
        t_n2 = at("t_n2", 384, [128, KD], F32)
        t_s3h = at("t_s3h", 416, [128, KD], F32)
        t_s1 = at("t_s1", 448, [128, KF], F32)
        t_s2 = at("t_s2", 576, [128, KF], F32)
        t_t1 = at("t_t1", 704, [128, KF], F32)
        t_t2 = at("t_t2", 832, [128, KF], F32)
        xsq_t = [at(f"xsq{i}", 1024 + i * 1024, [128, NT], BF16) for i in range(2)]
        tf_t = [at(f"tf{i}", 3072 + i * 2048, [128, NT], F32) for i in range(4)]
        rs_t = [at(f"rs{i}", 11264 + i * 2048, [128, NT], F32) for i in range(2)]
        mr_t = [at(f"mr{i}", 15360 + i * 2048, [128, NT], F32) for i in range(2)]
        xt_t = [at(f"xt{i}", 19456 + i * 2048, [128, NT], F32) for i in range(2)]
        xb_t = [at(f"xb{i}", 23552 + i * 8192, [128, KD * NT], BF16) for i in range(2)]
        pool_t = [at(f"pool{i}", 39936 + i * 8192, [128, KF * 128], BF16)
                  for i in range(NP)]
        OV = 64512

        pb_t = [ctx.enter_context(nc.psum_tensor(f"pb{i}", [128, NT], F32))
                for i in range(8)]

        EPS_LN = consts[:, 0:1]
        EPS_RMS = consts[:, 1:2]
        NEG_HALF = consts[:, 2:3]
        NEG_ONE = consts[:, 3:4]

        class BankSlot(Slot):
            __slots__ = ("ap",)
        banks = []
        for i in range(8):
            s = BankSlot(f"pb{i}")
            s.ap = pb_t[i][:, :]
            banks.append(s)
        p.banks = banks

        arena = Arena()

        sl_ones = Slot("ones")
        sl_consts = Slot("consts")
        sl_vec = {n: Slot(n) for n in
                  ("bg", "bcn", "n1", "n2", "s1", "s2", "t1", "t2", "s3h")}
        sl_xb = [Slot(f"xb{i}") for i in range(2)]
        sl_xsq = [Slot(f"xsq{i}") for i in range(2)]
        sl_tf = [Slot(f"tf{i}") for i in range(4)]
        sl_rs = [Slot(f"rs{i}") for i in range(2)]
        sl_mr = [Slot(f"mr{i}") for i in range(2)]
        sl_xt = [Slot(f"xt{i}") for i in range(2)]
        sl_xdram = [[Slot(f"xd{r}_{t}") for t in range(NCH)] for r in range(KD)]
        sl_y = [[Slot(f"yy{r}_{t}") for t in range(NCH)] for r in range(KD)]
        reg_pool = [Region(f"pool{i}") for i in range(NP)]

        block = ctx.enter_context(nc.Block())

        p.op("vector", lambda e: e.memset(ones_bf[:, :], 1.0 / D), writes=[sl_ones])
        p.op("vector", lambda e: e.memset(consts[:, 0:1], 1e-5), writes=[sl_consts])
        p.op("vector", lambda e: e.memset(consts[:, 1:2], 1e-6), writes=[sl_consts])
        p.op("vector", lambda e: e.memset(consts[:, 2:3], -0.5), writes=[sl_consts])
        p.op("vector", lambda e: e.memset(consts[:, 3:4], -1.0), writes=[sl_consts])
        sl_consts.frozen = True
        sl_ones.frozen = True
        with nc.allow_non_contiguous_dma(reason="tiny vector loads"):
            for nm, tile, slot in [("v_bg", t_bg, sl_vec["bg"]), ("v_bc", t_bcn, sl_vec["bcn"]),
                                   ("v_n1", t_n1, sl_vec["n1"]), ("v_n2", t_n2, sl_vec["n2"]),
                                   ("v_s1", t_s1, sl_vec["s1"]), ("v_s2", t_s2, sl_vec["s2"]),
                                   ("v_t1", t_t1, sl_vec["t1"]), ("v_t2", t_t2, sl_vec["t2"]),
                                   ("v_s3", t_s3h, sl_vec["s3h"])]:
                p.dma("sync", tile[:, :], vin[nm].transpose([1, 0]), writes=[slot])
        p.op("vector", lambda e: e.tensor_scalar_mul(t_bcn[:, :], t_bcn[:, :], -1.0),
             reads=[sl_vec["bcn"]], writes=[sl_vec["bcn"]])
        p.op("vector", lambda e: e.tensor_scalar_mul(t_s3h[:, :], t_s3h[:, :], 0.5),
             reads=[sl_vec["s3h"]], writes=[sl_vec["s3h"]])
        for s in sl_vec.values():
            s.frozen = True

        def load_xb(src_ap, src_slots, t0, bi):
            dst = xb_t[bi][:, :].rearrange("p (k t) -> p k t", t=NT)
            p.dma("gpsimd", dst, x_kview(src_ap, t0, NT),
                  reads=src_slots, writes=[sl_xb[bi]])
            return sl_xb[bi], xb_t[bi]

        def stats(xbslot, xb_tile, kind, si, sq_src=None, f1_out=None,
                  sq_eng="scalar"):
            xb_ap = lambda k: xb_tile[:, k * NT:(k + 1) * NT]
            sqs, sq_ap = (xbslot, xb_ap) if sq_src is None else sq_src
            bank_m = None
            if kind == "ln":
                bank_m = p.next_bank()
                p.mm_group(bank_m, [(ones_bf[:, :], xb_ap(k), (xbslot, sl_ones))
                                    for k in range(KD)])
            bank_q = p.next_bank()
            for batch in range(KD // 2):
                mms = []
                for j in range(2):
                    k = batch * 2 + j
                    if sq_eng == "scalar":
                        p.op("scalar", lambda e, k=k, j=j:
                             e.activation(xsq_t[j][:, :], sq_ap(k), AF.Square),
                             reads=[sqs], writes=[sl_xsq[j]])
                    else:
                        p.op(sq_eng, lambda e, k=k, j=j:
                             e.tensor_mul(xsq_t[j][:, :], sq_ap(k), sq_ap(k)),
                             reads=[sqs], writes=[sl_xsq[j]])
                    mms.append((ones_bf[:, :], xsq_t[j][:, :], (sl_xsq[j], sl_ones)))
                p.mm_group(bank_q, mms, start=(batch == 0), stop=(batch == KD // 2 - 1))
            r_sl, r_ap = sl_rs[si], rs_t[si][:, :]
            if kind == "ln":
                mr_sl, mr_ap = sl_mr[si], mr_t[si][:, :]
                p.op("scalar", lambda e: e.activation(mr_ap, bank_m.ap, AF.Square),
                     reads=[bank_m], writes=[mr_sl])
                p.op("vector", lambda e: e.tensor_sub(mr_ap, bank_q.ap, mr_ap),
                     reads=[bank_q, mr_sl], writes=[mr_sl])
                p.op("scalar", lambda e: e.activation(r_ap, mr_ap, AF.Ln, bias=EPS_LN),
                     reads=[mr_sl, sl_consts], writes=[r_sl])
                p.op("scalar", lambda e: e.activation(r_ap, r_ap, AF.Exp, scale=NEG_HALF),
                     reads=[r_sl, sl_consts], writes=[r_sl])
                if f1_out is not None:
                    f1sl, f1ap = f1_out
                    p.op("vector", lambda e: e.tensor_mul(f1ap, mr_ap, r_ap),
                         reads=[mr_sl, r_sl], writes=[f1sl])
                    p.op("vector", lambda e: e.tensor_mul(f1ap, f1ap, r_ap),
                         reads=[f1sl, r_sl], writes=[f1sl])
                    p.op("scalar", lambda e: e.activation(f1ap, f1ap, AF.Ln, bias=EPS_RMS),
                         reads=[f1sl, sl_consts], writes=[f1sl])
                    p.op("scalar", lambda e: e.activation(f1ap, f1ap, AF.Exp, scale=NEG_HALF),
                         reads=[f1sl, sl_consts], writes=[f1sl])
                return r_sl, r_ap, mr_sl, bank_m
            else:
                p.op("scalar", lambda e: e.activation(r_ap, bank_q.ap, AF.Ln, bias=EPS_RMS),
                     reads=[bank_q, sl_consts], writes=[r_sl])
                p.op("scalar", lambda e: e.activation(r_ap, r_ap, AF.Exp, scale=NEG_HALF),
                     reads=[r_sl, sl_consts], writes=[r_sl])
                return r_sl, r_ap, None, None

        def ln_apply(in_f, r_sl, r_ap, bank_m, out_f, tfi=(0, 1, 2, 3),
                     mul_eng="vector"):
            nb = len(tfi)
            for h4 in range(KD // nb):
                ks = range(h4 * nb, h4 * nb + nb)
                for k in ks:
                    iap, irds = in_f(k)
                    ti = tfi[k % nb]
                    p.op("vector", lambda e, iap=iap, ti=ti:
                         e.tensor_sub(tf_t[ti][:, :], iap, bank_m.ap),
                         reads=list(irds) + [bank_m], writes=[sl_tf[ti]])
                for k in ks:
                    oap, osl = out_f(k)
                    ti = tfi[k % nb]
                    p.op(mul_eng, lambda e, oap=oap, ti=ti:
                         e.tensor_mul(oap, tf_t[ti][:, :], r_ap),
                         reads=[sl_tf[ti], r_sl], writes=[osl])

        def pool_stage(M, loadf, K=2):
            st = {"slots": {}, "primed": False}

            def load(m):
                pi = m % NP
                reg_pool[pi].turn()
                st["slots"][m] = loadf(m, pool_t[pi], reg_pool[pi])

            def prime():
                if not st["primed"]:
                    st["primed"] = True
                    for m in range(min(K, M)):
                        load(m)

            def run(usef, hook=None):
                prime()
                for m in range(M):
                    if m + K < M:
                        load(m + K)
                    usef(m, pool_t[m % NP], st["slots"].pop(m))
                    if hook:
                        hook(m)
            return prime, run

        def s1_loads(src_ap, src_slots2d, half, tt):
            t0 = half * HT + tt * NT
            ch = t0 // NT
            return load_xb(src_ap, [src_slots2d[r][ch] for r in range(KD)],
                           t0, tt % 2)

        def s1_work(sl_zf_col, zf_ap_half, xb, tt, sq_eng="scalar",
                    tfi=(0, 1, 2, 3), mul_eng="vector"):
            xbs, xbt = xb
            r_sl, r_ap, mr_sl, bank_m = stats(xbs, xbt, "ln", tt % 2, sq_eng=sq_eng)
            ln_apply(lambda k: (xbt[:, k * NT:(k + 1) * NT], (xbs,)),
                     r_sl, r_ap, bank_m,
                     lambda k: (zf_ap_half(k, tt), sl_zf_col[k]), tfi=tfi,
                     mul_eng=mul_eng)

        def emit_ffn(src_ap, src_slots2d, dst_ap, dst_slots2d, A_t, reg_A,
                     zf_t, reg_zf, hooks=None, pre_s1h0=None):
            hooks = hooks or {}
            zf_ap = lambda k, t: zf_t[:, k * HT + t * NT: k * HT + t * NT + NT]
            zf_state = {}

            def do_s1(half, xbs=None):
                reg_zf.turn()
                sl_zf = [[reg_zf.slot(f"zf{half}_{k}_{t}") for t in range(NH)]
                         for k in range(KD)]
                for tt in range(NH):
                    xb = xbs[tt] if xbs else s1_loads(src_ap, src_slots2d, half, tt)
                    s1_work([sl_zf[k][tt] for k in range(KD)], zf_ap, xb, tt)
                zf_state[half] = sl_zf

            def s2_stage(half):
                def loadf(m, pt, reg):
                    w1sl = reg.slot(f"w1_{half}_{m}")
                    w2sl = reg.slot(f"w2_{half}_{m}")
                    p.dma("gpsimd", pt[:, 0:KD * 128], wp["p_w1"][m], writes=[w1sl])
                    p.dma("gpsimd", pt[:, KD * 128:2 * KD * 128], wp["p_w2"][m],
                          writes=[w2sl])
                    return (w1sl, w2sl)
                return pool_stage(KF, loadf)

            def h_pool(wname, tag):
                def loadf(m, pt, reg):
                    wsl = reg.slot(f"{tag}w_{m}")
                    p.dma("gpsimd", pt[:, :], wp[wname][m], writes=[wsl])
                    return wsl
                return pool_stage(KF, loadf)

            def s5_pool(tag):
                def loadf(mo, pt, reg):
                    wsl = reg.slot(f"{tag}_{mo}")
                    p.dma("gpsimd", pt[:, :], wp["p_w3"][mo], writes=[wsl])
                    return wsl
                return pool_stage(KD, loadf)

            for half in range(2):
                th0 = half * HT
                u_i, h1_i = half % 2, (half + 1) % 2

                pr_s2, run_s2 = s2_stage(half)
                if half == 0:
                    if pre_s1h0 is not None:
                        pr_s2()
                        zf_state[0] = pre_s1h0
                    else:
                        xbs = [s1_loads(src_ap, src_slots2d, 0, tt)
                               for tt in range(NH)]
                        pr_s2()
                        do_s1(0, xbs=xbs)
                else:
                    pr_s2()
                sl_zf = zf_state[half]

                reg_A[u_i].turn()
                sl_u = [[reg_A[u_i].slot(f"u{half}_{m}_{t}") for t in range(NH)]
                        for m in range(KF)]
                u_ap = lambda m, t, u_i=u_i: A_t[u_i][:, m * HT + t * NT: m * HT + t * NT + NT]

                def s2_use(m, pt, sl):
                    w1sl, w2sl = sl
                    for t in range(NH):
                        b1 = p.next_bank()
                        b2 = p.next_bank()
                        p.mm_group(b1, [(pt[:, k * 128:(k + 1) * 128], zf_ap(k, t),
                                         (w1sl, sl_zf[k][t])) for k in range(KD)])
                        p.mm_group(b2, [(pt[:, KD * 128 + k * 128:KD * 128 + (k + 1) * 128],
                                         zf_ap(k, t), (w2sl, sl_zf[k][t])) for k in range(KD)])
                        ti = (m * NH + t) % 2
                        p.op("scalar", lambda e, b1=b1, m=m, ti=ti:
                             e.activation(tf_t[ti][:, :], b1.ap, AF.Silu, scale=t_s1[:, m:m + 1]),
                             reads=[b1, sl_vec["s1"]], writes=[sl_tf[ti]])
                        p.op("vector", lambda e, b2=b2, m=m, t=t, ti=ti:
                             e.scalar_tensor_tensor(u_ap(m, t), b2.ap, t_s2[:, m:m + 1],
                                                    tf_t[ti][:, :], ALU.mult, ALU.mult),
                             reads=[b2, sl_tf[ti], sl_vec["s2"]], writes=[sl_u[m][t]])
                run_s2(s2_use)

                def hstage(prun, tvec_t, tvec_sl, in_ap, in_sl, out_i, tag,
                           hook=None):
                    reg_A[out_i].turn()
                    o_sl = [[reg_A[out_i].slot(f"{tag}_{m}_{t}") for t in range(NH)]
                            for m in range(KF)]
                    o_ap = lambda m, t, out_i=out_i: A_t[out_i][:, m * HT + t * NT: m * HT + t * NT + NT]

                    def use(m, pt, wsl):
                        for t in range(NH):
                            b = p.next_bank()
                            p.mm_group(b, [(pt[:, k * 128:(k + 1) * 128],
                                            in_ap(k, t), (wsl, in_sl[k][t]))
                                           for k in range(KF)])
                            p.op("scalar", lambda e, b=b, m=m, t=t:
                                 e.activation(o_ap(m, t), b.ap, AF.Gelu,
                                              scale=tvec_t[:, m:m + 1]),
                                 reads=[b, tvec_sl], writes=[o_sl[m][t]])
                    prun(use, hook=hook)
                    return o_sl, o_ap

                s3hook = None
                if half == 0:
                    s3st = {}

                    def s3hook(m):
                        if m == 4:
                            s3st["xb"] = [s1_loads(src_ap, src_slots2d, 1, tt)
                                          for tt in range(NH)]
                            reg_zf.turn()
                            s3st["sl"] = [[reg_zf.slot(f"zf1_{k}_{t}")
                                           for t in range(NH)] for k in range(KD)]
                elif "s3h1mo" in hooks:
                    s3hook = hooks["s3h1mo"]

                pr_h1, run_h1 = h_pool("p_h1", f"h1_{half}")
                pr_h1()
                sl_h1, h1_ap = hstage(run_h1, t_t1, sl_vec["t1"], u_ap, sl_u,
                                      h1_i, f"h1_{half}", hook=s3hook)
                if half == 0:
                    for tt in range(NH):
                        s1_work([s3st["sl"][k][tt] for k in range(KD)],
                                zf_ap, s3st["xb"][tt], tt)
                    zf_state[1] = s3st["sl"]
                pr_h2, run_h2 = h_pool("p_h2", f"h2_{half}")
                pr_h2()
                sl_h2, h2_ap = hstage(run_h2, t_t2, sl_vec["t2"], h1_ap, sl_h1,
                                      u_i, f"h2_{half}")
                if half == 1 and "s4h1" in hooks:
                    hooks["s4h1"]()

                pr_s5, run_s5 = s5_pool(f"w3_{half}")
                pr_s5()
                ntile = KD * NH
                xt_pend = {}

                def xt_load(i):
                    mo_, t_ = divmod(i, NH)
                    ch = (th0 + t_ * NT) // NT
                    xi_ = i % 2
                    p.dma("sync", xt_t[xi_][:, :],
                          src_ap[mo_ * 128:(mo_ + 1) * 128, th0 + t_ * NT: th0 + (t_ + 1) * NT],
                          reads=[src_slots2d[mo_][ch]], writes=[sl_xt[xi_]])
                xt_load(0)

                def s5_use(mo, pt, wsl):
                    for t in range(NH):
                        i = mo * NH + t
                        ch = (th0 + t * NT) // NT
                        b = p.next_bank()
                        p.mm_group(b, [(pt[:, k * 128:(k + 1) * 128],
                                        h2_ap(k, t), (wsl, sl_h2[k][t]))
                                       for k in range(KF)])
                        if i + 1 < ntile:
                            xt_load(i + 1)
                        xi_ = i % 2
                        xs = sl_xt[xi_]
                        p.op("vector", lambda e, b=b, mo=mo, xi_=xi_:
                             e.scalar_tensor_tensor(xt_t[xi_][:, :], b.ap, t_s3h[:, mo:mo + 1],
                                                    xt_t[xi_][:, :], ALU.mult, ALU.add),
                             reads=[b, xs, sl_vec["s3h"]], writes=[xs])
                        p.dma("sync",
                              dst_ap[mo * 128:(mo + 1) * 128, th0 + t * NT: th0 + (t + 1) * NT],
                              xt_t[xi_][:, :], reads=[xs], writes=[dst_slots2d[mo][ch]])
                hook = hooks.get("s5h1mo") if half == 1 else None
                run_s5(s5_use, hook=hook)

        attn = {}

        def attn_carve_head():
            attn["wg_t"] = at("wg", OV + 0, [128, KD * 1024], BF16)
            attn["reg_wg"] = arena.region("wg", 0, KD * 2048)
            attn["sl_wg"] = [attn["reg_wg"].slot(f"wg{m}") for m in range(KD)]

        def attn_carve_head2():
            attn["wc_t"] = at("wc", OV + 16384, [128, KD * 1024], BF16)
            attn["reg_wc"] = arena.region("wc", 16384, KD * 2048)
            attn["sl_wc"] = [attn["reg_wc"].slot(f"wc{m}") for m in range(KD)]
            attn["xi_t"] = [at(f"xi{i}", OV + 32768 + i * 16384, [128, KD * NT], F32)
                            for i in range(2)]
            attn["reg_xi"] = [arena.region(f"xi{i}", 32768 + i * 16384, KD * NT * 4)
                              for i in range(2)]
            attn["z1_t"] = [at(f"z1{i}", OV + 65536 + i * 8192, [128, KD * NT], BF16)
                            for i in range(2)]
            attn["reg_z1"] = [arena.region(f"z1{i}", 65536 + i * 8192, KD * NT * 2)
                              for i in range(2)]

        def attn_carve_body():
            attn["h_t"] = at("h", OV + 81920, [128, 4 * NT], F32)
            rh = arena.region("h", 81920, 4 * NT * 4)
            attn["hl_t"] = at("hl", OV + 94208, [128, 16], F32)
            rhl = arena.region("hl", 94208, 64)
            attn["g_t"] = at("g", OV + 131072, [128, 4 * NT], F32)
            rg = arena.region("g", 131072, 4 * NT * 4)
            attn["b_t"] = at("b", OV + 139264, [128, 4 * NT], F32)
            rb = arena.region("b", 139264, 4 * NT * 4)
            attn["sl_g"] = [rg.slot(f"gp{i}") for i in range(4)]
            attn["sl_b"] = [rb.slot(f"bp{i}") for i in range(4)]
            attn["sl_h"] = [rh.slot(f"hp{i}") for i in range(4)]
            attn["sl_hl"] = [rhl.slot(f"hl{i}") for i in range(2)]
            attn["z2_t"] = at("z2", OV + 98304, [128, KD * T], BF16)
            attn["reg_z2"] = arena.region("z2", 98304, KD * T * 2)
            attn["sl_z2"] = [[attn["reg_z2"].slot(f"z2_{k}_{c}") for c in range(NCH)]
                             for k in range(KD)]

        def z2_ap(k, c):
            return attn["z2_t"][:, k * T + c * NT: k * T + c * NT + NT]

        AC = {}

        def attn_load_xb(c):
            bi = c % 2
            src_sl = [sl_xdram[r][c] for r in range(KD)]
            AC.setdefault(c, {})["xb"] = load_xb(xdram, src_sl, c * NT, bi)

        def attn_load_xi(c):
            bi = c % 2
            attn["reg_xi"][bi].turn()
            xisl = attn["reg_xi"][bi].slot(f"xi{c}")
            src_sl = [sl_xdram[r][c] for r in range(KD)]
            p.dma("gpsimd", attn["xi_t"][bi][:, :].rearrange("p (k t) -> p k t", t=NT),
                  x_kview(xdram, c * NT, NT), reads=src_sl, writes=[xisl])
            AC.setdefault(c, {})["xi"] = xisl

        def attn_loads(c):
            attn_load_xb(c)
            attn_load_xi(c)

        def attn_head(c):
            bi = c % 2
            xbs, xbt = AC[c]["xb"]
            xisl = AC[c]["xi"]
            xi_ap = lambda k, bi=bi: attn["xi_t"][bi][:, k * NT:(k + 1) * NT]
            r_sl, r_ap, mr_sl, bank_m = stats(
                xbs, xbt, "ln", bi, f1_out=(sl_mr[bi], mr_t[bi][:, :]),
                sq_eng="gpsimd")
            ln_apply(lambda k: (xi_ap(k), (xisl,)), r_sl, r_ap, bank_m,
                     lambda k: (xi_ap(k), xisl), mul_eng="gpsimd")
            attn["reg_z1"][bi].turn()
            z1sl = attn["reg_z1"][bi].slot(f"z1_{c}")
            z1_ap = lambda k, bi=bi: attn["z1_t"][bi][:, k * NT:(k + 1) * NT]
            for k in range(KD):
                p.op("vector", lambda e, k=k, bi=bi:
                     e.scalar_tensor_tensor(z1_ap(k), xi_ap(k), t_n1[:, k:k + 1],
                                            mr_t[bi][:, :], ALU.mult, ALU.mult),
                     reads=[xisl, sl_mr[bi], sl_vec["n1"]], writes=[z1sl])
            AC[c]["z1"] = z1sl

        def attn_gates_scan(c):
            bi = c % 2
            z1sl = AC[c]["z1"]
            xisl = AC[c]["xi"]
            xi_ap = lambda k, bi=bi: attn["xi_t"][bi][:, k * NT:(k + 1) * NT]
            z1_ap = lambda k, bi=bi: attn["z1_t"][bi][:, k * NT:(k + 1) * NT]
            wg_ap = lambda m, k: attn["wg_t"][:, m * 1024 + k * 128: m * 1024 + (k + 1) * 128]
            wc_ap = lambda m, k: attn["wc_t"][:, m * 1024 + k * 128: m * 1024 + (k + 1) * 128]
            g_t, b_t, h_t = attn["g_t"], attn["b_t"], attn["h_t"]
            hl_t = attn["hl_t"]
            sl_g, sl_b, sl_h = attn["sl_g"], attn["sl_b"], attn["sl_h"]
            sl_hl = attn["sl_hl"]
            for m in range(KD):
                pi = m % 4
                g_ap = g_t[:, pi * NT:(pi + 1) * NT]
                b_ap = b_t[:, pi * NT:(pi + 1) * NT]
                h_ap = h_t[:, pi * NT:(pi + 1) * NT]
                bg_ = p.next_bank()
                bc_ = p.next_bank()
                p.mm_group(bg_, [(wg_ap(m, k), z1_ap(k), (attn["sl_wg"][m], z1sl))
                                 for k in range(KD)])
                p.mm_group(bc_, [(wc_ap(m, k), z1_ap(k), (attn["sl_wc"][m], z1sl))
                                 for k in range(KD)])
                p.op("scalar", lambda e, bg_=bg_, m=m, g_ap=g_ap:
                     e.activation(g_ap, bg_.ap, AF.Sigmoid, bias=t_bg[:, m:m + 1]),
                     reads=[bg_, sl_vec["bg"]], writes=[sl_g[pi]])
                p.op("scalar", lambda e, bc_=bc_, m=m, b_ap=b_ap:
                     e.activation(b_ap, bc_.ap, AF.Tanh,
                                  scale=NEG_ONE, bias=t_bcn[:, m:m + 1]),
                     reads=[bc_, sl_vec["bcn"], sl_consts], writes=[sl_b[pi]])
                p.op("vector", lambda e, g_ap=g_ap, b_ap=b_ap:
                     e.scalar_tensor_tensor(b_ap, g_ap, 1.0, b_ap,
                                            ALU.subtract, ALU.mult),
                     reads=[sl_g[pi], sl_b[pi]], writes=[sl_b[pi]])
                if c == 0:
                    init = 0.0
                    rds = [sl_g[pi], sl_b[pi]]
                else:
                    init = hl_t[:, ((c - 1) % 2) * 8 + m: ((c - 1) % 2) * 8 + m + 1]
                    rds = [sl_g[pi], sl_b[pi], sl_hl[(c - 1) % 2]]
                p.op("vector", lambda e, init=init, h_ap=h_ap, g_ap=g_ap, b_ap=b_ap:
                     e.tensor_tensor_scan(h_ap, g_ap, b_ap, init,
                                          ALU.mult, ALU.add),
                     reads=rds, writes=[sl_h[pi]])
                p.op("gpsimd", lambda e, c=c, m=m, h_ap=h_ap:
                     e.tensor_copy(hl_t[:, (c % 2) * 8 + m: (c % 2) * 8 + m + 1],
                                   h_ap[:, NT - 1:NT]),
                     reads=[sl_h[pi]], writes=[sl_hl[c % 2]])
                p.op("vector", lambda e, m=m, h_ap=h_ap:
                     e.tensor_add(xi_ap(m), xi_ap(m), h_ap),
                     reads=[sl_h[pi], xisl], writes=[xisl])

        def attn_deferred(c):
            bi = c % 2
            xisl = AC[c]["xi"]
            xi_ap = lambda k, bi=bi: attn["xi_t"][bi][:, k * NT:(k + 1) * NT]
            r2_sl, r2_ap, _, _ = stats(None, xb_t[bi], "rms", bi,
                                       sq_src=(xisl, xi_ap))
            for k in range(KD):
                p.op("vector", lambda e, k=k, c=c:
                     e.scalar_tensor_tensor(z2_ap(k, c), xi_ap(k), t_n2[:, k:k + 1],
                                            r2_ap, ALU.mult, ALU.mult),
                     reads=[xisl, r2_sl, sl_vec["n2"]], writes=[attn["sl_z2"][k][c]])
            p.dma("gpsimd", x_kview(xdram, c * NT, NT),
                  attn["xi_t"][bi][:, :].rearrange("p (k t) -> p k t", t=NT),
                  reads=[xisl] + [sl_xdram[r][c] for r in range(KD)],
                  writes=[sl_xdram[r][c] for r in range(KD)],
                  accum_op=ALU.add)

        def make_attn_B(w13h1_hook=None):
            def w13_pool(hb):
                def loadf(m, pt, reg):
                    w1sl = reg.slot(f"aw1_{hb}_{m}")
                    w3sl = reg.slot(f"aw3_{hb}_{m}")
                    p.dma("gpsimd", pt[:, 0:KD * 128], wp["p_aw1"][m], writes=[w1sl])
                    p.dma("gpsimd", pt[:, KD * 128:2 * KD * 128], wp["p_aw3"][m],
                          writes=[w3sl])
                    return (w1sl, w3sl)
                return pool_stage(KF, loadf)

            def w2_pool(hb):
                def loadf(mo, pt, reg):
                    wsl = reg.slot(f"aw2_{hb}_{mo}")
                    p.dma("gpsimd", pt[:, :], wp["p_aw2"][mo], writes=[wsl])
                    return wsl
                return pool_stage(KD, loadf)

            st_w13 = [w13_pool(0), w13_pool(1)]
            st_w2 = [w2_pool(0), w2_pool(1)]

            def prime_h0():
                st_w13[0][0]()

            def run():
                uh_t = at("uh", OV + 0, [128, KF * HT], BF16)
                reg_uh = arena.region("uh", 0, KF * HT * 2)
                for hb in range(2):
                    th0 = hb * HT
                    reg_uh.turn()
                    sl_uh = [[reg_uh.slot(f"uh{hb}_{m}_{t}") for t in range(NH)]
                             for m in range(KF)]
                    uh_ap = lambda m, t: uh_t[:, m * HT + t * NT: m * HT + t * NT + NT]

                    def w13_use(m, pt, sl):
                        w1sl, w3sl = sl
                        for t in range(NH):
                            c = hb * NH + t
                            b1 = p.next_bank()
                            b3 = p.next_bank()
                            p.mm_group(b1, [(pt[:, k * 128:(k + 1) * 128], z2_ap(k, c),
                                             (w1sl, attn["sl_z2"][k][c])) for k in range(KD)])
                            p.mm_group(b3, [(pt[:, KD * 128 + k * 128:KD * 128 + (k + 1) * 128],
                                             z2_ap(k, c), (w3sl, attn["sl_z2"][k][c]))
                                            for k in range(KD)])
                            ti = (m * NH + t) % 2
                            p.op("scalar", lambda e, b1=b1, ti=ti:
                                 e.activation(tf_t[ti][:, :], b1.ap, AF.Silu),
                                 reads=[b1], writes=[sl_tf[ti]])
                            p.op("vector", lambda e, b3=b3, m=m, t=t, ti=ti:
                                 e.tensor_mul(uh_ap(m, t), tf_t[ti][:, :], b3.ap),
                                 reads=[b3, sl_tf[ti]], writes=[sl_uh[m][t]])
                    hook = w13h1_hook if hb == 1 else None
                    st_w13[hb][1](w13_use, hook=hook)

                    ntile = KD * NH
                    st_w2[hb][0]()

                    def xt_load(i):
                        mo_, t_ = divmod(i, NH)
                        c = hb * NH + t_
                        xi_ = i % 2
                        p.dma("sync", xt_t[xi_][:, :],
                              xdram[mo_ * 128:(mo_ + 1) * 128, th0 + t_ * NT: th0 + (t_ + 1) * NT],
                              reads=[sl_xdram[mo_][c]], writes=[sl_xt[xi_]])
                    xt_load(0)

                    def w2_use(mo, pt, wsl):
                        for t in range(NH):
                            c = hb * NH + t
                            i = mo * NH + t
                            b = p.next_bank()
                            p.mm_group(b, [(pt[:, k * 128:(k + 1) * 128],
                                            uh_ap(k, t), (wsl, sl_uh[k][t]))
                                           for k in range(KF)])
                            if i + 1 < ntile:
                                xt_load(i + 1)
                            xi_ = i % 2
                            xs = sl_xt[xi_]
                            p.op("vector", lambda e, b=b, xi_=xi_:
                                 e.tensor_add(xt_t[xi_][:, :], xt_t[xi_][:, :], b.ap),
                                 reads=[b, xs], writes=[xs])
                            p.dma("sync",
                                  xdram[mo * 128:(mo + 1) * 128, th0 + t * NT: th0 + (t + 1) * NT],
                                  xt_t[xi_][:, :], reads=[xs], writes=[sl_xdram[mo][c]])
                    st_w2[hb][1](w2_use)
                    if hb == 0:
                        st_w13[1][0]()
            return prime_h0, run

        xin_slots = [[Slot(f"xin{r}_{t}", frozen=True) for t in range(NCH)]
                     for r in range(KD)]

        zf_t = at("zf", OV + 0, [128, KD * HT], BF16)
        reg_zf = arena.region("zf", 0, KD * HT * 2)
        A_t = [at("A0", OV + 16384, [128, KF * HT], BF16),
               at("A1", OV + 81920, [128, KF * HT], BF16)]
        reg_A = [arena.region("A0", 16384, KF * HT * 2),
                 arena.region("A1", 81920, KF * HT * 2)]

        def hook_s3h1mo(m):
            if m == 2:
                attn_carve_head()
            if m >= 4 and m % 4 == 0 and m <= 28:
                for mw in ([m // 4 - 1] if m < 28 else [6, 7]):
                    p.dma("gpsimd", attn["wg_t"][:, mw * 1024:(mw + 1) * 1024],
                          wp["p_wg"][mw], writes=[attn["sl_wg"][mw]])

        def hook_s4h1():
            attn_carve_head2()

        def hook_s5h1mo(mo):
            if mo == 0:
                attn_loads(0)
            elif mo == 1:
                attn_loads(1)
            elif mo == 2:
                for m in range(4):
                    p.dma("gpsimd", attn["wc_t"][:, m * 1024:(m + 1) * 1024],
                          wp["p_wc"][m], writes=[attn["sl_wc"][m]])
            elif mo == 3:
                for m in range(4, KD):
                    p.dma("gpsimd", attn["wc_t"][:, m * 1024:(m + 1) * 1024],
                          wp["p_wc"][m], writes=[attn["sl_wc"][m]])
            elif mo == 4:
                attn_head(0)
            elif mo == 6:
                attn_head(1)

        emit_ffn(xin, xin_slots, xdram, sl_xdram, A_t, reg_A, zf_t, reg_zf,
                 hooks={"s3h1mo": hook_s3h1mo, "s4h1": hook_s4h1,
                        "s5h1mo": hook_s5h1mo})

        zfb_t = None

        def mk_ffn2_s1_hooks():
            state = {}

            def hook(m):
                nonlocal zfb_t
                if m == 2:
                    zfb_t = at("zfb", OV + 65536, [128, KD * HT], BF16)
                    state["reg"] = arena.region("zfb", 65536, KD * HT * 2)
                    state["sl"] = [[state["reg"].slot(f"zfb0_{k}_{t}")
                                    for t in range(NH)] for k in range(KD)]
                    state["ap"] = lambda k, t: zfb_t[:, k * HT + t * NT: k * HT + t * NT + NT]
                    state["xb0"] = load_xb(xdram, [sl_xdram[r][0] for r in range(KD)],
                                           0, 0)
                elif m == 8:
                    xbs, xbt = state["xb0"]
                    r_sl, r_ap, mr_sl, bank_m = stats(xbs, xbt, "ln", 0,
                                                      sq_eng="scalar")
                    ln_apply(lambda k: (xbt[:, k * NT:(k + 1) * NT], (xbs,)),
                             r_sl, r_ap, bank_m,
                             lambda k: (state["ap"](k, 0), state["sl"][k][0]),
                             tfi=(2, 3))
                elif m == 16:
                    state["xb1"] = load_xb(xdram, [sl_xdram[r][1] for r in range(KD)],
                                           NT, 1)
                elif m == 22:
                    xbs, xbt = state["xb1"]
                    r_sl, r_ap, mr_sl, bank_m = stats(xbs, xbt, "ln", 1,
                                                      sq_eng="scalar")
                    ln_apply(lambda k: (xbt[:, k * NT:(k + 1) * NT], (xbs,)),
                             r_sl, r_ap, bank_m,
                             lambda k: (state["ap"](k, 1), state["sl"][k][1]),
                             tfi=(2, 3))
            return state, hook

        attn_carve_body()
        s1_state, s1_hook = mk_ffn2_s1_hooks()
        primeB, runB = make_attn_B(w13h1_hook=s1_hook)
        primeB()

        attn_load_xb(2)
        for c in range(NCH):
            if c >= 1:
                attn_deferred(c - 1)
                if c + 1 < NCH:
                    attn_load_xi(c + 1)
                if c + 2 < NCH:
                    attn_load_xb(c + 2)
                if c + 1 < NCH:
                    attn_head(c + 1)
            attn_gates_scan(c)
        attn_deferred(NCH - 1)

        runB()

        A2_t = [at("A0b", OV + 0, [128, KF * HT], BF16),
                at("A1b", OV + 81920, [128, KF * HT], BF16)]
        reg_A2 = [arena.region("A0b", 0, KF * HT * 2),
                  arena.region("A1b", 81920, KF * HT * 2)]
        reg_zf2 = s1_state["reg"]
        emit_ffn(xdram, sl_xdram, y_d, sl_y, A2_t, reg_A2, zfb_t, reg_zf2,
                 pre_s1h0=[[s1_state["sl"][k][t] for t in range(NH)]
                           for k in range(KD)])

        for i in range(NQSEM):
            if p.cnt[f"sy{i}"]:
                p.eng("sync").wait_ge(p.sem[f"sy{i}"], p.cnt[f"sy{i}"])
            if p.cnt[f"wg{i}"]:
                p.eng("gpsimd").wait_ge(p.sem[f"wg{i}"], p.cnt[f"wg{i}"])

    return nc



def _swizzle_panels(w_io):
    I, O = w_io.shape
    return np.ascontiguousarray(
        w_io.reshape(I // 128, 128, O // 128, 128).transpose(2, 1, 0, 3)
        .reshape(O // 128, 128, I))


def make_core_inputs(xb_core, W):
    d = {"xin": np.ascontiguousarray(xb_core.T)}
    d["p_w1"] = W["p_w1"]; d["p_w2"] = W["p_w2"]; d["p_h1"] = W["p_h1"]
    d["p_h2"] = W["p_h2"]; d["p_w3"] = W["p_w3"]; d["p_wg"] = W["p_wg"]
    d["p_wc"] = W["p_wc"]; d["p_aw1"] = W["p_aw1"]; d["p_aw3"] = W["p_aw3"]
    d["p_aw2"] = W["p_aw2"]
    for k in ("v_bg", "v_bc", "v_n1", "v_n2", "v_s1", "v_s2", "v_t1", "v_t2", "v_s3"):
        d[k] = W[k]
    return d


def prep_weights(L0_attn_Wg, L0_attn_bg, L0_attn_Wc, L0_attn_bc, L0_attn_n1_w,
                 L0_attn_n2_w, L0_attn_W1, L0_attn_W3, L0_attn_W2,
                 L0_ffn_w1_w, L0_ffn_w1_s, L0_ffn_w2_w, L0_ffn_w2_s,
                 L0_ffn_h1_w, L0_ffn_h1_s, L0_ffn_h2_w, L0_ffn_h2_s,
                 L0_ffn_w3_w, L0_ffn_w3_s):
    f32 = lambda a: np.ascontiguousarray(np.asarray(a), dtype=np.float32)
    W = {}
    W["p_w1"] = _swizzle_panels(f32(L0_ffn_w1_w).T)
    W["p_w2"] = _swizzle_panels(f32(L0_ffn_w2_w).T)
    W["p_h1"] = _swizzle_panels(f32(L0_ffn_h1_w).T)
    W["p_h2"] = _swizzle_panels(f32(L0_ffn_h2_w).T)
    W["p_w3"] = _swizzle_panels(f32(L0_ffn_w3_w).T)
    W["p_wg"] = _swizzle_panels(f32(L0_attn_Wg))
    W["p_wc"] = _swizzle_panels(f32(L0_attn_Wc))
    W["p_aw1"] = _swizzle_panels(f32(L0_attn_W1))
    W["p_aw3"] = _swizzle_panels(f32(L0_attn_W3))
    W["p_aw2"] = _swizzle_panels(f32(L0_attn_W2))
    W["v_bg"] = f32(L0_attn_bg).reshape(KD, 128)
    W["v_bc"] = f32(L0_attn_bc).reshape(KD, 128)
    W["v_n1"] = f32(L0_attn_n1_w).reshape(KD, 128)
    W["v_n2"] = f32(L0_attn_n2_w).reshape(KD, 128)
    W["v_s1"] = f32(L0_ffn_w1_s).reshape(KF, 128)
    W["v_s2"] = f32(L0_ffn_w2_s).reshape(KF, 128)
    W["v_t1"] = f32(L0_ffn_h1_s).reshape(KF, 128)
    W["v_t2"] = f32(L0_ffn_h2_s).reshape(KF, 128)
    W["v_s3"] = f32(L0_ffn_w3_s).reshape(KD, 128)
    return W


def kernel(L0_attn_x, L0_attn_Wg, L0_attn_bg, L0_attn_Wc, L0_attn_bc,
           L0_attn_n1_w, L0_attn_n2_w, L0_attn_W1, L0_attn_W3, L0_attn_W2,
           L0_ffn_w1_w, L0_ffn_w1_s, L0_ffn_w2_w, L0_ffn_w2_s,
           L0_ffn_h1_w, L0_ffn_h1_s, L0_ffn_h2_w, L0_ffn_h2_s,
           L0_ffn_w3_w, L0_ffn_w3_s):
    x = np.ascontiguousarray(np.asarray(L0_attn_x), dtype=np.float32)
    B, T, _ = x.shape
    W = prep_weights(L0_attn_Wg, L0_attn_bg, L0_attn_Wc, L0_attn_bc, L0_attn_n1_w,
                     L0_attn_n2_w, L0_attn_W1, L0_attn_W3, L0_attn_W2,
                     L0_ffn_w1_w, L0_ffn_w1_s, L0_ffn_w2_w, L0_ffn_w2_s,
                     L0_ffn_h1_w, L0_ffn_h1_s, L0_ffn_h2_w, L0_ffn_h2_s,
                     L0_ffn_w3_w, L0_ffn_w3_s)
    nc = build_program(T=T)
    in_maps = [make_core_inputs(x[i], W) for i in range(B)]
    res = run_bass_kernel_spmd(nc, in_maps, core_ids=list(range(B)))
    out = np.stack([res.results[i]["y"] for i in range(B)], axis=0)
    return np.ascontiguousarray(out.transpose(0, 2, 1))
